# version 16
# baseline (speedup 1.0000x reference)
import math
from contextlib import ExitStack

import numpy as np
import ml_dtypes

import concourse.bass as bass
import concourse.mybir as mybir
from concourse.bass_utils import run_bass_kernel_spmd

F32 = mybir.dt.float32
BF16 = mybir.dt.bfloat16
I32 = mybir.dt.int32
AF = mybir.ActivationFunctionType
ALU = mybir.AluOpType
AX = mybir.AxisListType
BF = ml_dtypes.bfloat16

NCORES = 8
SEQ = 8192
EMB = 2048
NE = EMB // 128
H, PDIM, NST, GRP, KCONV, CS = 64, 64, 128, 8, 4, 256
INTER = H * PDIM
CONV_DIM = INTER + 2 * GRP * NST
PROJ = INTER + CONV_DIM + H
FFN_I = 5504
NFI = 43

def blocks_of(c):
    return (c, 15 - c)

def own(j):
    return j if j <= 7 else 15 - j


class PB:
    def __init__(self, nc):
        self.nc = nc
        self.es = ExitStack()
        self.sems = {}
        self.cnt = {}

    def sem(self, name):
        if name not in self.sems:
            self.sems[name] = self.es.enter_context(self.nc.semaphore(name))
            self.cnt[name] = 0
        return self.sems[name]

    def dma(self, eng, out, in_, name="ld", **kw):
        h = self.sem(name)
        eng.dma_start(out=out, in_=in_, **kw).then_inc(h, 16)
        self.cnt[name] += 16

    def wait(self, eng, name):
        eng.wait_ge(self.sem(name), self.cnt[name])

    def guard(self, eng, instr):
        h = self.sem("gdve")
        instr.then_inc(h, 1)
        self.cnt["gdve"] += 1
        eng.wait_ge(h, self.cnt["gdve"])

    def sbuf(self, name, shape, dtype, es=None):
        return (es or self.es).enter_context(self.nc.sbuf_tensor(name, shape, dtype))

    def psum(self, name, shape, dtype=F32, es=None):
        return (es or self.es).enter_context(self.nc.psum_tensor(name, shape, dtype))

    def consts(self):
        nc = self.nc
        self.ident = self.sbuf("c_ident", [128, 128], BF16)
        self.ident_f = self.sbuf("c_ident_f", [128, 128], F32)
        self.ones_row = self.sbuf("c_ones_row", [1, 128], BF16)
        self.ones_col = self.sbuf("c_ones_col", [128, 1], BF16)
        self.ones_row_f = self.sbuf("c_ones_row_f", [1, 128], F32)
        self.ones_col_f = self.sbuf("c_ones_col_f", [128, 1], F32)
        self.iota_row = self.sbuf("c_iota_row", [128, 512], F32)
        self.iota_col = self.sbuf("c_iota_col", [128, 1], F32)
        with nc.Block() as blk:
            @blk.gpsimd
            def _(g):
                g.iota(self.iota_row[:, :], [[1, 512]], channel_multiplier=0,
                       allow_small_or_imprecise_dtypes=True)
                g.iota(self.iota_col[:, :], [[0, 1]], channel_multiplier=1,
                       allow_small_or_imprecise_dtypes=True)
        with nc.Block() as blk:
            @blk.vector
            def _(v):
                v.tensor_scalar(self.ident[:, :], self.iota_row[:, 0:128],
                                self.iota_col[:, 0:1], None, ALU.is_equal)
                v.tensor_scalar(self.ident_f[:, :], self.iota_row[:, 0:128],
                                self.iota_col[:, 0:1], None, ALU.is_equal)
                v.memset(self.ones_row[:, :], 1.0)
                v.memset(self.ones_col[:, :], 1.0)
                v.memset(self.ones_row_f[:, :], 1.0)
                v.memset(self.ones_col_f[:, :], 1.0)




def pipe_mm(pb, tag, ng, dma_emit, mm_emit, evac_emit, double_psum=True,
            act_emit=None, dma_eng="gpsimd"):
    nc = pb.nc
    sMM = pb.sem(tag + "m")
    sEv = pb.sem(tag + "e")
    sDn = pb.sem(tag + "d")
    bM = pb.cnt[tag + "m"]
    bE = pb.cnt[tag + "e"]
    dma_counts = []
    with nc.Block() as blk:
        def _dma_body(sy):
            for g in range(ng):
                if g >= 2:
                    sy.wait_ge(sMM, bM + g - 1)
                dma_emit(sy, g)
                dma_counts.append(pb.cnt[tag + "d"])
        if dma_eng == "gpsimd":
            blk.gpsimd(_dma_body)
        else:
            blk.sync(_dma_body)
        @blk.tensor
        def _(t):
            for g in range(ng):
                t.wait_ge(sDn, dma_counts[g])
                lag = 2 if double_psum else 1
                if g >= lag:
                    t.wait_ge(sEv, bE + g - lag + 1)
                mm_emit(t, g).then_inc(sMM, 1)
        @blk.vector
        def _(v):
            for g in range(ng):
                v.wait_ge(sMM, bM + g + 1)
                evac_emit(v, g).then_inc(sEv, 1)
        if act_emit is not None:
            @blk.scalar
            def _(sc):
                for g in range(ng):
                    sc.wait_ge(sEv, bE + g + 1)
                    act_emit(sc, g)
    pb.cnt[tag + "m"] += ng
    pb.cnt[tag + "e"] += ng


def transpose_seq(pb, items, ps2):
    nc = pb.nc
    sT = pb.sem("trT")
    sE = pb.sem("trE")
    bT = pb.cnt["trT"]
    bE = pb.cnt["trE"]
    n = len(items)
    with nc.Block() as blk:
        @blk.tensor
        def _(t):
            for i, (src, dst, idn) in enumerate(items):
                if i >= 2:
                    t.wait_ge(sE, bE + i - 1)
                pp, ff = src.shape
                t.transpose(ps2[i % 2][0:ff, 0:pp], src, idn).then_inc(sT, 1)
        @blk.vector
        def _(v):
            for i, (src, dst, idn) in enumerate(items):
                v.wait_ge(sT, bT + i + 1)
                pp, ff = src.shape
                v.tensor_copy(dst, ps2[i % 2][0:ff, 0:pp]).then_inc(sE, 1)
    pb.cnt["trT"] += n
    pb.cnt["trE"] += n


def ln_fm(pb, get_tile, nt, T, out_tile, gamma=None, beta=None, eps=1e-5,
          pre_scale=1.0, tag=""):
    nc = pb.nc
    nth = T // 512
    es = ExitStack()
    with es:
        sum_ps = [pb.psum(f"ln{tag}_s{i}", [1, 512], F32, es=es) for i in range(nth)]
        sq_ps = [pb.psum(f"ln{tag}_q{i}", [1, 512], F32, es=es) for i in range(nth)]
        sq = pb.sbuf(f"ln{tag}_sq", [128, T], F32, es=es)
        sqb = [pb.sbuf(f"ln{tag}_sqb{i}", [128, T], BF16, es=es) for i in range(2)]
        ab = pb.sbuf(f"ln{tag}_ab", [1, 2 * T], F32, es=es)
        abc = pb.sbuf(f"ln{tag}_abc", [128, 2 * T], F32, es=es)
        pbc = [pb.psum(f"ln{tag}_pbc{i}", [128, 512], F32, es=es) for i in range(2)]
        sQ = pb.sem("lnQ")
        sC = pb.sem("lnC")
        bQ = pb.cnt["lnQ"]
        bC = pb.cnt["lnC"]
        with nc.Block() as blk:
            @blk.scalar
            def _(s):
                for ct in range(NE):
                    if ct >= 2:
                        s.wait_ge(sC, bC + ct - 1)
                    s.activation(sqb[ct % 2][:, :], get_tile(ct), AF.Square
                                 ).then_inc(sQ, 1)
            @blk.tensor
            def _(t):
                for ct in range(NE):
                    for th in range(nth):
                        t.matmul(sum_ps[th][:, :], pb.ones_col[:, :],
                                 get_tile(ct)[:, 512 * th:512 * (th + 1)],
                                 start=(ct == 0), stop=(ct == NE - 1))
                    t.wait_ge(sQ, bQ + ct + 1)
                    for th in range(nth):
                        mm = t.matmul(sq_ps[th][:, :], pb.ones_col[:, :],
                                      sqb[ct % 2][:, 512 * th:512 * (th + 1)],
                                      start=(ct == 0), stop=(ct == NE - 1))
                        if th == nth - 1:
                            mm.then_inc(sC, 1)
        pb.cnt["lnQ"] += NE
        pb.cnt["lnC"] += NE
        mean = ab[0:1, 0:T]
        avar = ab[0:1, T:2 * T]
        sG = pb.sem("lnG")
        bG = pb.cnt["lnG"]
        bQ2 = pb.cnt["lnQ"]
        bC2 = pb.cnt["lnC"]
        nbc = 2 * nth
        with nc.Block() as blk:
            @blk.vector
            def _(v):
                for th in range(nth):
                    v.tensor_scalar(ab[0:1, 512 * th:512 * (th + 1)], sum_ps[th][:, :],
                                    1.0 / EMB, None, ALU.mult)
                    v.tensor_scalar(ab[0:1, T + 512 * th:T + 512 * (th + 1)],
                                    sq_ps[th][:, :], 1.0 / EMB, None, ALU.mult)
                v.tensor_mul(sq[0:1, 0:T], mean, mean)
                v.tensor_sub(avar, avar, sq[0:1, 0:T])
                v.tensor_scalar(avar, avar, float(eps), None, ALU.add
                                ).then_inc(sG, 1)
                v.wait_ge(sG, bG + 2)
                pb.guard(v, v.reciprocal(avar, avar))
                v.tensor_mul(mean, mean, avar)
                pb.guard(v, v.tensor_scalar(mean, mean, -1.0, None, ALU.mult))
                v.engine_nop().then_inc(sG, 1)
                for i in range(nbc):
                    v.wait_ge(sQ, bQ2 + i + 1)
                    half, th = divmod(i, nth)
                    v.tensor_copy(abc[:, half * T + 512 * th:half * T + 512 * (th + 1)],
                                  pbc[i % 2][:, :]).then_inc(sC, 1)
            @blk.scalar
            def _(s):
                s.wait_ge(sG, bG + 1)
                s.activation(avar, avar, AF.Sqrt).then_inc(sG, 1)
            @blk.tensor
            def _(t):
                t.wait_ge(sG, bG + 3)
                for i in range(nbc):
                    half, th = divmod(i, nth)
                    if i >= 2:
                        t.wait_ge(sC, bC2 + i - 1)
                    src = (1 - half) * T
                    t.matmul(pbc[i % 2][:, :], pb.ones_row_f[:, :],
                             ab[0:1, src + 512 * th:src + 512 * (th + 1)],
                             start=True, stop=True).then_inc(sQ, 1)
        pb.cnt["lnG"] = bG + 3
        pb.cnt["lnQ"] += nbc
        pb.cnt["lnC"] += nbc
        with nc.Block() as blk:
            @blk.vector
            def _(v):
                for ct in range(NE):
                    v.tensor_copy(sq[:, :], get_tile(ct))
                    v.tensor_mul(sq[:, :], sq[:, :], abc[:, 0:T])
                    v.tensor_add(sq[:, :], sq[:, :], abc[:, T:2 * T])
                    if gamma is not None:
                        v.tensor_scalar(sq[:, :], sq[:, :], gamma[:, ct:ct + 1],
                                        None, ALU.mult)
                    if beta is not None:
                        v.tensor_scalar(sq[:, :], sq[:, :], beta[:, ct:ct + 1],
                                        None, ALU.add)
                    v.tensor_copy(out_tile(ct), sq[:, :])


def _l0ffn_body(pb, x1_ap, lut_ap, gidx_ap, uidx_ap, didx_ap, x2_ap):
    nc = pb.nc
    T = 1024
    wg_sl = nc.dram_tensor("wg_sl", [256, FFN_I], BF16)
    wu_sl = nc.dram_tensor("wu_sl", [256, FFN_I], BF16)
    wd_sl = nc.dram_tensor("wd_sl", [688, EMB], BF16)
    WgT = nc.dram_tensor("WgT", [EMB, FFN_I], BF16, addr_space="Shared")
    WuT = nc.dram_tensor("WuT", [EMB, FFN_I], BF16, addr_space="Shared")
    WdT = nc.dram_tensor("WdT", [FFN_I, EMB], BF16, addr_space="Shared")
    if getattr(nc, "_dbg_wgt", False):
        dWg = nc.dram_tensor("dWgT", [256, FFN_I], BF16, kind="ExternalOutput")
        dWu = nc.dram_tensor("dWuT", [256, FFN_I], BF16, kind="ExternalOutput")
        dWd = nc.dram_tensor("dWdT", [688, EMB], BF16, kind="ExternalOutput")

    sc_gw = nc.enter_named_scope("ghostw", False)
    es2 = ExitStack()
    with es2:
        lutc = pb.sbuf("lutc", [128, 16], F32, es=es2)
        lutrow = pb.sbuf("lutrow", [1, 16], F32, es=es2)
        p_bc16 = pb.psum("p_bc16", [128, 16], F32, es=es2)
        with nc.Block() as blk:
            @blk.sync
            def _(sy):
                pb.dma(sy, lutrow[:, :], lut_ap)
                pb.wait(sy, "ld")
        with nc.Block() as blk:
            @blk.tensor
            def _(t):
                t.matmul(p_bc16[:, :], pb.ones_row_f[:, :], lutrow[:, :], start=True, stop=True)
        with nc.Block() as blk:
            @blk.vector
            def _(v):
                v.tensor_copy(lutc[:, :], p_bc16[:, :])

        idxs = pb.sbuf("g_idxs", [128, 768], I32, es=es2)
        idxb = pb.sbuf("g_idxb", [128, 768], BF16, es=es2)
        acc = pb.sbuf("g_acc", [128, 768], BF16, es=es2)
        tmp = pb.sbuf("g_tmp", [128, 768], BF16, es=es2)
        p_tr = pb.psum("g_ptr", [128, 128], BF16, es=es2)
        stA = pb.sbuf("g_stA", [128, FFN_I], BF16, es=es2)
        stB = pb.sbuf("g_stB", [128, FFN_I], BF16, es=es2)
        stD = [pb.sbuf(f"g_stD{j}", [128 if j < 5 else 48, EMB], BF16, es=es2) for j in range(6)]

        def convert_chunk(width, idx_src, out_tiles):
            with nc.Block() as blk:
                @blk.sync
                def _(sy):
                    pb.dma(sy, idxs[:, 0:width], idx_src)
                @blk.vector
                def _(v):
                    pb.wait(v, "ld")
                    v.tensor_copy(idxb[:, 0:width], idxs[:, 0:width])
                    v.memset(acc[:, 0:width], 0.0)
                    for k in range(16):
                        v.tensor_scalar(tmp[:, 0:width], idxb[:, 0:width],
                                        float(k), lutc[:, k:k + 1], ALU.is_equal, ALU.mult)
                        v.tensor_add(acc[:, 0:width], acc[:, 0:width], tmp[:, 0:width])
            ncol = (width + 127) // 128
            for j in range(ncol):
                w = min(128, width - 128 * j)
                with nc.Block() as blk:
                    @blk.tensor
                    def _(t, j=j, w=w):
                        t.transpose(p_tr[0:w, 0:128], acc[:, 128 * j:128 * j + w],
                                    pb.ident[:, :])
                with nc.Block() as blk:
                    @blk.vector
                    def _(v, j=j, w=w):
                        v.tensor_copy(out_tiles[j][0:w, :], p_tr[0:w, 0:128])

        for idx_ap, dst in ((gidx_ap, wg_sl), (uidx_ap, wu_sl)):
            for i in range(NFI):
                convert_chunk(256, idx_ap[128 * i:128 * (i + 1), :],
                              [stA[:, 128 * i:128 * (i + 1)],
                               stB[:, 128 * i:128 * (i + 1)]])
            with nc.Block() as blk:
                @blk.sync
                def _(sy, dst=dst):
                    pb.dma(sy, dst.ap()[0:128, :], stA[:, :])
                    pb.dma(sy, dst.ap()[128:256, :], stB[:, :])
                    pb.wait(sy, "ld")
        for i in range(NE):
            convert_chunk(688, didx_ap[128 * i:128 * (i + 1), :],
                          [stD[j][:, 128 * i:128 * (i + 1)] for j in range(6)])
        with nc.Block() as blk:
            @blk.sync
            def _(sy):
                for j in range(6):
                    hh = 128 if j < 5 else 48
                    pb.dma(sy, wd_sl.ap()[128 * j:128 * j + hh, :], stD[j][0:hh, :])
                pb.wait(sy, "ld")
        with nc.Block() as blk:
            @blk.gpsimd
            def _(g):
                ccs = pb.sem("cc")
                for src, dsts in ((wg_sl, WgT), (wu_sl, WuT), (wd_sl, WdT)):
                    g.collective_compute(
                        "AllGather", ALU.bypass, replica_groups=[list(range(NCORES))],
                        ins=[src.ap().opt()], outs=[dsts.ap().opt()]).then_inc(ccs)
                    pb.cnt["cc"] += 1
                g.wait_ge(ccs, pb.cnt["cc"])

    nc.leave_named_scope("ghostw", sc_gw[0], False)
    sc_fm = nc.enter_named_scope("ffnmm", False)
    es3 = ExitStack()
    with es3:
        xfm = pb.sbuf("f_xfm", [128, NE * T], BF16, es=es3)
        sgt = pb.sbuf("f_sg", [128, 4 * 512], BF16, es=es3)
        es_h = ExitStack()
        es_w = ExitStack()
        hbuf = pb.sbuf("f_h", [128, NFI * 1024], BF16, es=es_h)
        wband_g = pb.sbuf("f_wg", [128, NE * 256], BF16, es=es_w)
        wband_u = pb.sbuf("f_wu", [128, NE * 256], BF16, es=es_w)
        with nc.Block() as blk:
            @blk.sync
            def _(sy):
                pb.dma(sy, xfm[:, :].rearrange("p (ct t) -> p ct t", ct=NE),
                       x1_ap.rearrange("(ct p) t -> p ct t", p=128))
                pb.wait(sy, "ld")
        es_gu = ExitStack()
        pg = [pb.psum(f"f_pg{i}", [128, 512], F32, es=es_gu) for i in range(4)]
        pu = [pb.psum(f"f_pu{i}", [128, 512], F32, es=es_gu) for i in range(4)]
        def gu_dma(sy, g):
            off = (g % 2) * NE * 128
            for q4 in range(4):
                pb.dma(sy, wband_g[:, off + q4 * 512:off + (q4 + 1) * 512]
                       .rearrange("p (kt m) -> p kt m", kt=4),
                       WgT.ap()[512 * q4:512 * (q4 + 1), 128 * g:128 * (g + 1)]
                       .rearrange("(kt p) m -> p kt m", p=128), name="gud")
                pb.dma(sy, wband_u[:, off + q4 * 512:off + (q4 + 1) * 512]
                       .rearrange("p (kt m) -> p kt m", kt=4),
                       WuT.ap()[512 * q4:512 * (q4 + 1), 128 * g:128 * (g + 1)]
                       .rearrange("(kt p) m -> p kt m", p=128), name="gud")

        def gu_mm(t, g):
            off = (g % 2) * NE * 128
            pgs = pg[2 * (g % 2):2 * (g % 2) + 2]
            pus = pu[2 * (g % 2):2 * (g % 2) + 2]
            for th in range(2):
                for kt in range(NE):
                    xsl = xfm[:, kt * T + 512 * th:kt * T + 512 * (th + 1)]
                    t.matmul(pgs[th][:, :], wband_g[:, off + kt * 128:off + (kt + 1) * 128],
                             xsl, start=(kt == 0), stop=(kt == NE - 1))
                    mm = t.matmul(pus[th][:, :], wband_u[:, off + kt * 128:off + (kt + 1) * 128],
                                  xsl, start=(kt == 0), stop=(kt == NE - 1))
            return mm

        def gu_evac(v, g):
            pus = pu[2 * (g % 2):2 * (g % 2) + 2]
            for th in range(2):
                cp = v.tensor_mul(
                    hbuf[:, g * T + 512 * th:g * T + 512 * (th + 1)],
                    sgt[:, (g % 2) * 1024 + 512 * th:(g % 2) * 1024 + 512 * (th + 1)],
                    pus[th][:, :])
            return cp

        def gu_act(sc, g):
            pgs = pg[2 * (g % 2):2 * (g % 2) + 2]
            for th in range(2):
                sc.activation(sgt[:, (g % 2) * 1024 + 512 * th:(g % 2) * 1024 + 512 * (th + 1)],
                              pgs[th][:, :], AF.Silu)
        sGU = pb.sem("gus")
        bGU = pb.cnt["gus"]
        sGM = pb.sem("gum")
        bGM = pb.cnt["gum"]
        sGE = pb.sem("gue")
        bGE = pb.cnt["gue"]
        dcnt = []
        with nc.Block() as blk:
            @blk.sync
            def _(sy):
                for g in range(NFI):
                    if g >= 2:
                        sy.wait_ge(sGM, bGM + g - 1)
                    gu_dma(sy, g)
                    dcnt.append(pb.cnt["gud"])
            @blk.tensor
            def _(t):
                for g in range(NFI):
                    t.wait_ge(pb.sem("gud"), dcnt[g])
                    if g >= 2:
                        t.wait_ge(sGE, bGE + g - 1)
                    gu_mm(t, g).then_inc(sGM, 1)
            @blk.scalar
            def _(sc):
                for g in range(NFI):
                    sc.wait_ge(sGM, bGM + g + 1)
                    if g >= 2:
                        sc.wait_ge(sGE, bGE + g - 1)
                    for th in range(2):
                        aa = sc.activation(
                            sgt[:, (g % 2) * 1024 + 512 * th:(g % 2) * 1024 + 512 * (th + 1)],
                            pg[2 * (g % 2) + th][:, :], AF.Silu)
                    aa.then_inc(sGU, 1)
            @blk.vector
            def _(v):
                for g in range(NFI):
                    v.wait_ge(sGU, bGU + g + 1)
                    gu_evac(v, g).then_inc(sGE, 1)
        pb.cnt["gus"] += NFI
        pb.cnt["gum"] += NFI
        pb.cnt["gue"] += NFI
        es_gu.close()
        es_w.close()
        es_d = ExitStack()
        wband_d = pb.sbuf("f_wd", [128, 2 * NFI * 256], BF16, es=es_d)
        pd = [pb.psum(f"f_pd{i}", [128, 512], F32, es=es_d) for i in range(8)]
        tfm = xfm
        def dn_dma(sy, g):
            off = (g % 2) * NFI * 256
            for q6 in range(6):
                r0 = 1024 * q6
                rr = min(1024, FFN_I - r0)
                pb.dma(sy, wband_d[:, off + (r0 // 128) * 256:off + ((r0 + rr) // 128) * 256]
                       .rearrange("p (kt m) -> p kt m", m=256),
                       WdT.ap()[r0:r0 + rr, 256 * g:256 * (g + 1)]
                       .rearrange("(kt p) m -> p kt m", p=128), name="dnd")

        def dn_mm(t, g):
            off = (g % 2) * NFI * 256
            ps = pd[4 * (g % 2):4 * (g % 2) + 4]
            for mi in range(2):
                for th in range(2):
                    for kt in range(NFI):
                        wo = off + kt * 256 + mi * 128
                        mm = t.matmul(ps[2 * mi + th][:, :], wband_d[:, wo:wo + 128],
                                      hbuf[:, kt * T + 512 * th:kt * T + 512 * (th + 1)],
                                      start=(kt == 0), stop=(kt == NFI - 1))
            return mm

        def dn_evac(v, g):
            ps = pd[4 * (g % 2):4 * (g % 2) + 4]
            for mi in range(2):
                for th in range(2):
                    sl = slice((2 * g + mi) * T + 512 * th, (2 * g + mi) * T + 512 * (th + 1))
                    cp = v.tensor_add(tfm[:, sl], tfm[:, sl], ps[2 * mi + th][:, :])
            return cp
        pipe_mm(pb, "dn", 8, dn_dma, dn_mm, dn_evac, dma_eng="sync")
        es_d.close()
        es_h.close()
        nc.leave_named_scope("ffnmm", sc_fm[0], False)
        ln_fm(pb, lambda ct: tfm[:, ct * T:(ct + 1) * T], NE, T,
              lambda ct: tfm[:, ct * T:(ct + 1) * T], tag="f")
        with nc.Block() as blk:
            @blk.sync
            def _(sy):
                pb.dma(sy, x2_ap.rearrange("(ct p) t -> p ct t", p=128),
                       tfm[:, :].rearrange("p (ct t) -> p ct t", ct=NE), name="st")
                pb.wait(sy, "st")


def build_l0ffn():
    nc = bass.Bass("TRN2", target_bir_lowering=False)
    pb = PB(nc)
    x1 = nc.dram_tensor("x1fm", [EMB, 1024], BF16, kind="ExternalInput").ap()
    lut = nc.dram_tensor("lut", [1, 16], F32, kind="ExternalInput").ap()
    gidx = nc.dram_tensor("gidx_sl", [FFN_I, 256], I32, kind="ExternalInput").ap()
    uidx = nc.dram_tensor("uidx_sl", [FFN_I, 256], I32, kind="ExternalInput").ap()
    didx = nc.dram_tensor("didx_sl", [EMB, 688], I32, kind="ExternalInput").ap()
    x2 = nc.dram_tensor("x2fm", [EMB, 1024], BF16, kind="ExternalOutput").ap()
    with pb.es:
        pb.consts()
        _l0ffn_body(pb, x1, lut, gidx, uidx, didx, x2)
    return nc



def _l0attn_body(pb, x_aps, wproj_ap, cw_ap, cb_ap, dtb_ap, alog_ap, dpair_ap,
                 nw_ap, wout_ap, x1_ap, dbg=None):
    nc = pb.nc
    esP = ExitStack()

    def dump(name, src_ap):
        if dbg is None or name not in dbg:
            return
        with nc.Block() as blk:
            @blk.sync
            def _(sy):
                pb.dma(sy, dbg[name], src_ap, name="st")
                pb.wait(sy, "st")
    with esP:
        cw = pb.sbuf("a_cw", [128, 192], F32, es=esP)
        cb = pb.sbuf("a_cb", [128, 48], F32, es=esP)
        dtb = pb.sbuf("a_dtb", [64, 1], F32, es=esP)
        alog = pb.sbuf("a_alog", [64, 1], F32, es=esP)
        acol = pb.sbuf("a_acol", [64, 1], F32, es=esP)
        dpair = pb.sbuf("a_dpair", [128, 32], F32, es=esP)
        nw = pb.sbuf("a_nw", [128, 32], F32, es=esP)
        tri_w = pb.sbuf("a_tri", [128, 512], BF16, es=esP)
        ones64 = pb.sbuf("a_ones64", [64, 256], F32, es=esP)
        with nc.Block() as blk:
            @blk.sync
            def _(sy):
                pb.dma(sy, cw[:, :], cw_ap)
                pb.dma(sy, cb[:, :], cb_ap)
                pb.dma(sy, dtb[:, :], dtb_ap)
                pb.dma(sy, alog[:, :], alog_ap)
                pb.dma(sy, dpair[:, :], dpair_ap)
                pb.dma(sy, nw[:, :], nw_ap)
                pb.wait(sy, "ld")
        with nc.Block() as blk:
            @blk.scalar
            def _(s):
                s.activation(acol[:, :], alog[:, :], AF.Exp)
            @blk.vector
            def _(v):
                v.tensor_scalar(tri_w[:, 0:256], pb.iota_row[:, 0:256],
                                pb.iota_col[:, 0:1], None, ALU.is_ge)
                v.tensor_scalar(tri_w[:, 256:512], pb.iota_row[:, 0:256],
                                pb.iota_col[:, 0:1], 128.0, ALU.subtract, ALU.is_ge)
                v.memset(ones64[:, :], 1.0)
        with nc.Block() as blk:
            @blk.vector
            def _(v):
                v.tensor_scalar(acol[:, :], acol[:, :], -1.0, None, ALU.mult)
        ps512 = [pb.psum(f"a_ps512_{i}", [128, 512], F32, es=esP) for i in range(2)]
        acum_dram = nc.dram_tensor("a_acum_dram", [64, 256], F32)

        for half in range(2):
            x_ap = x_aps[half]
            esH = ExitStack()
            xfm = pb.sbuf(f"h{half}_xfm", [128, NE * 515], BF16, es=esH)
            gatebuf = pb.sbuf(f"h{half}_gate", [128, 32 * 512], BF16, es=esH)
            hidfm = pb.sbuf(f"h{half}_hid", [128, 32 * 512], BF16, es=esH)
            bfm = pb.sbuf(f"h{half}_bfm", [128, 8 * 512], BF16, es=esH)
            cfm = pb.sbuf(f"h{half}_cfm", [128, 8 * 512], BF16, es=esH)
            dtbuf = pb.sbuf(f"h{half}_dt", [64, 512], F32, es=esH)
            dt2buf = pb.sbuf(f"h{half}_dt2", [64, 512], F32, es=esH)
            scr64 = pb.sbuf(f"h{half}_scr64", [64, 512], F32, es=esH)
            x1buf = pb.sbuf(f"h{half}_x1", [128, NE * 512], BF16, es=esH)

            sc_xT = nc.enter_named_scope(f"xT{half}", False)
            esX = ExitStack()
            xrow = [pb.sbuf(f"h{half}_xrow{i}", [128, 2048], BF16, es=esX) for i in range(2)]
            psTx = [pb.psum(f"h{half}_psTx{i}", [128, 128], BF16, es=esX) for i in range(2)]
            for rt in range(5):
                r0 = 128 * rt
                rr = min(128, 515 - r0)
                with nc.Block() as blk:
                    @blk.gpsimd
                    def _(g, r0=r0, rr=rr, rt=rt):
                        pb.dma(g, xrow[rt % 2][0:rr, :].rearrange("p (ct f) -> p ct f", ct=NE),
                               x_ap[r0:r0 + rr, :].rearrange("p (ct f) -> p ct f", ct=NE))
                        pb.wait(g, "ld")
                transpose_seq(pb, [
                    (xrow[rt % 2][0:rr, 128 * ct:128 * (ct + 1)],
                     xfm[:, 515 * ct + r0:515 * ct + r0 + rr],
                     pb.ident[0:rr, 0:rr]) for ct in range(NE)], psTx)
            esX.close()
            nc.leave_named_scope(f"xT{half}", sc_xT[0], False)

            sc_ip = nc.enter_named_scope(f"inproj{half}", False)
            esW = ExitStack()
            wband = pb.sbuf(f"h{half}_wband", [128, NE * 512], BF16, es=esW)
            hbst = pb.sbuf(f"h{half}_hbst", [128, 2 * 515], BF16, es=esW)
            convtap = pb.sbuf(f"h{half}_ctap", [128, 512], F32, es=esW)
            convacc = pb.sbuf(f"h{half}_cacc", [128, 512], F32, es=esW)
            psH = pb.psum(f"h{half}_psH", [128, 16], F32, es=esW)
            psD = pb.psum(f"h{half}_psD", [64, 512], F32, es=esW)

            def inproj_group(c0, dst_write, is_hbc):
                nm = 2
                with nc.Block() as blk:
                    @blk.gpsimd
                    def _(g, c0=c0):
                        for q4 in range(4):
                            pb.dma(g, wband[:, q4 * 4 * 128 * nm:(q4 + 1) * 4 * 128 * nm]
                                   .rearrange("p (kt m) -> p kt m", kt=4),
                                   wproj_ap[512 * q4:512 * (q4 + 1), c0:c0 + 128 * nm]
                                   .rearrange("(kt p) m -> p kt m", p=128))
                        pb.wait(g, "ld")
                with nc.Block() as blk:
                    @blk.tensor
                    def _(t, c0=c0, is_hbc=is_hbc):
                        for mi in range(nm):
                            for kt in range(NE):
                                off = kt * 128 * nm + mi * 128
                                if is_hbc:
                                    t.matmul(ps512[mi][:, :], wband[:, off:off + 128],
                                             xfm[:, 515 * kt:515 * kt + 512],
                                             start=(kt == 0), stop=(kt == NE - 1))
                                    t.matmul(psH[:, 4 * mi:4 * mi + 3], wband[:, off:off + 128],
                                             xfm[:, 515 * kt + 512:515 * kt + 515],
                                             start=(kt == 0), stop=(kt == NE - 1))
                                else:
                                    t.matmul(ps512[mi][:, :], wband[:, off:off + 128],
                                             xfm[:, 515 * kt + 3:515 * kt + 515],
                                             start=(kt == 0), stop=(kt == NE - 1))
                with nc.Block() as blk:
                    @blk.vector
                    def _(v):
                        dst_write(v)

            for g2 in range(16):
                def wr(v, g2=g2):
                    for mi in range(2):
                        v.tensor_copy(gatebuf[:, (2 * g2 + mi) * 512:(2 * g2 + mi + 1) * 512],
                                      ps512[mi][:, :])
                inproj_group(256 * g2, wr, False)
            for g2 in range(24):
                def wr(v, g2=g2):
                    for mi in range(2):
                        v.tensor_copy(hbst[:, mi * 515:mi * 515 + 512], ps512[mi][:, :])
                        v.tensor_copy(hbst[:, mi * 515 + 512:mi * 515 + 515],
                                      psH[:, 4 * mi:4 * mi + 3])
                inproj_group(4096 + 256 * g2, wr, True)
                for mi in range(2):
                    j = 2 * g2 + mi
                    with nc.Block() as blk:
                        @blk.vector
                        def _(v, mi=mi, j=j):
                            iw = hbst[:, mi * 515:(mi + 1) * 515]
                            v.tensor_scalar(convacc[:, :], iw[:, 0:512],
                                            cw[:, 4 * j:4 * j + 1], None, ALU.mult)
                            for kk in range(1, 4):
                                v.tensor_scalar(convtap[:, :], iw[:, kk:kk + 512],
                                                cw[:, 4 * j + kk:4 * j + kk + 1],
                                                None, ALU.mult)
                                v.tensor_add(convacc[:, :], convacc[:, :], convtap[:, :])
                    with nc.Block() as blk:
                        @blk.scalar
                        def _(s, j=j):
                            if j < 32:
                                dst = hidfm[:, 512 * j:512 * (j + 1)]
                            elif j < 40:
                                dst = bfm[:, 512 * (j - 32):512 * (j - 31)]
                            else:
                                dst = cfm[:, 512 * (j - 40):512 * (j - 39)]
                            s.activation(dst, convacc[:, :], AF.Silu, bias=cb[:, j:j + 1])
            with nc.Block() as blk:
                @blk.gpsimd
                def _(g):
                    pb.dma(g, wband[:, 0:NE * 64].rearrange("p (kt m) -> p kt m", kt=NE),
                           wproj_ap[:, 10240:10304].rearrange("(kt p) m -> p kt m", p=128))
                    pb.wait(g, "ld")
            with nc.Block() as blk:
                @blk.tensor
                def _(t):
                    for kt in range(NE):
                        t.matmul(psD[0:64, 0:256], wband[:, kt * 64:kt * 64 + 64],
                                 xfm[:, 515 * kt + 3:515 * kt + 259],
                                 start=(kt == 0), stop=(kt == NE - 1))
                    for kt in range(NE):
                        t.matmul(psD[0:64, 256:512], wband[:, kt * 64:kt * 64 + 64],
                                 xfm[:, 515 * kt + 259:515 * kt + 515],
                                 start=(kt == 0), stop=(kt == NE - 1))
            with nc.Block() as blk:
                @blk.vector
                def _(v):
                    v.tensor_copy(dtbuf[:, :], psD[0:64, :])
            esW.close()
            nc.leave_named_scope(f"inproj{half}", sc_ip[0], False)

            if half == 0:
                dump("d_gate", gatebuf[:, :])
                dump("d_hid", hidfm[:, :])
                dump("d_bfm", bfm[:, :])
                dump("d_cfm", cfm[:, :])
                dump("d_dt", dtbuf[:, :])
            for ct in range(32):
                with nc.Block() as blk:
                    @blk.scalar
                    def _(s, ct=ct):
                        s.activation(gatebuf[:, 512 * ct:512 * (ct + 1)],
                                     gatebuf[:, 512 * ct:512 * (ct + 1)], AF.Silu)

            with nc.Block() as blk:
                @blk.vector
                def _(v):
                    v.tensor_scalar(dtbuf[:, :], dtbuf[:, :], dtb[:, 0:1], None, ALU.add)
            with nc.Block() as blk:
                @blk.scalar
                def _(s):
                    s.activation(scr64[:, :], dtbuf[:, :], AF.Abs)
            with nc.Block() as blk:
                @blk.scalar
                def _(s):
                    s.activation(scr64[:, :], scr64[:, :], AF.Exp, scale=-1.0)
            with nc.Block() as blk:
                @blk.scalar
                def _(s):
                    s.activation(scr64[:, :], scr64[:, :], AF.Ln, bias=1.0)
                @blk.vector
                def _(v):
                    v.tensor_relu(dt2buf[:, :], dtbuf[:, :])
            with nc.Block() as blk:
                @blk.vector
                def _(v):
                    v.tensor_add(dt2buf[:, :], dt2buf[:, :], scr64[:, :])
                    v.tensor_scalar(dtbuf[:, :], dt2buf[:, :], acol[:, 0:1],
                                    None, ALU.mult)

            if half == 0:
                dump("d_dt2", dt2buf[:, :])
                dump("d_adt", dtbuf[:, :])
            sc_ssd = nc.enter_named_scope(f"ssd{half}", False)
            esM = ExitStack()
            hdt = [pb.sbuf(f"h{half}_hdt{k}", [128, 4096], BF16, es=esM) for k in range(2)]
            gmtbuf = pb.sbuf(f"h{half}_gmt", [128, 8 * 512], BF16, es=esM)
            acumfm = pb.sbuf(f"h{half}_acfm", [64, 256], F32, es=esM)
            acumtm = pb.sbuf(f"h{half}_actm", [128, 128], F32, es=esM)
            acumtm2 = pb.sbuf(f"h{half}_actm2", [128, 128], F32, es=esM)
            dt2tm = pb.sbuf(f"h{half}_dt2tm", [128, 128], F32, es=esM)
            acr8 = [pb.sbuf(f"h{half}_acr8_{i}", [1, 2048], F32, es=esM) for i in range(2)]
            a255row = pb.sbuf(f"h{half}_a255", [1, 64], F32, es=esM)
            s1buf = [pb.sbuf(f"h{half}_s1_{i}", [128, 512], BF16, es=esM) for i in range(2)]
            t1buf = [pb.sbuf(f"h{half}_t1_{i}", [128, 512], BF16, es=esM) for i in range(2)]
            t2buf = [pb.sbuf(f"h{half}_t2_{i}", [128, 512], BF16, es=esM) for i in range(2)]
            mbuf = [pb.sbuf(f"h{half}_m_{i}", [128, 512], BF16, es=esM) for i in range(2)]
            ev64 = pb.sbuf(f"h{half}_ev64", [128, 512], F32, es=esM)
            for c2 in range(2):
                esS = ExitStack()
                psT2 = pb.psum(f"h{half}_psT_{c2}", [128, 128], BF16, es=esS)
                esSf = ExitStack()
                psTf = pb.psum(f"h{half}_psTf_{c2}", [128, 128], F32, es=esSf)
                with nc.Block() as blk:
                    @blk.vector
                    def _(v, c2=c2):
                        v.tensor_tensor_scan(acumfm[:, :], ones64[:, 0:256],
                                             dtbuf[:, 256 * c2:256 * (c2 + 1)],
                                             0.0, ALU.mult, ALU.add)
                for kc in range(2):
                    with nc.Block() as blk:
                        @blk.tensor
                        def _(t, kc=kc):
                            t.transpose(psTf[0:128, 0:64], acumfm[:, 128 * kc:128 * (kc + 1)],
                                        pb.ident_f[0:64, 0:64])
                    with nc.Block() as blk:
                        @blk.vector
                        def _(v, kc=kc):
                            v.tensor_copy(acumtm[:, 64 * kc:64 * (kc + 1)], psTf[0:128, 0:64])
                    with nc.Block() as blk:
                        @blk.tensor
                        def _(t, kc=kc, c2=c2):
                            t.transpose(psTf[0:128, 0:64],
                                        dt2buf[:, 256 * c2 + 128 * kc:256 * c2 + 128 * (kc + 1)],
                                        pb.ident_f[0:64, 0:64])
                    with nc.Block() as blk:
                        @blk.vector
                        def _(v, kc=kc):
                            v.tensor_copy(dt2tm[:, 64 * kc:64 * (kc + 1)], psTf[0:128, 0:64])
                with nc.Block() as blk:
                    @blk.sync
                    def _(sy):
                        pb.dma(sy, acum_dram.ap(), acumfm[:, :], name="st")
                        pb.wait(sy, "st")
                with nc.Block() as blk:
                    @blk.sync
                    def _(sy):
                        pb.dma(sy, a255row[:, :].rearrange("o (h q) -> o h q", h=64),
                               acum_dram.ap()[:, 255:256].rearrange("(o h) q -> o h q", o=1),
                               allow_slow_non_contiguous=True)
                        pb.wait(sy, "ld")
                with nc.Block() as blk:
                    @blk.tensor
                    def _(t):
                        t.matmul(psTf[:, 0:64], pb.ones_row_f[:, :], a255row[:, :],
                                 start=True, stop=True)
                with nc.Block() as blk:
                    @blk.vector
                    def _(v):
                        for kc in range(2):
                            v.tensor_sub(acumtm2[:, 64 * kc:64 * (kc + 1)], psTf[:, 0:64],
                                         acumtm[:, 64 * kc:64 * (kc + 1)])
                if half == 0 and c2 == 0:
                    dump("d_acum", acumfm[:, :])
                    dump("d_actm", acumtm[:, :])
                    dump("d_actm2", acumtm2[:, :])
                esSf.close()
                psT2b = pb.psum(f"h{half}_psT2b_{c2}", [128, 128], BF16, es=esS)
                transpose_seq(pb, [
                    (hidfm[:, 512 * ct + 256 * c2 + 128 * kc:
                           512 * ct + 256 * c2 + 128 * (kc + 1)],
                     hdt[kc][:, 128 * ct:128 * (ct + 1)], pb.ident[:, :])
                    for ct in range(32) for kc in range(2)], [psT2, psT2b])
                with nc.Block() as blk:
                    @blk.vector
                    def _(v):
                        for kc in range(2):
                            for hh in range(64):
                                v.tensor_scalar(hdt[kc][:, 64 * hh:64 * (hh + 1)],
                                                hdt[kc][:, 64 * hh:64 * (hh + 1)],
                                                dt2tm[:, 64 * kc + hh:64 * kc + hh + 1],
                                                None, ALU.mult)
                sT = pb.sem("trT"); sE = pb.sem("trE")
                bT = pb.cnt["trT"]; bE = pb.cnt["trE"]
                with nc.Block() as blk:
                    @blk.tensor
                    def _(t, c2=c2, bT=bT, bE=bE):
                        for g8 in range(8):
                            if g8 >= 2:
                                t.wait_ge(sE, bE + g8 - 1)
                            for kc in range(2):
                                mm = t.matmul(ps512[g8 % 2][:, 256 * kc:256 * (kc + 1)],
                                              bfm[:, 512 * g8 + 256 * c2 + 128 * kc:
                                                  512 * g8 + 256 * c2 + 128 * (kc + 1)],
                                              cfm[:, 512 * g8 + 256 * c2:
                                                  512 * g8 + 256 * c2 + 256],
                                              start=True, stop=True)
                            mm.then_inc(sT, 1)
                    @blk.vector
                    def _(v, bT=bT, bE=bE):
                        for g8 in range(8):
                            v.wait_ge(sT, bT + g8 + 1)
                            v.tensor_copy(gmtbuf[:, 512 * g8:512 * (g8 + 1)],
                                          ps512[g8 % 2][:, :]).then_inc(sE, 1)
                pb.cnt["trT"] += 8
                pb.cnt["trE"] += 8
                if half == 0 and c2 == 0:
                    dump("d_hdt0", hdt[0][:, :])
                    dump("d_hdt1", hdt[1][:, :])
                    dump("d_gmt", gmtbuf[:, :])
                psA = [pb.psum(f"h{half}_psA{c2}_{i}", [128, 256], F32, es=esS)
                       for i in range(2)]
                psY = [pb.psum(f"h{half}_psY{c2}_{i}", [128, 256], F32, es=esS)
                       for i in range(2)]
                sA = pb.sem("ssdA"); bA = pb.cnt["ssdA"]
                sS = pb.sem("ssdS"); bS = pb.cnt["ssdS"]
                sT1 = pb.sem("ssdT1"); bT1 = pb.cnt["ssdT1"]
                sT2 = pb.sem("ssdT2"); bT2 = pb.cnt["ssdT2"]
                sW = pb.sem("ssdW"); bW = pb.cnt["ssdW"]
                sY = pb.sem("ssdY"); bY = pb.cnt["ssdY"]
                sE = pb.sem("ssdE"); bE = pb.cnt["ssdE"]
                sR = pb.sem("ssdR"); bR = pb.cnt["ssdR"]
                with nc.Block() as blk:
                    @blk.sync
                    def _(sy, bA=bA, bR=bR):
                        for oct in range(8):
                            if oct >= 2:
                                sy.wait_ge(sA, bA + 8 * (oct - 1))
                            sy.dma_start(
                                out=acr8[oct % 2][:, :].rearrange("o (h q) -> o h q", h=8),
                                in_=acum_dram.ap()[8 * oct:8 * (oct + 1), :]
                                .rearrange("(o h) q -> o h q", o=1)).then_inc(sR, 16)
                    @blk.tensor
                    def _(t, c2=c2, bA=bA, bS=bS, bT2=bT2, bW=bW, bE=bE, bR=bR):
                        for h in range(65):
                            if h < 64:
                                if h % 8 == 0:
                                    t.wait_ge(sR, bR + 16 * (h // 8 + 1))
                                if h >= 2:
                                    t.wait_ge(sS, bS + h - 1)
                                    t.wait_ge(sT2, bT2 + h - 1)
                                t.matmul(psA[h % 2][:, :], pb.ones_row_f[:, :],
                                         acr8[(h // 8) % 2][0:1, 256 * (h % 8):256 * (h % 8 + 1)],
                                         start=True, stop=True).then_inc(sA, 1)
                            if h >= 1:
                                hh = h - 1
                                pi = hh % 2
                                t.wait_ge(sW, bW + hh + 1)
                                if hh >= 2:
                                    t.wait_ge(sE, bE + hh - 1)
                                ph = 64 * (hh % 2)
                                tp = (0, ph) if ph else None
                                t.matmul(psY[pi][ph:ph + 64, :],
                                         hdt[0][:, 64 * hh:64 * (hh + 1)],
                                         mbuf[pi][:, 0:256], start=True, stop=False,
                                         tile_position=tp)
                                t.matmul(psY[pi][ph:ph + 64, :],
                                         hdt[1][:, 64 * hh:64 * (hh + 1)],
                                         mbuf[pi][:, 256:512], start=False, stop=True,
                                         tile_position=tp).then_inc(sY, 1)
                    @blk.vector
                    def _(v, c2=c2, bA=bA, bS=bS, bT1=bT1, bY=bY):
                        for h in range(65):
                            if h < 64:
                                v.wait_ge(sA, bA + h + 1)
                                if h >= 2:
                                    v.wait_ge(sT1, bT1 + h - 1)
                                pi = h % 2
                                v.tensor_scalar(s1buf[pi][:, 0:256], psA[pi][:, :],
                                                acumtm[:, h:h + 1], 0.0,
                                                ALU.subtract, ALU.min)
                                v.tensor_scalar(s1buf[pi][:, 256:512], psA[pi][:, :],
                                                acumtm[:, 64 + h:64 + h + 1], 0.0,
                                                ALU.subtract, ALU.min).then_inc(sS, 1)
                            if h >= 1:
                                hh = h - 1
                                pi = hh % 2
                                v.wait_ge(sY, bY + hh + 1)
                                ph = 64 * (hh % 2)
                                g2 = hh // 2
                                csl = slice(512 * g2 + 256 * c2, 512 * g2 + 256 * c2 + 256)
                                esl = ev64[ph:ph + 64, 256 * pi:256 * (pi + 1)]
                                v.tensor_scalar(esl, hidfm[ph:ph + 64, csl],
                                                dpair[ph:ph + 64, g2:g2 + 1], None, ALU.mult)
                                v.tensor_add(esl, esl, psY[pi][ph:ph + 64, :])
                                v.tensor_mul(hidfm[ph:ph + 64, csl], esl,
                                             gatebuf[ph:ph + 64, csl]).then_inc(sE, 1)
                    @blk.scalar
                    def _(s, bS=bS, bW=bW):
                        for h in range(64):
                            s.wait_ge(sS, bS + h + 1)
                            if h >= 2:
                                s.wait_ge(sW, bW + h - 1)
                            pi = h % 2
                            s.activation(t1buf[pi][:, :], s1buf[pi][:, :], AF.Exp
                                         ).then_inc(sT1, 1)
                            s.activation(t2buf[pi][:, 0:256], psA[pi][:, :], AF.Exp,
                                         bias=acumtm2[:, h:h + 1])
                            s.activation(t2buf[pi][:, 256:512], psA[pi][:, :], AF.Exp,
                                         bias=acumtm2[:, 64 + h:64 + h + 1]).then_inc(sT2, 1)
                    @blk.gpsimd
                    def _(g, bT1=bT1, bT2=bT2, bY=bY):
                        for h in range(64):
                            g.wait_ge(sT1, bT1 + h + 1)
                            g.wait_ge(sT2, bT2 + h + 1)
                            if h >= 2:
                                g.wait_ge(sY, bY + h - 1)
                            pi = h % 2
                            g.tensor_mul(mbuf[pi][:, :], t1buf[pi][:, :], tri_w[:, :])
                            g.tensor_add(mbuf[pi][:, :], mbuf[pi][:, :], t2buf[pi][:, :])
                            g.tensor_mul(mbuf[pi][:, :], mbuf[pi][:, :],
                                         gmtbuf[:, 512 * (h % 8):512 * (h % 8 + 1)]
                                         ).then_inc(sW, 1)
                for nm_ in ("ssdA", "ssdS", "ssdT1", "ssdT2", "ssdW", "ssdY", "ssdE"):
                    pb.cnt[nm_] += 64
                pb.cnt["ssdR"] += 8 * 16
                esS.close()
            esM.close()
            nc.leave_named_scope(f"ssd{half}", sc_ssd[0], False)

            if half == 0:
                dump("d_yg", hidfm[:, :])
            sc_op = nc.enter_named_scope(f"outproj{half}", False)
            esO = ExitStack()
            sq_ps = pb.psum(f"h{half}_sqps", [1, 512], F32, es=esO)
            arow = pb.sbuf(f"h{half}_arow", [1, 512], F32, es=esO)
            abc512 = pb.sbuf(f"h{half}_abc512", [128, 512], F32, es=esO)
            sqb = pb.sbuf(f"h{half}_sqb", [128, 512], BF16, es=esO)
            for ct in range(32):
                with nc.Block() as blk:
                    @blk.scalar
                    def _(s, ct=ct):
                        s.activation(sqb[:, :], hidfm[:, 512 * ct:512 * (ct + 1)], AF.Square)
                with nc.Block() as blk:
                    @blk.tensor
                    def _(t, ct=ct):
                        t.matmul(sq_ps[:, :], pb.ones_col[:, :], sqb[:, :],
                                 start=(ct == 0), stop=(ct == 31))
            with nc.Block() as blk:
                @blk.vector
                def _(v):
                    v.tensor_scalar(arow[:, :], sq_ps[:, :], 1.0 / INTER, 1e-6,
                                    ALU.mult, ALU.add)
            with nc.Block() as blk:
                @blk.scalar
                def _(s):
                    s.activation(arow[:, :], arow[:, :], AF.Sqrt)
            with nc.Block() as blk:
                @blk.vector
                def _(v):
                    v.reciprocal(arow[:, :], arow[:, :])
            with nc.Block() as blk:
                @blk.tensor
                def _(t):
                    t.matmul(ps512[0][:, :], pb.ones_row_f[:, :], arow[:, :],
                             start=True, stop=True)
            with nc.Block() as blk:
                @blk.vector
                def _(v):
                    v.tensor_copy(abc512[:, :], ps512[0][:, :])
            for ct in range(32):
                with nc.Block() as blk:
                    @blk.vector
                    def _(v, ct=ct):
                        sl = hidfm[:, 512 * ct:512 * (ct + 1)]
                        v.tensor_mul(sl, sl, abc512[:, :])
                        v.tensor_scalar(sl, sl, nw[:, ct:ct + 1], None, ALU.mult)
            wband2 = pb.sbuf(f"h{half}_wband2", [128, 2 * 32 * 256], BF16, es=esO)
            pso = [pb.psum(f"h{half}_pso{i}", [128, 512], F32, es=esO) for i in range(4)]

            def wo_dma(sy, g):
                off = (g % 2) * 32 * 256
                for q4 in range(8):
                    pb.dma(sy, wband2[:, off + q4 * 1024:off + (q4 + 1) * 1024]
                           .rearrange("p (kt m) -> p kt m", kt=4),
                           wout_ap[512 * q4:512 * (q4 + 1), 256 * g:256 * (g + 1)]
                           .rearrange("(kt p) m -> p kt m", p=128), name=f"wo{half}d")

            def wo_mm(t, g):
                off = (g % 2) * 32 * 256
                ps = pso[2 * (g % 2):2 * (g % 2) + 2]
                for mi in range(2):
                    for kt in range(32):
                        wo = off + kt * 256 + mi * 128
                        mm = t.matmul(ps[mi][:, :], wband2[:, wo:wo + 128],
                                      hidfm[:, 512 * kt:512 * (kt + 1)],
                                      start=(kt == 0), stop=(kt == 31))
                return mm

            def wo_evac(v, g):
                ps = pso[2 * (g % 2):2 * (g % 2) + 2]
                for mi in range(2):
                    m0 = 2 * g + mi
                    cp = v.tensor_add(x1buf[:, 512 * m0:512 * (m0 + 1)], ps[mi][:, :],
                                      xfm[:, 515 * m0 + 3:515 * m0 + 515])
                return cp
            pipe_mm(pb, f"wo{half}", 8, wo_dma, wo_mm, wo_evac)
            esO.close()
            nc.leave_named_scope(f"outproj{half}", sc_op[0], False)
            ln_fm(pb, lambda ct: x1buf[:, 512 * ct:512 * (ct + 1)], NE, 512,
                  lambda ct: x1buf[:, 512 * ct:512 * (ct + 1)], tag=f"a{half}")
            with nc.Block() as blk:
                @blk.sync
                def _(sy, half=half):
                    pb.dma(sy, x1_ap[:, 512 * half:512 * (half + 1)]
                           .rearrange("(ct p) t -> p ct t", p=128),
                           x1buf[:, :].rearrange("p (ct t) -> p ct t", ct=NE), name="st")
                    pb.wait(sy, "st")
            esH.close()


def build_l0attn(debug=False):
    nc = bass.Bass("TRN2", target_bir_lowering=False)
    pb = PB(nc)
    x_lo = nc.dram_tensor("x_lo", [515, EMB], F32, kind="ExternalInput").ap()
    x_hi = nc.dram_tensor("x_hi", [515, EMB], F32, kind="ExternalInput").ap()
    wproj = nc.dram_tensor("w_inproj_t", [EMB, PROJ], F32, kind="ExternalInput").ap()
    cw = nc.dram_tensor("conv_w", [128, 192], F32, kind="ExternalInput").ap()
    cb = nc.dram_tensor("conv_b", [128, 48], F32, kind="ExternalInput").ap()
    dtb = nc.dram_tensor("dt_bias", [64, 1], F32, kind="ExternalInput").ap()
    alog = nc.dram_tensor("a_log", [64, 1], F32, kind="ExternalInput").ap()
    dpair = nc.dram_tensor("d_pair", [128, 32], F32, kind="ExternalInput").ap()
    nw = nc.dram_tensor("norm_w", [128, 32], F32, kind="ExternalInput").ap()
    wout = nc.dram_tensor("w_outproj_t", [INTER, EMB], F32, kind="ExternalInput").ap()
    x1 = nc.dram_tensor("x1fm", [EMB, 1024], BF16, kind="ExternalOutput").ap()
    dbg = None
    if debug:
        dbg = {}
        for nm_, shp, dt_ in (("d_gate", [128, 32 * 512], BF16), ("d_hid", [128, 32 * 512], BF16),
                              ("d_bfm", [128, 8 * 512], BF16), ("d_cfm", [128, 8 * 512], BF16),
                              ("d_dt", [64, 512], F32), ("d_dt2", [64, 512], F32),
                              ("d_adt", [64, 512], F32), ("d_acum", [64, 256], F32),
                              ("d_actm", [128, 128], F32), ("d_actm2", [128, 128], F32),
                              ("d_hdt0", [128, 4096], BF16), ("d_hdt1", [128, 4096], BF16),
                              ("d_gmt", [128, 8 * 512], BF16), ("d_yg", [128, 32 * 512], BF16)):
            dbg[nm_] = nc.dram_tensor(nm_, shp, dt_, kind="ExternalOutput").ap()
    with pb.es:
        pb.consts()
        _l0attn_body(pb, (x_lo, x_hi), wproj, cw, cb, dtb, alog, dpair, nw, wout, x1, dbg=dbg)
    return nc



def _l1attn_body(pb, core_blocks_dummy, x2_ap, qkv_ap, oproj_ap, gam_ap, bet_ap,
                 allow_ap, x3_ap):
    nc = pb.nc
    T = 1024
    SC = 1.0 / math.sqrt(2048.0)
    kag_in = nc.dram_tensor("kag_in", [EMB, T], BF16)
    vag_in = nc.dram_tensor("vag_in", [T, EMB], BF16)
    KAG = nc.dram_tensor("KAG", [8 * EMB, T], BF16, addr_space="Shared")
    VAG = nc.dram_tensor("VAG", [8 * T, EMB], BF16, addr_space="Shared")
    esP = ExitStack()
    with esP:
        gam = pb.sbuf("t_gam", [128, 16], F32, es=esP)
        bet = pb.sbuf("t_bet", [128, 16], F32, es=esP)
        xbuf = pb.sbuf("t_x", [128, NE * T], BF16, es=esP)
        qT = pb.sbuf("t_qT", [128, NE * T], BF16, es=esP)
        with nc.Block() as blk:
            @blk.sync
            def _(sy):
                pb.dma(sy, gam[:, :], gam_ap)
                pb.dma(sy, bet[:, :], bet_ap)
                pb.dma(sy, xbuf[:, :].rearrange("p (ct t) -> p ct t", ct=NE),
                       x2_ap.rearrange("(ct p) t -> p ct t", p=128))
                pb.wait(sy, "ld")
        sc_pj = nc.enter_named_scope("l1aproj", False)
        es1 = ExitStack()
        xn = pb.sbuf("t_xn", [128, NE * T], BF16, es=es1)
        ln_fm(pb, lambda ct: xbuf[:, ct * T:(ct + 1) * T], NE, T,
              lambda ct: xn[:, ct * T:(ct + 1) * T], gamma=gam, beta=bet, tag="t1")
        ktl = pb.sbuf("t_ktl", [128, NE * T], BF16, es=es1)
        vt = pb.sbuf("t_vt", [128, 8 * EMB], BF16, es=es1)
        wband = pb.sbuf("t_wband", [128, 2 * NE * 512], BF16, es=es1)
        psq = [pb.psum(f"t_psq{i}", [128, 512], F32, es=es1) for i in range(8)]
        def qk_dma(sy, g):
            which, mg = divmod(g, 8)
            colbase = 2048 * which
            m0 = 2 * mg
            wb = wband[:, (g % 2) * 4096:(g % 2) * 4096 + 4096]
            for q4 in range(4):
                pb.dma(sy, wb[:, q4 * 1024:(q4 + 1) * 1024]
                       .rearrange("p (kt m) -> p kt m", kt=4),
                       qkv_ap[512 * q4:512 * (q4 + 1),
                              colbase + 128 * m0:colbase + 128 * (m0 + 2)]
                       .rearrange("(kt p) m -> p kt m", p=128), name="qkd")

        def qk_mm(t, g):
            wb = wband[:, (g % 2) * 4096:(g % 2) * 4096 + 4096]
            ps = psq[4 * (g % 2):4 * (g % 2) + 4]
            for mi in range(2):
                for th in range(2):
                    for kt in range(NE):
                        off = kt * 256 + mi * 128
                        mm = t.matmul(ps[2 * mi + th][:, :], wb[:, off:off + 128],
                                      xn[:, kt * T + 512 * th:kt * T + 512 * (th + 1)],
                                      start=(kt == 0), stop=(kt == NE - 1))
            return mm

        def qk_evac(v, g):
            which, mg = divmod(g, 8)
            m0 = 2 * mg
            dstbuf = qT if which == 0 else ktl
            ps = psq[4 * (g % 2):4 * (g % 2) + 4]
            for mi in range(2):
                for th in range(2):
                    cp = v.tensor_copy(
                        dstbuf[:, (m0 + mi) * T + 512 * th:(m0 + mi) * T + 512 * (th + 1)],
                        ps[2 * mi + th][:, :])
            return cp
        pipe_mm(pb, "qk", 16, qk_dma, qk_mm, qk_evac)
        def v_dma(sy, g):
            wb = wband[:, (g % 2) * 8192:(g % 2) * 8192 + 8192]
            for q4 in range(4):
                pb.dma(sy, wb[:, q4 * 2048:(q4 + 1) * 2048]
                       .rearrange("p (kt m) -> p kt m", kt=4),
                       qkv_ap[512 * q4:512 * (q4 + 1),
                              4096 + 512 * g:4096 + 512 * (g + 1)]
                       .rearrange("(kt p) m -> p kt m", p=128), name="qkd")

        def v_mm(t, g):
            wb = wband[:, (g % 2) * 8192:(g % 2) * 8192 + 8192]
            for tc in range(8):
                for kt in range(NE):
                    mm = t.matmul(psq[tc][:, :], xn[:, kt * T + 128 * tc:kt * T + 128 * (tc + 1)],
                                  wb[:, kt * 512:(kt + 1) * 512],
                                  start=(kt == 0), stop=(kt == NE - 1))
            return mm

        def v_evac(v, g):
            for tc in range(8):
                cp = v.tensor_copy(vt[:, tc * EMB + 512 * g:tc * EMB + 512 * (g + 1)],
                                   psq[tc][:, :])
            return cp
        pipe_mm(pb, "qk", 4, v_dma, v_mm, v_evac, double_psum=False)
        with nc.Block() as blk:
            @blk.sync
            def _(sy):
                pb.dma(sy, kag_in.ap().rearrange("(ct p) t -> p ct t", p=128),
                       ktl[:, :].rearrange("p (ct t) -> p ct t", ct=NE), name="st")
                pb.dma(sy, vag_in.ap().rearrange("(tc p) m -> p tc m", p=128),
                       vt[:, :].rearrange("p (tc m) -> p tc m", tc=8), name="st")
                pb.wait(sy, "st")
        with nc.Block() as blk:
            @blk.gpsimd
            def _(g):
                ccs = pb.sem("cc")
                for src, dst in ((kag_in, KAG), (vag_in, VAG)):
                    g.collective_compute(
                        "AllGather", ALU.bypass, replica_groups=[list(range(NCORES))],
                        ins=[src.ap().opt()], outs=[dst.ap().opt()]).then_inc(ccs)
                    pb.cnt["cc"] += 1
                g.wait_ge(ccs, pb.cnt["cc"])
        es1.close()
        nc.leave_named_scope("l1aproj", sc_pj[0], False)
        sc_at = nc.enter_named_scope("l1aattn", False)

        OT = pb.sbuf("t_OT", [128, NE * T], BF16, es=esP)
        es2 = ExitStack()
        kbuf = [pb.sbuf(f"t_kb{i}", [128, 16 * 512], BF16, es=es2) for i in range(2)]
        vbuf = [pb.sbuf(f"t_vb{i}", [128, 4 * EMB], BF16, es=es2) for i in range(2)]
        abuf = pb.sbuf("t_ab", [128, 4 * 512], BF16, es=es2)
        pbuf = [pb.sbuf(f"t_pb{i}", [128, 512], BF16, es=es2) for i in range(2)]
        ptbuf = [pb.sbuf(f"t_pt{i}", [128, 512], BF16, es=es2) for i in range(2)]
        obig = [pb.sbuf(f"t_obig{i}", [128, EMB], F32, es=es2) for i in range(2)]
        obuf = pb.sbuf("t_ob", [128, EMB], BF16, es=es2)
        mrow = pb.sbuf("t_mrow", [128, 4 * 20], F32, es=es2)
        lrow = pb.sbuf("t_lrow", [128, 4 * 20], F32, es=es2)
        psS = [pb.psum(f"t_psS{i}", [128, 512], F32, es=es2) for i in range(2)]
        psPT = [pb.psum(f"t_psPT{i}", [128, 512], BF16, es=es2) for i in range(2)]
        psO = [pb.psum(f"t_psO{i}", [128, 512], F32, es=es2) for i in range(4)]
        pb.sem("aK")
        for hb in range(2):
          for sub in range(2):
            nslots = 8 if hb == 0 else 16
            si0 = 32 * hb
            ni = 2 * nslots
            sM = pb.sem("aM"); bM = pb.cnt["aM"]
            sR = pb.sem("aR"); bR = pb.cnt["aR"]
            bKl = pb.cnt["aK"]
            with nc.Block() as blk:
                @blk.sync
                def _(sy, hb=hb, nslots=nslots, bM=bM, sub=sub):
                    for j in range(nslots):
                        if j >= 2:
                            sy.wait_ge(sM, bM + 2 * (j - 1))
                        r = own(j)
                        co = 512 * (0 if j <= 7 else 1)
                        pb.dma(sy, kbuf[j % 2][:, :].rearrange("p (kt f) -> p kt f", kt=NE),
                               KAG.ap()[EMB * r:EMB * (r + 1), co:co + 512]
                               .rearrange("(kt p) f -> p kt f", p=128), name="aK")
                @blk.tensor
                def _(t, hb=hb, ni=ni, bKl=bKl, bM=bM, bR=bR, sub=sub):
                    for i in range(ni):
                        j, qt = divmod(i, 2)
                        if qt == 0:
                            t.wait_ge(pb.sem("aK"), bKl + 16 * (j + 1))
                        if i >= 2:
                            t.wait_ge(sR, bR + i - 1)
                        mq = 4 * hb + 2 * sub + qt
                        for kt in range(NE):
                            mm = t.matmul(psS[i % 2][:, :],
                                          qT[:, kt * T + 128 * mq:kt * T + 128 * (mq + 1)],
                                          kbuf[j % 2][:, kt * 512:(kt + 1) * 512],
                                          start=(kt == 0), stop=(kt == NE - 1))
                        mm.then_inc(sM, 1)
                @blk.vector
                def _(v, ni=ni, bM=bM):
                    for i in range(ni):
                        j, qt = divmod(i, 2)
                        v.wait_ge(sM, bM + i + 1)
                        v.reduce_max(mrow[:, 20 * qt + j:20 * qt + j + 1],
                                     psS[i % 2][:, :], AX.X).then_inc(sR, 1)
            pb.cnt["aM"] += ni
            pb.cnt["aR"] += ni
            with nc.Block() as blk:
                @blk.vector
                def _(v, nslots=nslots):
                    for qt in range(4):
                        pb.guard(v, v.reduce_max(mrow[:, 20 * qt + 16:20 * qt + 17],
                                                 mrow[:, 20 * qt:20 * qt + nslots], AX.X))
                        v.tensor_scalar(mrow[:, 20 * qt + 17:20 * qt + 18],
                                        mrow[:, 20 * qt + 16:20 * qt + 17],
                                        -SC, None, ALU.mult)
            bKl = pb.cnt["aK"]
            bM = pb.cnt["aM"]
            sE = pb.sem("aE"); bE = pb.cnt["aE"]
            sD = pb.sem("aD"); bD = pb.cnt["aD"]
            sPT = pb.sem("aPT"); bPT = pb.cnt["aPT"]
            sV = pb.sem("aV"); bV = pb.cnt["aV"]
            sO = pb.sem("aO"); bO = pb.cnt["aO"]
            sA = pb.sem("aA"); bA = pb.cnt["aA"]
            with nc.Block() as blk:
                @blk.sync
                def _(sy, hb=hb, nslots=nslots, si0=si0, bO=bO, sub=sub):
                    for j in range(nslots):
                        if j >= 2:
                            sy.wait_ge(sO, bO + 2 * (j - 1))
                        r = own(j)
                        co = 512 * (0 if j <= 7 else 1)
                        pb.dma(sy, kbuf[j % 2][:, :].rearrange("p (kt f) -> p kt f", kt=NE),
                               KAG.ap()[EMB * r:EMB * (r + 1), co:co + 512]
                               .rearrange("(kt p) f -> p kt f", p=128), name="aK")
                        pb.dma(sy, vbuf[j % 2][:, :].rearrange("p (kc f) -> p kc f", kc=4),
                               VAG.ap()[T * r + co:T * r + co + 512, :]
                               .rearrange("(kc p) f -> p kc f", p=128), name="aK")
                        for qt in range(2):
                            pb.dma(sy, abuf[:, 512 * (2 * (j % 2) + qt):
                                            512 * (2 * (j % 2) + qt) + 512],
                                   allow_ap[si0 + nslots * (2 * sub + qt) + j, :, :],
                                   name="aK")
                @blk.tensor
                def _(t, hb=hb, ni=ni, bKl=bKl, bE=bE, bD=bD, bV=bV, bA=bA, sub=sub):
                    for i in range(ni + 1):
                        if i < ni:
                            j, qt = divmod(i, 2)
                            if qt == 0:
                                t.wait_ge(pb.sem("aK"), bKl + 64 * (j + 1))
                            if i >= 2:
                                t.wait_ge(sE, bE + i - 1)
                            mq = 4 * hb + 2 * sub + qt
                            for kt in range(NE):
                                mm = t.matmul(psS[i % 2][:, :],
                                              qT[:, kt * T + 128 * mq:kt * T + 128 * (mq + 1)],
                                              kbuf[j % 2][:, kt * 512:(kt + 1) * 512],
                                              start=(kt == 0), stop=(kt == NE - 1))
                            mm.then_inc(sM, 1)
                        if i >= 1:
                            k = i - 1
                            jj = k // 4
                            t.wait_ge(sD, bD + k + 1)
                            if k >= 2:
                                t.wait_ge(sV, bV + k - 1)
                            for kc in range(4):
                                tt = t.transpose(psPT[k % 2][:, 128 * kc:128 * (kc + 1)],
                                                 pbuf[k % 2][:, 128 * kc:128 * (kc + 1)],
                                                 pb.ident[:, :])
                            tt.then_inc(sPT, 1)
                            t.wait_ge(sV, bV + k + 1)
                            if k >= 1:
                                t.wait_ge(sA, bA + k)
                            for kc in range(4):
                                for og in range(4):
                                    mm = t.matmul(psO[og][:, :],
                                                  ptbuf[k % 2][:, 128 * kc:128 * (kc + 1)],
                                                  vbuf[jj % 2][:, kc * EMB + 512 * og:
                                                               kc * EMB + 512 * (og + 1)],
                                                  start=(kc == 0), stop=(kc == 3))
                            mm.then_inc(sO, 1)
                @blk.scalar
                def _(s, ni=ni, bM=bM, bD=bD, bPT=bPT):
                    for i in range(ni):
                        j, qt = divmod(i, 2)
                        s.wait_ge(sM, bM + i + 1)
                        if i >= 2:
                            s.wait_ge(sD, bD + i - 1)
                            s.wait_ge(sPT, bPT + i - 1)
                        s.activation(pbuf[i % 2][:, :], psS[i % 2][:, :], AF.Exp,
                                     bias=mrow[:, 20 * qt + 17:20 * qt + 18],
                                     scale=SC).then_inc(sE, 1)
                @blk.vector
                def _(v, ni=ni, bE=bE, bPT=bPT, bO=bO):
                    for i in range(ni):
                        j, qt = divmod(i, 2)
                        v.wait_ge(sE, bE + i + 1)
                        v.tensor_mul(pbuf[i % 2][:, :], pbuf[i % 2][:, :],
                                     abuf[:, 512 * (2 * (j % 2) + qt):
                                          512 * (2 * (j % 2) + qt) + 512])
                        v.reduce_sum(lrow[:, 20 * qt + j:20 * qt + j + 1],
                                     pbuf[i % 2][:, :], AX.X).then_inc(sD, 1)
                        v.wait_ge(sPT, bPT + i + 1)
                        v.tensor_copy(ptbuf[i % 2][:, :], psPT[i % 2][:, :]).then_inc(sV, 1)
                        v.wait_ge(sO, bO + i + 1)
                        for og in range(4):
                            if i < 2:
                                cp = v.tensor_copy(obig[qt][:, 512 * og:512 * (og + 1)],
                                                   psO[og][:, :])
                            else:
                                cp = v.tensor_add(obig[qt][:, 512 * og:512 * (og + 1)],
                                                  obig[qt][:, 512 * og:512 * (og + 1)],
                                                  psO[og][:, :])
                        cp.then_inc(sA, 1)
            pb.cnt["aM"] += ni
            pb.cnt["aE"] += ni
            pb.cnt["aD"] += ni
            pb.cnt["aPT"] += ni
            pb.cnt["aV"] += ni
            pb.cnt["aO"] += ni
            pb.cnt["aA"] += ni
            for qt in range(2):
                m = 4 * hb + 2 * sub + qt
                with nc.Block() as blk:
                    @blk.vector
                    def _(v, qt=qt, nslots=nslots):
                        pb.guard(v, v.reduce_sum(lrow[:, 20 * qt + 16:20 * qt + 17],
                                                 lrow[:, 20 * qt:20 * qt + nslots], AX.X))
                        pb.guard(v, v.reciprocal(lrow[:, 20 * qt + 17:20 * qt + 18],
                                                 lrow[:, 20 * qt + 16:20 * qt + 17]))
                        for og in range(4):
                            v.tensor_scalar(obuf[:, 512 * og:512 * (og + 1)],
                                            obig[qt][:, 512 * og:512 * (og + 1)],
                                            lrow[:, 20 * qt + 17:20 * qt + 18],
                                            None, ALU.mult)
                transpose_seq(pb, [
                    (obuf[:, 128 * ct:128 * (ct + 1)],
                     OT[:, ct * T + 128 * m:ct * T + 128 * (m + 1)], pb.ident[:, :])
                    for ct in range(NE)], psPT)
        es2.close()
        nc.leave_named_scope("l1aattn", sc_at[0], False)
        sc_oo = nc.enter_named_scope("l1aoproj", False)
        es3 = ExitStack()
        wband2 = pb.sbuf("t_wband2", [128, 2 * 4096], BF16, es=es3)
        pso = [pb.psum(f"t_pso{i}", [128, 512], F32, es=es3) for i in range(8)]
        def op_dma(sy, g):
            wb = wband2[:, (g % 2) * 4096:(g % 2) * 4096 + 4096]
            for q4 in range(4):
                pb.dma(sy, wb[:, q4 * 1024:(q4 + 1) * 1024]
                       .rearrange("p (kt m) -> p kt m", kt=4),
                       oproj_ap[512 * q4:512 * (q4 + 1), 256 * g:256 * (g + 1)]
                       .rearrange("(kt p) m -> p kt m", p=128), name="opd")

        def op_mm(t, g):
            wb = wband2[:, (g % 2) * 4096:(g % 2) * 4096 + 4096]
            ps = pso[4 * (g % 2):4 * (g % 2) + 4]
            for mi in range(2):
                for th in range(2):
                    for kt in range(NE):
                        off = kt * 256 + mi * 128
                        mm = t.matmul(ps[2 * mi + th][:, :], wb[:, off:off + 128],
                                      OT[:, kt * T + 512 * th:kt * T + 512 * (th + 1)],
                                      start=(kt == 0), stop=(kt == NE - 1))
            return mm

        def op_evac(v, g):
            ps = pso[4 * (g % 2):4 * (g % 2) + 4]
            for mi in range(2):
                for th in range(2):
                    sl = slice((2 * g + mi) * T + 512 * th, (2 * g + mi) * T + 512 * (th + 1))
                    cp = v.scalar_tensor_tensor(xbuf[:, sl], xbuf[:, sl], 2.0,
                                                ps[2 * mi + th][:, :], ALU.mult, ALU.add)
            return cp
        pipe_mm(pb, "op", 8, op_dma, op_mm, op_evac)
        es3.close()
        nc.leave_named_scope("l1aoproj", sc_oo[0], False)
        ln_fm(pb, lambda ct: xbuf[:, ct * T:(ct + 1) * T], NE, T,
              lambda ct: xbuf[:, ct * T:(ct + 1) * T], tag="t2")
        with nc.Block() as blk:
            @blk.sync
            def _(sy):
                pb.dma(sy, x3_ap.rearrange("(ct p) t -> p ct t", p=128),
                       xbuf[:, :].rearrange("p (ct t) -> p ct t", ct=NE), name="st")
                pb.wait(sy, "st")


def build_l1attn():
    nc = bass.Bass("TRN2", target_bir_lowering=False)
    pb = PB(nc)
    x2 = nc.dram_tensor("x2fm", [EMB, 1024], BF16, kind="ExternalInput").ap()
    qkv = nc.dram_tensor("qkv1", [EMB, 6144], F32, kind="ExternalInput").ap()
    oproj = nc.dram_tensor("oproj", [EMB, EMB], F32, kind="ExternalInput").ap()
    gam = nc.dram_tensor("gamma", [128, 16], F32, kind="ExternalInput").ap()
    bet = nc.dram_tensor("beta", [128, 16], F32, kind="ExternalInput").ap()
    allow = nc.dram_tensor("allow", [96, 128, 512], BF16, kind="ExternalInput").ap()
    x3 = nc.dram_tensor("x3fm", [EMB, 1024], BF16, kind="ExternalOutput").ap()
    with pb.es:
        pb.consts()
        _l1attn_body(pb, None, x2, qkv, oproj, gam, bet, allow, x3)
    return nc



def _l1ffn_body(pb, x3_ap, wqkv_ap, wgate_ap, out_ap, dbg=None):
    nc = pb.nc

    def dump(name, src_ap):
        if dbg is None or name not in dbg:
            return
        with nc.Block() as blk:
            @blk.sync
            def _(sy):
                pb.dma(sy, dbg[name], src_ap, name="st")
                pb.wait(sy, "st")
    from concourse.aot_env import allow_ici_all_to_all
    T = 1024
    SC2 = 1.0 / math.sqrt(128.0)
    ksh_in = nc.dram_tensor("ksh_in", [128, T], BF16)
    vsh_in = nc.dram_tensor("vsh_in", [T, 128], BF16)
    knm_in = nc.dram_tensor("knm_in", [1, 2], F32)
    KSHg = nc.dram_tensor("KSHg", [8 * 128, T], BF16, addr_space="Shared")
    Vg = nc.dram_tensor("Vg", [8 * T, 128], BF16, addr_space="Shared")
    KNg = nc.dram_tensor("KNg", [8, 2], F32, addr_space="Shared")
    qa_in = nc.dram_tensor("qa_in", [EMB, T], BF16)
    QA = nc.dram_tensor("QA", [EMB, T], BF16)
    ya_in = nc.dram_tensor("ya_in", [8 * T, 256], BF16)
    YA = nc.dram_tensor("YA", [8 * T, 256], BF16)
    esP = ExitStack()
    with esP:
        xbuf = pb.sbuf("u_x", [128, NE * T], BF16, es=esP)
        triv = pb.sbuf("u_triv", [128, 4 * 512], BF16, es=esP)
        knm = pb.sbuf("u_knm", [1, 18], F32, es=esP)
        with nc.Block() as blk:
            @blk.sync
            def _(sy):
                pb.dma(sy, xbuf[:, :].rearrange("p (ct t) -> p ct t", ct=NE),
                       x3_ap.rearrange("(ct p) t -> p ct t", p=128))
                pb.wait(sy, "ld")
            @blk.vector
            def _(v):
                for vv in range(4):
                    v.tensor_scalar(triv[:, 512 * vv:512 * (vv + 1)], pb.iota_row[:, 0:512],
                                    pb.iota_col[:, 0:1], float(128 * vv),
                                    ALU.subtract, ALU.is_ge)

        sc_p2 = nc.enter_named_scope("l1fproj", False)
        es1 = ExitStack()
        qTl = pb.sbuf("u_qT", [128, NE * T], BF16, es=es1)
        ktm = pb.sbuf("u_ktm", [128, 8 * 512], BF16, es=es1)
        vtm = pb.sbuf("u_vtm", [128, 8 * 512], BF16, es=es1)
        kfull = pb.sbuf("u_kfull", [128, 8 * EMB], BF16, es=es1)
        vfull = pb.sbuf("u_vfull", [128, 8 * EMB], BF16, es=es1)
        wband = pb.sbuf("u_wband", [128, 2 * NE * 512], BF16, es=es1)
        glog = pb.sbuf("u_glog", [128, 8 * 16], F32, es=es1)
        kshtm = pb.sbuf("u_kshtm", [128, 8 * 128], BF16, es=es1)
        vshtm = pb.sbuf("u_vshtm", [128, 8 * 128], BF16, es=es1)
        kshT = pb.sbuf("u_kshT", [128, 8 * 128], BF16, es=es1)
        gsc = pb.sbuf("u_gsc", [128, 4], F32, es=es1)
        expg = pb.sbuf("u_expg", [128, 16], F32, es=es1)
        indb = pb.sbuf("u_indb", [128, 16], F32, es=es1)
        gt1b = pb.sbuf("u_gt1b", [128, 16], F32, es=es1)
        tmp128 = pb.sbuf("u_tmp128", [128, 128], BF16, es=es1)
        sq128 = pb.sbuf("u_sq128", [128, 128], F32, es=es1)
        kn2 = pb.sbuf("u_kn2", [128, 12], F32, es=es1)
        wgb = pb.sbuf("u_wgb", [128, NE * 16], F32, es=es1)
        es_pq = ExitStack()
        psq = [pb.psum(f"u_psq{i}", [128, 512], F32, es=es_pq) for i in range(8)]
        def q2_dma(sy, g):
            wb = wband[:, (g % 2) * 4096:(g % 2) * 4096 + 4096]
            for q4 in range(4):
                pb.dma(sy, wb[:, q4 * 1024:(q4 + 1) * 1024]
                       .rearrange("p (kt m) -> p kt m", kt=4),
                       wqkv_ap[512 * q4:512 * (q4 + 1), 256 * g:256 * (g + 1)]
                       .rearrange("(kt p) m -> p kt m", p=128), name="q2d")

        def q2_mm(t, g):
            wb = wband[:, (g % 2) * 4096:(g % 2) * 4096 + 4096]
            ps = psq[4 * (g % 2):4 * (g % 2) + 4]
            for mi in range(2):
                for th in range(2):
                    for kt in range(NE):
                        off = kt * 256 + mi * 128
                        mm = t.matmul(ps[2 * mi + th][:, :], wb[:, off:off + 128],
                                      xbuf[:, kt * T + 512 * th:kt * T + 512 * (th + 1)],
                                      start=(kt == 0), stop=(kt == NE - 1))
            return mm

        def q2_evac(v, g):
            ps = psq[4 * (g % 2):4 * (g % 2) + 4]
            for mi in range(2):
                for th in range(2):
                    cp = v.tensor_copy(
                        qTl[:, (2 * g + mi) * T + 512 * th:(2 * g + mi) * T + 512 * (th + 1)],
                        ps[2 * mi + th][:, :])
            return cp
        pipe_mm(pb, "q2", 8, q2_dma, q2_mm, q2_evac)
        def kv_dma(sy, g):
            wb = wband[:, (g % 2) * 8192:(g % 2) * 8192 + 8192]
            colbase = 2048 + 2048 * (g // 4)
            og = g % 4
            for q4 in range(4):
                pb.dma(sy, wb[:, q4 * 2048:(q4 + 1) * 2048]
                       .rearrange("p (kt m) -> p kt m", kt=4),
                       wqkv_ap[512 * q4:512 * (q4 + 1),
                               colbase + 512 * og:colbase + 512 * (og + 1)]
                       .rearrange("(kt p) m -> p kt m", p=128), name="q2d")

        def kv_mm(t, g):
            wb = wband[:, (g % 2) * 8192:(g % 2) * 8192 + 8192]
            for tc in range(8):
                for kt in range(NE):
                    mm = t.matmul(psq[tc][:, :],
                                  xbuf[:, kt * T + 128 * tc:kt * T + 128 * (tc + 1)],
                                  wb[:, kt * 512:(kt + 1) * 512],
                                  start=(kt == 0), stop=(kt == NE - 1))
            return mm

        def kv_evac(v, g):
            dstbuf = kfull if g < 4 else vfull
            og = g % 4
            for tc in range(8):
                cp = v.tensor_copy(dstbuf[:, tc * EMB + 512 * og:tc * EMB + 512 * (og + 1)],
                                   psq[tc][:, :])
            return cp
        pipe_mm(pb, "q2", 8, kv_dma, kv_mm, kv_evac, double_psum=False)
        xg32 = pb.sbuf("u_xg32", [128, T], F32, es=es1)
        with nc.Block() as blk:
            @blk.sync
            def _(sy):
                pb.dma(sy, wgb[:, :].rearrange("p (kt m) -> p kt m", kt=NE),
                       wgate_ap.rearrange("(kt p) m -> p kt m", p=128))
                pb.wait(sy, "ld")
        for kt in range(NE):
            with nc.Block() as blk:
                @blk.vector
                def _(v, kt=kt):
                    v.tensor_copy(xg32[:, :], xbuf[:, kt * T:(kt + 1) * T])
            with nc.Block() as blk:
                @blk.tensor
                def _(t, kt=kt):
                    for tc in range(8):
                        t.matmul(psq[tc][:, 0:16], xg32[:, 128 * tc:128 * (tc + 1)],
                                 wgb[:, kt * 16:(kt + 1) * 16],
                                 start=(kt == 0), stop=(kt == NE - 1))
        with nc.Block() as blk:
            @blk.vector
            def _(v):
                for tc in range(8):
                    v.tensor_copy(glog[:, 16 * tc:16 * (tc + 1)], psq[tc][:, 0:16])
        es_pq.close()
        dump("d_glog", glog[:, :])
        dump("d_vfull", vfull[:, 0:2048])
        dump("d_kfull", kfull[:, 0:2048])
        for tc in range(8):
            gsl = glog[:, 16 * tc:16 * (tc + 1)]
            with nc.Block() as blk:
                @blk.vector
                def _(v, gsl=gsl):
                    pb.guard(v, v.reduce_max(gsc[:, 0:1], gsl, AX.X))
                    v.tensor_scalar(gsc[:, 1:2], gsc[:, 0:1], -1.0, None, ALU.mult)
            with nc.Block() as blk:
                @blk.scalar
                def _(s, gsl=gsl):
                    s.activation(expg[:, :], gsl, AF.Exp, bias=gsc[:, 1:2])
            with nc.Block() as blk:
                @blk.vector
                def _(v, gsl=gsl):
                    v.reduce_sum(gsc[:, 2:3], expg[:, :], AX.X)
            with nc.Block() as blk:
                @blk.vector
                def _(v):
                    v.reciprocal(gsc[:, 3:4], gsc[:, 2:3])
            with nc.Block() as blk:
                @blk.vector
                def _(v, gsl=gsl):
                    v.tensor_scalar(indb[:, :], gsl, gsc[:, 0:1], None, ALU.is_equal)
                    v.tensor_scalar(gt1b[:, :], indb[:, :], gsc[:, 3:4], None, ALU.mult)
            with nc.Block() as blk:
                @blk.vector
                def _(v, tc=tc):
                    ksl = kshtm[:, 128 * tc:128 * (tc + 1)]
                    vsl = vshtm[:, 128 * tc:128 * (tc + 1)]
                    v.tensor_scalar(ksl, kfull[:, tc * EMB:tc * EMB + 128],
                                    indb[:, 0:1], None, ALU.mult)
                    v.tensor_scalar(vsl, vfull[:, tc * EMB:tc * EMB + 128],
                                    gt1b[:, 0:1], None, ALU.mult)
                    for h in range(1, 16):
                        v.tensor_scalar(tmp128[:, :],
                                        kfull[:, tc * EMB + 128 * h:tc * EMB + 128 * (h + 1)],
                                        indb[:, h:h + 1], None, ALU.mult)
                        v.tensor_add(ksl, ksl, tmp128[:, :])
                        v.tensor_scalar(tmp128[:, :],
                                        vfull[:, tc * EMB + 128 * h:tc * EMB + 128 * (h + 1)],
                                        gt1b[:, h:h + 1], None, ALU.mult)
                        v.tensor_add(vsl, vsl, tmp128[:, :])
            with nc.Block() as blk:
                @blk.vector
                def _(v, tc=tc):
                    ksl = kshtm[:, 128 * tc:128 * (tc + 1)]
                    pb.guard(v, v.tensor_mul(sq128[:, :], ksl, ksl))
                    v.reduce_sum(kn2[:, tc:tc + 1], sq128[:, :], AX.X)
        esT = ExitStack()
        psTk = pb.psum("u_psTk", [128, 128], BF16, es=esT)
        psTn = pb.psum("u_psTn", [1, 128], F32, es=esT)
        for hh in range(2):
            with nc.Block() as blk:
                @blk.vector
                def _(v, hh=hh):
                    v.reduce_max(kn2[:, 8 + hh:9 + hh], kn2[:, 4 * hh:4 * (hh + 1)], AX.X)
            with nc.Block() as blk:
                @blk.tensor
                def _(t, hh=hh):
                    t.transpose(psTn[0:1, 0:128], kn2[:, 8 + hh:9 + hh], pb.ident_f[:, :])
            with nc.Block() as blk:
                @blk.vector
                def _(v, hh=hh):
                    v.reduce_max(knm[0:1, hh:hh + 1], psTn[0:1, 0:128], AX.X)
        psTk2 = pb.psum("u_psTk2", [128, 128], BF16, es=esT)
        transpose_seq(pb, [
            (kshtm[:, 128 * tc:128 * (tc + 1)], kshT[:, 128 * tc:128 * (tc + 1)],
             pb.ident[:, :]) for tc in range(8)], [psTk, psTk2])
        esT.close()
        dump("d_gt1", gt1b[:, :])
        dump("d_ind", indb[:, :])
        dump("d_top1", gsc[:, :])
        dump("d_ksh", kshtm[:, :])
        dump("d_vsh", vshtm[:, :])
        dump("d_kshT", kshT[:, :])
        dump("d_qT", qTl[:, 0:2048])
        with nc.Block() as blk:
            @blk.sync
            def _(sy):
                pb.dma(sy, ksh_in.ap(), kshT[:, :], name="st")
                pb.dma(sy, vsh_in.ap().rearrange("(tc p) m -> p tc m", p=128),
                       vshtm[:, :].rearrange("p (tc m) -> p tc m", tc=8), name="st")
                pb.dma(sy, knm_in.ap(), knm[0:1, 0:2], name="st")
                pb.dma(sy, qa_in.ap().rearrange("(ct p) t -> p ct t", p=128),
                       qTl[:, :].rearrange("p (ct t) -> p ct t", ct=NE), name="st")
                pb.wait(sy, "st")
        with nc.Block() as blk:
            @blk.gpsimd
            def _(g):
                ccs = pb.sem("cc")
                for src, dst in ((ksh_in, KSHg), (vsh_in, Vg), (knm_in, KNg)):
                    g.collective_compute(
                        "AllGather", ALU.bypass, replica_groups=[list(range(NCORES))],
                        ins=[src.ap().opt()], outs=[dst.ap().opt()]).then_inc(ccs)
                    pb.cnt["cc"] += 1
                with allow_ici_all_to_all():
                    g.collective_compute(
                        "AllToAll", ALU.bypass, replica_groups=[list(range(NCORES))],
                        ins=[qa_in.ap().opt()], outs=[QA.ap().opt()]).then_inc(ccs)
                    pb.cnt["cc"] += 1
                g.wait_ge(ccs, pb.cnt["cc"])
        es1.close()
        nc.leave_named_scope("l1fproj", sc_p2[0], False)

        with nc.Block() as blk:
            @blk.sync
            def _(sy):
                pb.dma(sy, knm[0:1, 0:16].rearrange("o (r h) -> o r h", r=8),
                       KNg.ap().rearrange("(o r) h -> o r h", o=1))
                pb.wait(sy, "ld")
        with nc.Block() as blk:
            @blk.vector
            def _(v):
                v.reduce_max(knm[0:1, 16:17], knm[0:1, 0:16], AX.X)
        dump("d_knm", knm[0:1, 0:18])

        sc_a2 = nc.enter_named_scope("l1fattn", False)
        es2 = ExitStack()
        kshbuf = pb.sbuf("u_kshb", [128, 16 * 512], BF16, es=es2)
        vshbuf = pb.sbuf("u_vshb", [128, 64 * 128], BF16, es=es2)
        qbuf = pb.sbuf("u_qbuf", [128, 512], BF16, es=es2)
        sqv = pb.sbuf("u_sqv", [128, 512], F32, es=es2)
        brow = pb.sbuf("u_brow", [1, 8], F32, es=es2)
        nbcol = pb.sbuf("u_nbcol", [128, 1], F32, es=es2)
        ptile = [pb.sbuf(f"u_pt{i}", [128, 512], BF16, es=es2) for i in range(2)]
        otbuf = pb.sbuf("u_otb", [128, 512], BF16, es=es2)
        ybuf = pb.sbuf("u_ybuf", [128, 512], BF16, es=es2)
        lrow = pb.sbuf("u_lrow", [1, 512], F32, es=es2)
        linv = pb.sbuf("u_linv", [128, 4], F32, es=es2)
        psS = [pb.psum(f"u_psS{i}", [128, 512], F32, es=es2) for i in range(2)]
        psOT = pb.psum("u_psOT", [128, 512], F32, es=es2)
        psL = pb.psum("u_psL", [1, 512], F32, es=es2)
        psB = pb.psum("u_psB", [128, 512], F32, es=es2)
        psTy = pb.psum("u_psTy", [128, 512], BF16, es=es2)
        for qb in range(16):
            nkc = 4 * (qb + 1)
            with nc.Block() as blk:
                @blk.sync
                def _(sy, qb=qb):
                    for j in range(qb + 1):
                        r = own(j)
                        co = 512 * (0 if j <= 7 else 1)
                        pb.dma(sy, kshbuf[:, 512 * j:512 * (j + 1)],
                               KSHg.ap()[128 * r:128 * (r + 1), co:co + 512])
                        pb.dma(sy, vshbuf[:, 512 * j:512 * (j + 1)]
                               .rearrange("p (kc m) -> p kc m", kc=4),
                               Vg.ap()[T * r + co:T * r + co + 512, :]
                               .rearrange("(kc p) m -> p kc m", p=128))
                    pb.wait(sy, "ld")
            for hl in range(2):
                r = own(qb)
                co = 512 * (0 if qb <= 7 else 1)
                with nc.Block() as blk:
                    @blk.sync
                    def _(sy, qb=qb, hl=hl, r=r, co=co):
                        pb.dma(sy, qbuf[:, :],
                               QA.ap()[256 * r + 128 * hl:256 * r + 128 * (hl + 1), co:co + 512])
                        pb.wait(sy, "ld")
                with nc.Block() as blk:
                    @blk.vector
                    def _(v):
                        v.tensor_mul(sqv[:, :], qbuf[:, :], qbuf[:, :])
                with nc.Block() as blk:
                    @blk.tensor
                    def _(t):
                        t.matmul(psL[:, :], pb.ones_col_f[:, :], sqv[:, :],
                                 start=True, stop=True)
                with nc.Block() as blk:
                    @blk.vector
                    def _(v):
                        pb.guard(v, v.reduce_max(brow[0:1, 0:1], psL[:, :], AX.X))
                        v.tensor_mul(brow[0:1, 1:2], brow[0:1, 0:1], knm[0:1, 16:17])
                with nc.Block() as blk:
                    @blk.scalar
                    def _(s):
                        s.activation(brow[0:1, 2:3], brow[0:1, 1:2], AF.Sqrt)
                with nc.Block() as blk:
                    @blk.vector
                    def _(v):
                        v.tensor_scalar(brow[0:1, 3:4], brow[0:1, 2:3], -SC2, None, ALU.mult)
                with nc.Block() as blk:
                    @blk.tensor
                    def _(t):
                        t.matmul(psB[:, 0:1], pb.ones_row_f[:, :], brow[0:1, 3:4],
                                 start=True, stop=True)
                with nc.Block() as blk:
                    @blk.vector
                    def _(v):
                        v.tensor_copy(nbcol[:, :], psB[:, 0:1])
                sM = pb.sem("l1fM"); bM = pb.cnt["l1fM"]
                sE = pb.sem("l1fE"); bE = pb.cnt["l1fE"]
                sD = pb.sem("l1fD"); bD = pb.cnt["l1fD"]
                sV = pb.sem("l1fV"); bV = pb.cnt["l1fV"]
                with nc.Block() as blk:
                    @blk.tensor
                    def _(t, qb=qb, nkc=nkc, bM=bM, bE=bE, bD=bD, bV=bV):
                        for i in range(nkc + 1):
                            if i < nkc:
                                if i >= 2:
                                    t.wait_ge(sE, bE + i - 1)
                                t.matmul(psS[i % 2][:, :], kshbuf[:, 128 * i:128 * (i + 1)],
                                         qbuf[:, :], start=True, stop=True).then_inc(sM, 1)
                            if i >= 1:
                                k = i - 1
                                t.wait_ge(sE, bE + k + 1)
                                if k >= 4 * qb:
                                    t.wait_ge(sD, bD + (k - 4 * qb) + 1)
                                t.matmul(psL[:, :], pb.ones_col[:, :], ptile[k % 2][:, :],
                                         start=(k == 0), stop=(k == nkc - 1))
                                t.matmul(psOT[:, :], vshbuf[:, 128 * k:128 * (k + 1)],
                                         ptile[k % 2][:, :], start=(k == 0),
                                         stop=(k == nkc - 1)).then_inc(sV, 1)
                    @blk.scalar
                    def _(s, qb=qb, nkc=nkc, bM=bM, bV=bV):
                        for k in range(nkc):
                            s.wait_ge(sM, bM + k + 1)
                            if k >= 2:
                                s.wait_ge(sV, bV + k - 1)
                            s.activation(ptile[k % 2][:, :], psS[k % 2][:, :], AF.Exp,
                                         bias=nbcol[:, 0:1], scale=SC2).then_inc(sE, 1)
                    @blk.vector
                    def _(v, qb=qb, nkc=nkc, bE=bE):
                        for k in range(4 * qb, nkc):
                            v.wait_ge(sE, bE + k + 1)
                            v.tensor_mul(ptile[k % 2][:, :], ptile[k % 2][:, :],
                                         triv[:, 512 * (k % 4):512 * (k % 4 + 1)]
                                         ).then_inc(sD, 1)
                pb.cnt["l1fM"] += nkc
                pb.cnt["l1fE"] += nkc
                pb.cnt["l1fD"] += 4
                pb.cnt["l1fV"] += nkc
                with nc.Block() as blk:
                    @blk.vector
                    def _(v):
                        v.tensor_copy(lrow[0:1, :], psL[:, :])
                        v.tensor_copy(otbuf[:, :], psOT[:, :])
                for qt in range(4):
                    with nc.Block() as blk:
                        @blk.tensor
                        def _(t, qt=qt):
                            t.transpose(psTy[:, 128 * qt:128 * (qt + 1)],
                                        otbuf[:, 128 * qt:128 * (qt + 1)], pb.ident[:, :])
                with nc.Block() as blk:
                    @blk.tensor
                    def _(t):
                        for qt in range(4):
                            t.transpose(psB[:, qt:qt + 1], lrow[0:1, 128 * qt:128 * (qt + 1)],
                                        pb.ident_f[0:1, 0:1])
                with nc.Block() as blk:
                    @blk.vector
                    def _(v):
                        v.reciprocal(linv[:, :], psB[:, 0:4])
                with nc.Block() as blk:
                    @blk.vector
                    def _(v):
                        for qt in range(4):
                            v.tensor_scalar(ybuf[:, 128 * qt:128 * (qt + 1)],
                                            psTy[:, 128 * qt:128 * (qt + 1)],
                                            linv[:, qt:qt + 1], None, ALU.mult)
                with nc.Block() as blk:
                    @blk.sync
                    def _(sy, qb=qb, hl=hl):
                        dd = own(qb)
                        row0 = T * dd + 512 * (0 if qb <= 7 else 1)
                        pb.dma(sy, ya_in.ap()[row0:row0 + 512, 128 * hl:128 * (hl + 1)]
                               .rearrange("(qt p) m -> p qt m", p=128),
                               ybuf[:, :].rearrange("p (qt m) -> p qt m", qt=4), name="st")
                        pb.wait(sy, "st")
        es2.close()
        nc.leave_named_scope("l1fattn", sc_a2[0], False)
        if dbg is not None and "d_yin" in dbg:
            with nc.Block() as blk:
                @blk.sync
                def _(sy):
                    pb.dma(sy, dbg["d_yin"], ya_in.ap(), name="st")
                    pb.wait(sy, "st")
        with nc.Block() as blk:
            @blk.gpsimd
            def _(g):
                ccs = pb.sem("cc")
                with allow_ici_all_to_all():
                    g.collective_compute(
                        "AllToAll", ALU.bypass, replica_groups=[list(range(NCORES))],
                        ins=[ya_in.ap().opt()], outs=[YA.ap().opt()]).then_inc(ccs)
                    pb.cnt["cc"] += 1
                g.wait_ge(ccs, pb.cnt["cc"])

        sc_fn = nc.enter_named_scope("l1ffinal", False)
        es3 = ExitStack()
        ytm = pb.sbuf("u_ytm", [128, EMB], BF16, es=es3)
        x3tm = pb.sbuf("u_x3tm", [128, EMB], BF16, es=es3)
        tfin = pb.sbuf("u_tfin", [128, EMB], F32, es=es3)
        sqf = pb.sbuf("u_sqf", [128, EMB], F32, es=es3)
        stf = pb.sbuf("u_stf", [128, 6], F32, es=es3)
        psTz = pb.psum("u_psTz", [128, 128], BF16, es=es3)
        psTz2 = pb.psum("u_psTz2", [128, 128], BF16, es=es3)
        for tc in range(8):
            with nc.Block() as blk:
                @blk.sync
                def _(sy, tc=tc):
                    pb.dma(sy, ytm[:, :].rearrange("p (s m) -> p s m", s=8),
                           YA.ap()[:, :].rearrange("(s q p) m -> p s q m", p=128, q=8)
                           [:, :, tc, :])
                    pb.wait(sy, "ld")
            transpose_seq(pb, [
                (xbuf[:, ct * T + 128 * tc:ct * T + 128 * (tc + 1)],
                 x3tm[:, 128 * ct:128 * (ct + 1)], pb.ident[:, :])
                for ct in range(NE)], [psTz, psTz2])
            with nc.Block() as blk:
                @blk.vector
                def _(v):
                    v.tensor_add(tfin[:, :], x3tm[:, :], ytm[:, :])
            with nc.Block() as blk:
                @blk.vector
                def _(v):
                    pb.guard(v, v.reduce_sum(stf[:, 0:1], tfin[:, :], AX.X))
                    v.tensor_scalar(stf[:, 0:1], stf[:, 0:1], 1.0 / EMB, None, ALU.mult)
                @blk.scalar
                def _(s):
                    s.activation(sqf[:, :], tfin[:, :], AF.Square)
            with nc.Block() as blk:
                @blk.vector
                def _(v):
                    pb.guard(v, v.reduce_sum(stf[:, 1:2], sqf[:, :], AX.X))
                    v.tensor_scalar(stf[:, 1:2], stf[:, 1:2], 1.0 / EMB, None, ALU.mult)
                    v.tensor_mul(stf[:, 2:3], stf[:, 0:1], stf[:, 0:1])
                    v.tensor_sub(stf[:, 1:2], stf[:, 1:2], stf[:, 2:3])
                    v.tensor_scalar(stf[:, 1:2], stf[:, 1:2], 1e-5, None, ALU.add)
            with nc.Block() as blk:
                @blk.scalar
                def _(s):
                    s.activation(stf[:, 3:4], stf[:, 1:2], AF.Sqrt)
            with nc.Block() as blk:
                @blk.vector
                def _(v):
                    v.reciprocal(stf[:, 4:5], stf[:, 3:4])
            with nc.Block() as blk:
                @blk.vector
                def _(v):
                    v.tensor_scalar(tfin[:, :], tfin[:, :], stf[:, 0:1], stf[:, 4:5],
                                    ALU.subtract, ALU.mult)
            with nc.Block() as blk:
                @blk.sync
                def _(sy, tc=tc):
                    pb.dma(sy, out_ap[128 * tc:128 * (tc + 1), :], tfin[:, :], name="st")
                    pb.wait(sy, "st")
        es3.close()
        nc.leave_named_scope("l1ffinal", sc_fn[0], False)


def build_l1ffn(debug=False):
    nc = bass.Bass("TRN2", target_bir_lowering=False)
    pb = PB(nc)
    x3 = nc.dram_tensor("x3fm", [EMB, 1024], BF16, kind="ExternalInput").ap()
    wqkv = nc.dram_tensor("wqkv2_t", [EMB, 6144], F32, kind="ExternalInput").ap()
    wgate = nc.dram_tensor("wgate2_t", [EMB, 16], F32, kind="ExternalInput").ap()
    out = nc.dram_tensor("outrows", [1024, EMB], F32, kind="ExternalOutput").ap()
    dbg = None
    if debug:
        dbg = {}
        for nm_, shp, dt_ in (("d_glog", [128, 128], F32), ("d_ksh", [128, 1024], BF16),
                              ("d_vsh", [128, 1024], BF16), ("d_kshT", [128, 1024], BF16),
                              ("d_qT", [128, 2048], BF16), ("d_knm", [1, 18], F32),
                              ("d_yin", [8 * 1024, 256], BF16),
                              ("d_vfull", [128, 2048], BF16), ("d_kfull", [128, 2048], BF16),
                              ("d_gt1", [128, 16], F32), ("d_ind", [128, 16], F32),
                              ("d_top1", [128, 4], F32)):
            dbg[nm_] = nc.dram_tensor(nm_, shp, dt_, kind="ExternalOutput").ap()
    with pb.es:
        pb.consts()
        _l1ffn_body(pb, x3, wqkv, wgate, out, dbg=dbg)
    return nc


def build_fused(dbg_mid=False, nphases=4):
    nc = bass.Bass("TRN2", target_bir_lowering=False)
    pb = PB(nc)
    ap = lambda *a, **k: nc.dram_tensor(*a, kind="ExternalInput", **k).ap()
    x_lo = ap("x_lo", [515, EMB], F32)
    x_hi = ap("x_hi", [515, EMB], F32)
    wproj = ap("w_inproj_t", [EMB, PROJ], F32)
    cw = ap("conv_w", [128, 192], F32)
    cb = ap("conv_b", [128, 48], F32)
    dtb = ap("dt_bias", [64, 1], F32)
    alog = ap("a_log", [64, 1], F32)
    dpair = ap("d_pair", [128, 32], F32)
    nw = ap("norm_w", [128, 32], F32)
    wout = ap("w_outproj_t", [INTER, EMB], F32)
    lut = ap("lut", [1, 16], F32)
    gidx = ap("gidx_sl", [FFN_I, 256], I32)
    uidx = ap("uidx_sl", [FFN_I, 256], I32)
    didx = ap("didx_sl", [EMB, 688], I32)
    qkv = ap("qkv1", [EMB, 6144], F32)
    oproj = ap("oproj", [EMB, EMB], F32)
    gam = ap("gamma", [128, 16], F32)
    bet = ap("beta", [128, 16], F32)
    allow = ap("allow", [96, 128, 512], BF16)
    wqkv = ap("wqkv2_t", [EMB, 6144], F32)
    wgate = ap("wgate2_t", [EMB, 16], F32)
    out = nc.dram_tensor("outrows", [1024, EMB], F32, kind="ExternalOutput").ap()
    kind = "ExternalOutput" if dbg_mid else "Internal"
    x1 = nc.dram_tensor("x1fm", [EMB, 1024], BF16, kind=kind).ap()
    if dbg_mid:
        nc._dbg_wgt = True
    x2 = nc.dram_tensor("x2fm", [EMB, 1024], BF16, kind=kind).ap()
    x3 = nc.dram_tensor("x3fm", [EMB, 1024], BF16, kind=kind).ap()
    with pb.es:
        pb.consts()
        _l0attn_body(pb, (x_lo, x_hi), wproj, cw, cb, dtb, alog, dpair, nw, wout, x1)
        if nphases >= 2:
            _l0ffn_body(pb, x1, lut, gidx, uidx, didx, x2)
        if nphases >= 3:
            _l1attn_body(pb, None, x2, qkv, oproj, gam, bet, allow, x3)
        if nphases >= 4:
            _l1ffn_body(pb, x3, wqkv, wgate, out)
    return nc


def _prep_host(ins):
    x0 = np.ascontiguousarray(ins["L0_attn_input_states"][0])
    A = {
        "w_inproj_t": np.ascontiguousarray(ins["L0_attn_in_proj_w"].T),
        "conv_w": np.ascontiguousarray(
            ins["L0_attn_conv_w"][:, 0, :].reshape(48, 128, 4).transpose(1, 0, 2)
            .reshape(128, 192)),
        "conv_b": np.ascontiguousarray(ins["L0_attn_conv_b"].reshape(48, 128).T),
        "dt_bias": ins["L0_attn_dt_bias"].reshape(64, 1).copy(),
        "a_log": ins["L0_attn_A_log"].reshape(64, 1).copy(),
        "d_pair": np.ascontiguousarray(
            ins["L0_attn_D"].reshape(32, 2).repeat(64, axis=1).reshape(32, 128).T),
        "norm_w": np.ascontiguousarray(ins["L0_attn_norm_w"].reshape(32, 128).T),
        "w_outproj_t": np.ascontiguousarray(ins["L0_attn_out_proj_w"].T),
    }
    B = {"lut": np.ascontiguousarray(ins["L0_ffn_lut"].reshape(1, 16))}
    C = {
        "qkv1": np.ascontiguousarray(ins["L1_attn_qkv"]),
        "oproj": np.ascontiguousarray(ins["L1_attn_o_proj"]),
        "gamma": np.ascontiguousarray(ins["L1_attn_gamma"].reshape(16, 128).T),
        "beta": np.ascontiguousarray(ins["L1_attn_beta"].reshape(16, 128).T),
    }
    D = {
        "wqkv2_t": np.ascontiguousarray(ins["L1_ffn_W_qkv"].T),
        "wgate2_t": np.ascontiguousarray(ins["L1_ffn_W_gate"].T),
    }
    mapsA, mapsB, mapsC, mapsD = [], [], [], []
    for c in range(NCORES):
        mA = dict(A)
        for key, b in zip(("x_lo", "x_hi"), blocks_of(c)):
            xe = np.zeros((515, EMB), np.float32)
            xe[3:] = x0[512 * b:512 * (b + 1)]
            if b > 0:
                xe[:3] = x0[512 * b - 3:512 * b]
            mA[key] = xe
        mapsA.append(mA)
        mB = dict(B)
        mB["gidx_sl"] = np.ascontiguousarray(ins["L0_ffn_gate_idx"][:, 256 * c:256 * (c + 1)])
        mB["uidx_sl"] = np.ascontiguousarray(ins["L0_ffn_up_idx"][:, 256 * c:256 * (c + 1)])
        mB["didx_sl"] = np.ascontiguousarray(ins["L0_ffn_down_idx"][:, 688 * c:688 * (c + 1)])
        mapsB.append(mB)
        mC = dict(C)
        allow = np.zeros((96, 128, 512), np.float32)
        i = 0
        for half, b in enumerate(blocks_of(c)):
            nslots = 8 if half == 0 else 16
            for qt in range(4):
                qglob = 512 * b + 128 * qt + np.arange(128)
                for j in range(nslots):
                    if j < b:
                        allow[i] = 1.0
                    elif j == b:
                        kglob = 512 * j + np.arange(512)
                        allow[i] = (kglob[None, :] <= qglob[:, None])
                    i += 1
        mC["allow"] = allow.astype(BF)
        mapsC.append(mC)
        mapsD.append(dict(D))
    return mapsA, mapsB, mapsC, mapsD


def kernel(**inputs):
    mapsA, mapsB, mapsC, mapsD = _prep_host(inputs)
    cores = list(range(NCORES))
    maps = [{**mapsA[c], **mapsB[c], **mapsC[c], **mapsD[c]} for c in range(NCORES)]
    res = run_bass_kernel_spmd(build_fused(), maps, core_ids=cores)
    out = np.zeros((1, SEQ, EMB), np.float32)
    for c in range(NCORES):
        rows = res.results[c]["outrows"]
        b0, b1 = blocks_of(c)
        out[0, 512 * b0:512 * (b0 + 1)] = rows[0:512]
        out[0, 512 * b1:512 * (b1 + 1)] = rows[512:1024]
    return out


# version 17
# speedup vs baseline: 1.0621x; 1.0621x over previous
import math
from contextlib import ExitStack

import numpy as np
import ml_dtypes

import concourse.bass as bass
import concourse.mybir as mybir
from concourse.bass_utils import run_bass_kernel_spmd

F32 = mybir.dt.float32
BF16 = mybir.dt.bfloat16
I32 = mybir.dt.int32
AF = mybir.ActivationFunctionType
ALU = mybir.AluOpType
AX = mybir.AxisListType
BF = ml_dtypes.bfloat16

NCORES = 8
SEQ = 8192
EMB = 2048
NE = EMB // 128
H, PDIM, NST, GRP, KCONV, CS = 64, 64, 128, 8, 4, 256
INTER = H * PDIM
CONV_DIM = INTER + 2 * GRP * NST
PROJ = INTER + CONV_DIM + H
FFN_I = 5504
NFI = 43

def blocks_of(c):
    return (c, 15 - c)

def own(j):
    return j if j <= 7 else 15 - j


class PB:
    def __init__(self, nc):
        self.nc = nc
        self.es = ExitStack()
        self.sems = {}
        self.cnt = {}

    def sem(self, name):
        if name not in self.sems:
            self.sems[name] = self.es.enter_context(self.nc.semaphore(name))
            self.cnt[name] = 0
        return self.sems[name]

    def dma(self, eng, out, in_, name="ld", **kw):
        h = self.sem(name)
        eng.dma_start(out=out, in_=in_, **kw).then_inc(h, 16)
        self.cnt[name] += 16

    def wait(self, eng, name):
        eng.wait_ge(self.sem(name), self.cnt[name])

    def guard(self, eng, instr):
        h = self.sem("gdve")
        instr.then_inc(h, 1)
        self.cnt["gdve"] += 1
        eng.wait_ge(h, self.cnt["gdve"])

    def sbuf(self, name, shape, dtype, es=None):
        return (es or self.es).enter_context(self.nc.sbuf_tensor(name, shape, dtype))

    def psum(self, name, shape, dtype=F32, es=None):
        return (es or self.es).enter_context(self.nc.psum_tensor(name, shape, dtype))

    def consts(self):
        nc = self.nc
        self.ident = self.sbuf("c_ident", [128, 128], BF16)
        self.ident_f = self.sbuf("c_ident_f", [128, 128], F32)
        self.ones_row = self.sbuf("c_ones_row", [1, 128], BF16)
        self.ones_col = self.sbuf("c_ones_col", [128, 1], BF16)
        self.ones_row_f = self.sbuf("c_ones_row_f", [1, 128], F32)
        self.ones_col_f = self.sbuf("c_ones_col_f", [128, 1], F32)
        self.iota_row = self.sbuf("c_iota_row", [128, 512], F32)
        self.iota_col = self.sbuf("c_iota_col", [128, 1], F32)
        with nc.Block() as blk:
            @blk.gpsimd
            def _(g):
                g.iota(self.iota_row[:, :], [[1, 512]], channel_multiplier=0,
                       allow_small_or_imprecise_dtypes=True)
                g.iota(self.iota_col[:, :], [[0, 1]], channel_multiplier=1,
                       allow_small_or_imprecise_dtypes=True)
        with nc.Block() as blk:
            @blk.vector
            def _(v):
                v.tensor_scalar(self.ident[:, :], self.iota_row[:, 0:128],
                                self.iota_col[:, 0:1], None, ALU.is_equal)
                v.tensor_scalar(self.ident_f[:, :], self.iota_row[:, 0:128],
                                self.iota_col[:, 0:1], None, ALU.is_equal)
                v.memset(self.ones_row[:, :], 1.0)
                v.memset(self.ones_col[:, :], 1.0)
                v.memset(self.ones_row_f[:, :], 1.0)
                v.memset(self.ones_col_f[:, :], 1.0)




def pipe_mm(pb, tag, ng, dma_emit, mm_emit, evac_emit, double_psum=True,
            act_emit=None, dma_eng="gpsimd"):
    nc = pb.nc
    sMM = pb.sem(tag + "m")
    sEv = pb.sem(tag + "e")
    sDn = pb.sem(tag + "d")
    bM = pb.cnt[tag + "m"]
    bE = pb.cnt[tag + "e"]
    dma_counts = []
    with nc.Block() as blk:
        def _dma_body(sy):
            for g in range(ng):
                if g >= 2:
                    sy.wait_ge(sMM, bM + g - 1)
                dma_emit(sy, g)
                dma_counts.append(pb.cnt[tag + "d"])
        if dma_eng == "gpsimd":
            blk.gpsimd(_dma_body)
        else:
            blk.sync(_dma_body)
        @blk.tensor
        def _(t):
            for g in range(ng):
                t.wait_ge(sDn, dma_counts[g])
                lag = 2 if double_psum else 1
                if g >= lag:
                    t.wait_ge(sEv, bE + g - lag + 1)
                mm_emit(t, g).then_inc(sMM, 1)
        @blk.vector
        def _(v):
            for g in range(ng):
                v.wait_ge(sMM, bM + g + 1)
                evac_emit(v, g).then_inc(sEv, 1)
        if act_emit is not None:
            @blk.scalar
            def _(sc):
                for g in range(ng):
                    sc.wait_ge(sEv, bE + g + 1)
                    act_emit(sc, g)
    pb.cnt[tag + "m"] += ng
    pb.cnt[tag + "e"] += ng


def transpose_seq(pb, items, ps2):
    nc = pb.nc
    sT = pb.sem("trT")
    sE = pb.sem("trE")
    bT = pb.cnt["trT"]
    bE = pb.cnt["trE"]
    n = len(items)
    with nc.Block() as blk:
        @blk.tensor
        def _(t):
            for i, (src, dst, idn) in enumerate(items):
                if i >= 2:
                    t.wait_ge(sE, bE + i - 1)
                pp, ff = src.shape
                t.transpose(ps2[i % 2][0:ff, 0:pp], src, idn).then_inc(sT, 1)
        @blk.vector
        def _(v):
            for i, (src, dst, idn) in enumerate(items):
                v.wait_ge(sT, bT + i + 1)
                pp, ff = src.shape
                v.tensor_copy(dst, ps2[i % 2][0:ff, 0:pp]).then_inc(sE, 1)
    pb.cnt["trT"] += n
    pb.cnt["trE"] += n


def ln_fm(pb, get_tile, nt, T, out_tile, gamma=None, beta=None, eps=1e-5,
          pre_scale=1.0, tag=""):
    nc = pb.nc
    nth = T // 512
    es = ExitStack()
    with es:
        sum_ps = [pb.psum(f"ln{tag}_s{i}", [1, 512], F32, es=es) for i in range(nth)]
        sq_ps = [pb.psum(f"ln{tag}_q{i}", [1, 512], F32, es=es) for i in range(nth)]
        sq = pb.sbuf(f"ln{tag}_sq", [128, T], F32, es=es)
        sqb = [pb.sbuf(f"ln{tag}_sqb{i}", [128, T], BF16, es=es) for i in range(2)]
        ab = pb.sbuf(f"ln{tag}_ab", [1, 2 * T], F32, es=es)
        abc = pb.sbuf(f"ln{tag}_abc", [128, 2 * T], F32, es=es)
        pbc = [pb.psum(f"ln{tag}_pbc{i}", [128, 512], F32, es=es) for i in range(2)]
        sQ = pb.sem("lnQ")
        sC = pb.sem("lnC")
        bQ = pb.cnt["lnQ"]
        bC = pb.cnt["lnC"]
        with nc.Block() as blk:
            @blk.scalar
            def _(s):
                for ct in range(NE):
                    if ct >= 2:
                        s.wait_ge(sC, bC + ct - 1)
                    s.activation(sqb[ct % 2][:, :], get_tile(ct), AF.Square
                                 ).then_inc(sQ, 1)
            @blk.tensor
            def _(t):
                for ct in range(NE):
                    for th in range(nth):
                        t.matmul(sum_ps[th][:, :], pb.ones_col[:, :],
                                 get_tile(ct)[:, 512 * th:512 * (th + 1)],
                                 start=(ct == 0), stop=(ct == NE - 1))
                    t.wait_ge(sQ, bQ + ct + 1)
                    for th in range(nth):
                        mm = t.matmul(sq_ps[th][:, :], pb.ones_col[:, :],
                                      sqb[ct % 2][:, 512 * th:512 * (th + 1)],
                                      start=(ct == 0), stop=(ct == NE - 1))
                        if th == nth - 1:
                            mm.then_inc(sC, 1)
        pb.cnt["lnQ"] += NE
        pb.cnt["lnC"] += NE
        mean = ab[0:1, 0:T]
        avar = ab[0:1, T:2 * T]
        sG = pb.sem("lnG")
        bG = pb.cnt["lnG"]
        bQ2 = pb.cnt["lnQ"]
        bC2 = pb.cnt["lnC"]
        nbc = 2 * nth
        with nc.Block() as blk:
            @blk.vector
            def _(v):
                for th in range(nth):
                    v.tensor_scalar(ab[0:1, 512 * th:512 * (th + 1)], sum_ps[th][:, :],
                                    1.0 / EMB, None, ALU.mult)
                    v.tensor_scalar(ab[0:1, T + 512 * th:T + 512 * (th + 1)],
                                    sq_ps[th][:, :], 1.0 / EMB, None, ALU.mult)
                v.tensor_mul(sq[0:1, 0:T], mean, mean)
                v.tensor_sub(avar, avar, sq[0:1, 0:T])
                v.tensor_scalar(avar, avar, float(eps), None, ALU.add
                                ).then_inc(sG, 1)
                v.wait_ge(sG, bG + 2)
                pb.guard(v, v.reciprocal(avar, avar))
                v.tensor_mul(mean, mean, avar)
                pb.guard(v, v.tensor_scalar(mean, mean, -1.0, None, ALU.mult))
                v.engine_nop().then_inc(sG, 1)
                for i in range(nbc):
                    v.wait_ge(sQ, bQ2 + i + 1)
                    half, th = divmod(i, nth)
                    v.tensor_copy(abc[:, half * T + 512 * th:half * T + 512 * (th + 1)],
                                  pbc[i % 2][:, :]).then_inc(sC, 1)
            @blk.scalar
            def _(s):
                s.wait_ge(sG, bG + 1)
                s.activation(avar, avar, AF.Sqrt).then_inc(sG, 1)
            @blk.tensor
            def _(t):
                t.wait_ge(sG, bG + 3)
                for i in range(nbc):
                    half, th = divmod(i, nth)
                    if i >= 2:
                        t.wait_ge(sC, bC2 + i - 1)
                    src = (1 - half) * T
                    t.matmul(pbc[i % 2][:, :], pb.ones_row_f[:, :],
                             ab[0:1, src + 512 * th:src + 512 * (th + 1)],
                             start=True, stop=True).then_inc(sQ, 1)
        pb.cnt["lnG"] = bG + 3
        pb.cnt["lnQ"] += nbc
        pb.cnt["lnC"] += nbc
        with nc.Block() as blk:
            @blk.vector
            def _(v):
                for ct in range(NE):
                    v.tensor_copy(sq[:, :], get_tile(ct))
                    v.tensor_mul(sq[:, :], sq[:, :], abc[:, 0:T])
                    v.tensor_add(sq[:, :], sq[:, :], abc[:, T:2 * T])
                    if gamma is not None:
                        v.tensor_scalar(sq[:, :], sq[:, :], gamma[:, ct:ct + 1],
                                        None, ALU.mult)
                    if beta is not None:
                        v.tensor_scalar(sq[:, :], sq[:, :], beta[:, ct:ct + 1],
                                        None, ALU.add)
                    v.tensor_copy(out_tile(ct), sq[:, :])


def _l0ffn_body(pb, x1_ap, lut_ap, gidx_ap, uidx_ap, didx_ap, x2_ap):
    nc = pb.nc
    T = 1024
    wg_sl = nc.dram_tensor("wg_sl", [256, FFN_I], BF16)
    wu_sl = nc.dram_tensor("wu_sl", [256, FFN_I], BF16)
    wd_sl = nc.dram_tensor("wd_sl", [688, EMB], BF16)
    WgT = nc.dram_tensor("WgT", [EMB, FFN_I], BF16, addr_space="Shared")
    WuT = nc.dram_tensor("WuT", [EMB, FFN_I], BF16, addr_space="Shared")
    WdT = nc.dram_tensor("WdT", [FFN_I, EMB], BF16, addr_space="Shared")
    if getattr(nc, "_dbg_wgt", False):
        dWg = nc.dram_tensor("dWgT", [256, FFN_I], BF16, kind="ExternalOutput")
        dWu = nc.dram_tensor("dWuT", [256, FFN_I], BF16, kind="ExternalOutput")
        dWd = nc.dram_tensor("dWdT", [688, EMB], BF16, kind="ExternalOutput")

    sc_gw = nc.enter_named_scope("ghostw", False)
    es2 = ExitStack()
    with es2:
        lutc = pb.sbuf("lutc", [128, 16], F32, es=es2)
        lutrow = pb.sbuf("lutrow", [1, 16], F32, es=es2)
        p_bc16 = pb.psum("p_bc16", [128, 16], F32, es=es2)
        with nc.Block() as blk:
            @blk.sync
            def _(sy):
                pb.dma(sy, lutrow[:, :], lut_ap)
                pb.wait(sy, "ld")
        with nc.Block() as blk:
            @blk.tensor
            def _(t):
                t.matmul(p_bc16[:, :], pb.ones_row_f[:, :], lutrow[:, :], start=True, stop=True)
        with nc.Block() as blk:
            @blk.vector
            def _(v):
                v.tensor_copy(lutc[:, :], p_bc16[:, :])

        idxs = pb.sbuf("g_idxs", [128, 768], I32, es=es2)
        idxb = pb.sbuf("g_idxb", [128, 768], BF16, es=es2)
        acc = pb.sbuf("g_acc", [128, 768], BF16, es=es2)
        tmp = pb.sbuf("g_tmp", [128, 768], BF16, es=es2)
        p_tr = pb.psum("g_ptr", [128, 128], BF16, es=es2)
        stA = pb.sbuf("g_stA", [128, FFN_I], BF16, es=es2)
        stB = pb.sbuf("g_stB", [128, FFN_I], BF16, es=es2)
        stD = [pb.sbuf(f"g_stD{j}", [128 if j < 5 else 48, EMB], BF16, es=es2) for j in range(6)]

        def convert_chunk(width, idx_src, out_tiles):
            with nc.Block() as blk:
                @blk.sync
                def _(sy):
                    pb.dma(sy, idxs[:, 0:width], idx_src)
                @blk.vector
                def _(v):
                    pb.wait(v, "ld")
                    v.tensor_copy(idxb[:, 0:width], idxs[:, 0:width])
                    v.memset(acc[:, 0:width], 0.0)
                    for k in range(16):
                        v.tensor_scalar(tmp[:, 0:width], idxb[:, 0:width],
                                        float(k), lutc[:, k:k + 1], ALU.is_equal, ALU.mult)
                        v.tensor_add(acc[:, 0:width], acc[:, 0:width], tmp[:, 0:width])
            ncol = (width + 127) // 128
            for j in range(ncol):
                w = min(128, width - 128 * j)
                with nc.Block() as blk:
                    @blk.tensor
                    def _(t, j=j, w=w):
                        t.transpose(p_tr[0:w, 0:128], acc[:, 128 * j:128 * j + w],
                                    pb.ident[:, :])
                with nc.Block() as blk:
                    @blk.vector
                    def _(v, j=j, w=w):
                        v.tensor_copy(out_tiles[j][0:w, :], p_tr[0:w, 0:128])

        for idx_ap, dst in ((gidx_ap, wg_sl), (uidx_ap, wu_sl)):
            for i in range(NFI):
                convert_chunk(256, idx_ap[128 * i:128 * (i + 1), :],
                              [stA[:, 128 * i:128 * (i + 1)],
                               stB[:, 128 * i:128 * (i + 1)]])
            with nc.Block() as blk:
                @blk.sync
                def _(sy, dst=dst):
                    pb.dma(sy, dst.ap()[0:128, :], stA[:, :])
                    pb.dma(sy, dst.ap()[128:256, :], stB[:, :])
                    pb.wait(sy, "ld")
        for i in range(NE):
            convert_chunk(688, didx_ap[128 * i:128 * (i + 1), :],
                          [stD[j][:, 128 * i:128 * (i + 1)] for j in range(6)])
        with nc.Block() as blk:
            @blk.sync
            def _(sy):
                for j in range(6):
                    hh = 128 if j < 5 else 48
                    pb.dma(sy, wd_sl.ap()[128 * j:128 * j + hh, :], stD[j][0:hh, :])
                pb.wait(sy, "ld")
        with nc.Block() as blk:
            @blk.gpsimd
            def _(g):
                ccs = pb.sem("cc")
                for src, dsts in ((wg_sl, WgT), (wu_sl, WuT), (wd_sl, WdT)):
                    g.collective_compute(
                        "AllGather", ALU.bypass, replica_groups=[list(range(NCORES))],
                        ins=[src.ap().opt()], outs=[dsts.ap().opt()]).then_inc(ccs)
                    pb.cnt["cc"] += 1
                g.wait_ge(ccs, pb.cnt["cc"])

    nc.leave_named_scope("ghostw", sc_gw[0], False)
    sc_fm = nc.enter_named_scope("ffnmm", False)
    es3 = ExitStack()
    with es3:
        xfm = pb.sbuf("f_xfm", [128, NE * T], BF16, es=es3)
        sgt = pb.sbuf("f_sg", [128, 4 * 512], BF16, es=es3)
        es_h = ExitStack()
        es_w = ExitStack()
        hbuf = pb.sbuf("f_h", [128, NFI * 1024], BF16, es=es_h)
        wband_g = pb.sbuf("f_wg", [128, NE * 256], BF16, es=es_w)
        wband_u = pb.sbuf("f_wu", [128, NE * 256], BF16, es=es_w)
        with nc.Block() as blk:
            @blk.sync
            def _(sy):
                pb.dma(sy, xfm[:, :].rearrange("p (ct t) -> p ct t", ct=NE),
                       x1_ap.rearrange("(ct p) t -> p ct t", p=128))
                pb.wait(sy, "ld")
        es_gu = ExitStack()
        pg = [pb.psum(f"f_pg{i}", [128, 512], F32, es=es_gu) for i in range(4)]
        pu = [pb.psum(f"f_pu{i}", [128, 512], F32, es=es_gu) for i in range(4)]
        def gu_dma(sy, g):
            off = (g % 2) * NE * 128
            for q4 in range(4):
                pb.dma(sy, wband_g[:, off + q4 * 512:off + (q4 + 1) * 512]
                       .rearrange("p (kt m) -> p kt m", kt=4),
                       WgT.ap()[512 * q4:512 * (q4 + 1), 128 * g:128 * (g + 1)]
                       .rearrange("(kt p) m -> p kt m", p=128), name="gud")
                pb.dma(sy, wband_u[:, off + q4 * 512:off + (q4 + 1) * 512]
                       .rearrange("p (kt m) -> p kt m", kt=4),
                       WuT.ap()[512 * q4:512 * (q4 + 1), 128 * g:128 * (g + 1)]
                       .rearrange("(kt p) m -> p kt m", p=128), name="gud")

        def gu_mm(t, g):
            off = (g % 2) * NE * 128
            pgs = pg[2 * (g % 2):2 * (g % 2) + 2]
            pus = pu[2 * (g % 2):2 * (g % 2) + 2]
            for th in range(2):
                for kt in range(NE):
                    xsl = xfm[:, kt * T + 512 * th:kt * T + 512 * (th + 1)]
                    t.matmul(pgs[th][:, :], wband_g[:, off + kt * 128:off + (kt + 1) * 128],
                             xsl, start=(kt == 0), stop=(kt == NE - 1))
                    mm = t.matmul(pus[th][:, :], wband_u[:, off + kt * 128:off + (kt + 1) * 128],
                                  xsl, start=(kt == 0), stop=(kt == NE - 1))
            return mm

        def gu_evac(v, g):
            pus = pu[2 * (g % 2):2 * (g % 2) + 2]
            for th in range(2):
                cp = v.tensor_mul(
                    hbuf[:, g * T + 512 * th:g * T + 512 * (th + 1)],
                    sgt[:, (g % 2) * 1024 + 512 * th:(g % 2) * 1024 + 512 * (th + 1)],
                    pus[th][:, :])
            return cp

        def gu_act(sc, g):
            pgs = pg[2 * (g % 2):2 * (g % 2) + 2]
            for th in range(2):
                sc.activation(sgt[:, (g % 2) * 1024 + 512 * th:(g % 2) * 1024 + 512 * (th + 1)],
                              pgs[th][:, :], AF.Silu)
        sGU = pb.sem("gus")
        bGU = pb.cnt["gus"]
        sGM = pb.sem("gum")
        bGM = pb.cnt["gum"]
        sGE = pb.sem("gue")
        bGE = pb.cnt["gue"]
        dcnt = []
        with nc.Block() as blk:
            @blk.sync
            def _(sy):
                for g in range(NFI):
                    if g >= 2:
                        sy.wait_ge(sGM, bGM + g - 1)
                    gu_dma(sy, g)
                    dcnt.append(pb.cnt["gud"])
            @blk.tensor
            def _(t):
                for g in range(NFI):
                    t.wait_ge(pb.sem("gud"), dcnt[g])
                    if g >= 2:
                        t.wait_ge(sGE, bGE + g - 1)
                    gu_mm(t, g).then_inc(sGM, 1)
            @blk.scalar
            def _(sc):
                for g in range(NFI):
                    sc.wait_ge(sGM, bGM + g + 1)
                    if g >= 2:
                        sc.wait_ge(sGE, bGE + g - 1)
                    for th in range(2):
                        aa = sc.activation(
                            sgt[:, (g % 2) * 1024 + 512 * th:(g % 2) * 1024 + 512 * (th + 1)],
                            pg[2 * (g % 2) + th][:, :], AF.Silu)
                    aa.then_inc(sGU, 1)
            @blk.vector
            def _(v):
                for g in range(NFI):
                    v.wait_ge(sGU, bGU + g + 1)
                    gu_evac(v, g).then_inc(sGE, 1)
        pb.cnt["gus"] += NFI
        pb.cnt["gum"] += NFI
        pb.cnt["gue"] += NFI
        es_gu.close()
        es_w.close()
        es_d = ExitStack()
        wband_d = pb.sbuf("f_wd", [128, 2 * NFI * 256], BF16, es=es_d)
        pd = [pb.psum(f"f_pd{i}", [128, 512], F32, es=es_d) for i in range(8)]
        tfm = xfm
        def dn_dma(sy, g):
            off = (g % 2) * NFI * 256
            for q6 in range(6):
                r0 = 1024 * q6
                rr = min(1024, FFN_I - r0)
                pb.dma(sy, wband_d[:, off + (r0 // 128) * 256:off + ((r0 + rr) // 128) * 256]
                       .rearrange("p (kt m) -> p kt m", m=256),
                       WdT.ap()[r0:r0 + rr, 256 * g:256 * (g + 1)]
                       .rearrange("(kt p) m -> p kt m", p=128), name="dnd")

        def dn_mm(t, g):
            off = (g % 2) * NFI * 256
            ps = pd[4 * (g % 2):4 * (g % 2) + 4]
            for mi in range(2):
                for th in range(2):
                    for kt in range(NFI):
                        wo = off + kt * 256 + mi * 128
                        mm = t.matmul(ps[2 * mi + th][:, :], wband_d[:, wo:wo + 128],
                                      hbuf[:, kt * T + 512 * th:kt * T + 512 * (th + 1)],
                                      start=(kt == 0), stop=(kt == NFI - 1))
            return mm

        def dn_evac(v, g):
            ps = pd[4 * (g % 2):4 * (g % 2) + 4]
            for mi in range(2):
                for th in range(2):
                    sl = slice((2 * g + mi) * T + 512 * th, (2 * g + mi) * T + 512 * (th + 1))
                    cp = v.tensor_add(tfm[:, sl], tfm[:, sl], ps[2 * mi + th][:, :])
            return cp
        pipe_mm(pb, "dn", 8, dn_dma, dn_mm, dn_evac, dma_eng="sync")
        es_d.close()
        es_h.close()
        nc.leave_named_scope("ffnmm", sc_fm[0], False)
        ln_fm(pb, lambda ct: tfm[:, ct * T:(ct + 1) * T], NE, T,
              lambda ct: tfm[:, ct * T:(ct + 1) * T], tag="f")
        with nc.Block() as blk:
            @blk.sync
            def _(sy):
                pb.dma(sy, x2_ap.rearrange("(ct p) t -> p ct t", p=128),
                       tfm[:, :].rearrange("p (ct t) -> p ct t", ct=NE), name="st")
                pb.wait(sy, "st")


def build_l0ffn():
    nc = bass.Bass("TRN2", target_bir_lowering=False)
    pb = PB(nc)
    x1 = nc.dram_tensor("x1fm", [EMB, 1024], BF16, kind="ExternalInput").ap()
    lut = nc.dram_tensor("lut", [1, 16], F32, kind="ExternalInput").ap()
    gidx = nc.dram_tensor("gidx_sl", [FFN_I, 256], I32, kind="ExternalInput").ap()
    uidx = nc.dram_tensor("uidx_sl", [FFN_I, 256], I32, kind="ExternalInput").ap()
    didx = nc.dram_tensor("didx_sl", [EMB, 688], I32, kind="ExternalInput").ap()
    x2 = nc.dram_tensor("x2fm", [EMB, 1024], BF16, kind="ExternalOutput").ap()
    with pb.es:
        pb.consts()
        _l0ffn_body(pb, x1, lut, gidx, uidx, didx, x2)
    return nc



def _l0attn_body(pb, x_aps, wproj_ap, cw_ap, cb_ap, dtb_ap, alog_ap, dpair_ap,
                 nw_ap, wout_ap, x1_ap, dbg=None):
    nc = pb.nc
    esP = ExitStack()

    def dump(name, src_ap):
        if dbg is None or name not in dbg:
            return
        with nc.Block() as blk:
            @blk.sync
            def _(sy):
                pb.dma(sy, dbg[name], src_ap, name="st")
                pb.wait(sy, "st")
    with esP:
        cw = pb.sbuf("a_cw", [128, 192], F32, es=esP)
        cb = pb.sbuf("a_cb", [128, 48], F32, es=esP)
        dtb = pb.sbuf("a_dtb", [64, 1], F32, es=esP)
        alog = pb.sbuf("a_alog", [64, 1], F32, es=esP)
        acol = pb.sbuf("a_acol", [64, 1], F32, es=esP)
        dpair = pb.sbuf("a_dpair", [128, 32], F32, es=esP)
        nw = pb.sbuf("a_nw", [128, 32], F32, es=esP)
        tri_w = pb.sbuf("a_tri", [128, 512], BF16, es=esP)
        ones64 = pb.sbuf("a_ones64", [64, 256], F32, es=esP)
        with nc.Block() as blk:
            @blk.sync
            def _(sy):
                pb.dma(sy, cw[:, :], cw_ap)
                pb.dma(sy, cb[:, :], cb_ap)
                pb.dma(sy, dtb[:, :], dtb_ap)
                pb.dma(sy, alog[:, :], alog_ap)
                pb.dma(sy, dpair[:, :], dpair_ap)
                pb.dma(sy, nw[:, :], nw_ap)
                pb.wait(sy, "ld")
        with nc.Block() as blk:
            @blk.scalar
            def _(s):
                s.activation(acol[:, :], alog[:, :], AF.Exp)
            @blk.vector
            def _(v):
                v.tensor_scalar(tri_w[:, 0:256], pb.iota_row[:, 0:256],
                                pb.iota_col[:, 0:1], None, ALU.is_ge)
                v.tensor_scalar(tri_w[:, 256:512], pb.iota_row[:, 0:256],
                                pb.iota_col[:, 0:1], 128.0, ALU.subtract, ALU.is_ge)
                v.memset(ones64[:, :], 1.0)
        with nc.Block() as blk:
            @blk.vector
            def _(v):
                v.tensor_scalar(acol[:, :], acol[:, :], -1.0, None, ALU.mult)
        ps512 = [pb.psum(f"a_ps512_{i}", [128, 512], F32, es=esP) for i in range(2)]
        acum_dram = nc.dram_tensor("a_acum_dram", [64, 256], F32)

        for half in range(2):
            x_ap = x_aps[half]
            esH = ExitStack()
            xfm = pb.sbuf(f"h{half}_xfm", [128, NE * 515], BF16, es=esH)
            gatebuf = pb.sbuf(f"h{half}_gate", [128, 32 * 512], BF16, es=esH)
            hidfm = pb.sbuf(f"h{half}_hid", [128, 32 * 512], BF16, es=esH)
            bfm = pb.sbuf(f"h{half}_bfm", [128, 8 * 512], BF16, es=esH)
            cfm = pb.sbuf(f"h{half}_cfm", [128, 8 * 512], BF16, es=esH)
            dtbuf = pb.sbuf(f"h{half}_dt", [64, 512], F32, es=esH)
            dt2buf = pb.sbuf(f"h{half}_dt2", [64, 512], F32, es=esH)
            scr64 = pb.sbuf(f"h{half}_scr64", [64, 512], F32, es=esH)
            x1buf = pb.sbuf(f"h{half}_x1", [128, NE * 512], BF16, es=esH)

            sc_xT = nc.enter_named_scope(f"xT{half}", False)
            esX = ExitStack()
            xrow = [pb.sbuf(f"h{half}_xrow{i}", [128, 2048], BF16, es=esX) for i in range(2)]
            psTx = [pb.psum(f"h{half}_psTx{i}", [128, 128], BF16, es=esX) for i in range(2)]
            for rt in range(5):
                r0 = 128 * rt
                rr = min(128, 515 - r0)
                with nc.Block() as blk:
                    @blk.gpsimd
                    def _(g, r0=r0, rr=rr, rt=rt):
                        pb.dma(g, xrow[rt % 2][0:rr, :].rearrange("p (ct f) -> p ct f", ct=NE),
                               x_ap[r0:r0 + rr, :].rearrange("p (ct f) -> p ct f", ct=NE))
                        pb.wait(g, "ld")
                transpose_seq(pb, [
                    (xrow[rt % 2][0:rr, 128 * ct:128 * (ct + 1)],
                     xfm[:, 515 * ct + r0:515 * ct + r0 + rr],
                     pb.ident[0:rr, 0:rr]) for ct in range(NE)], psTx)
            esX.close()
            nc.leave_named_scope(f"xT{half}", sc_xT[0], False)

            sc_ip = nc.enter_named_scope(f"inproj{half}", False)
            esW = ExitStack()
            wband = pb.sbuf(f"h{half}_wband", [128, NE * 512], BF16, es=esW)
            hbst = pb.sbuf(f"h{half}_hbst", [128, 2 * 515], BF16, es=esW)
            convtap = pb.sbuf(f"h{half}_ctap", [128, 512], F32, es=esW)
            convacc = pb.sbuf(f"h{half}_cacc", [128, 512], F32, es=esW)
            psH = pb.psum(f"h{half}_psH", [128, 16], F32, es=esW)
            psD = pb.psum(f"h{half}_psD", [64, 512], F32, es=esW)

            def inproj_group(c0, dst_write, is_hbc):
                nm = 2
                with nc.Block() as blk:
                    @blk.gpsimd
                    def _(g, c0=c0):
                        for q4 in range(4):
                            pb.dma(g, wband[:, q4 * 4 * 128 * nm:(q4 + 1) * 4 * 128 * nm]
                                   .rearrange("p (kt m) -> p kt m", kt=4),
                                   wproj_ap[512 * q4:512 * (q4 + 1), c0:c0 + 128 * nm]
                                   .rearrange("(kt p) m -> p kt m", p=128))
                        pb.wait(g, "ld")
                with nc.Block() as blk:
                    @blk.tensor
                    def _(t, c0=c0, is_hbc=is_hbc):
                        for mi in range(nm):
                            for kt in range(NE):
                                off = kt * 128 * nm + mi * 128
                                if is_hbc:
                                    t.matmul(ps512[mi][:, :], wband[:, off:off + 128],
                                             xfm[:, 515 * kt:515 * kt + 512],
                                             start=(kt == 0), stop=(kt == NE - 1))
                                    t.matmul(psH[:, 4 * mi:4 * mi + 3], wband[:, off:off + 128],
                                             xfm[:, 515 * kt + 512:515 * kt + 515],
                                             start=(kt == 0), stop=(kt == NE - 1))
                                else:
                                    t.matmul(ps512[mi][:, :], wband[:, off:off + 128],
                                             xfm[:, 515 * kt + 3:515 * kt + 515],
                                             start=(kt == 0), stop=(kt == NE - 1))
                with nc.Block() as blk:
                    @blk.vector
                    def _(v):
                        dst_write(v)

            for g2 in range(16):
                def wr(v, g2=g2):
                    for mi in range(2):
                        v.tensor_copy(gatebuf[:, (2 * g2 + mi) * 512:(2 * g2 + mi + 1) * 512],
                                      ps512[mi][:, :])
                inproj_group(256 * g2, wr, False)
            for g2 in range(24):
                def wr(v, g2=g2):
                    for mi in range(2):
                        v.tensor_copy(hbst[:, mi * 515:mi * 515 + 512], ps512[mi][:, :])
                        v.tensor_copy(hbst[:, mi * 515 + 512:mi * 515 + 515],
                                      psH[:, 4 * mi:4 * mi + 3])
                inproj_group(4096 + 256 * g2, wr, True)
                for mi in range(2):
                    j = 2 * g2 + mi
                    with nc.Block() as blk:
                        @blk.vector
                        def _(v, mi=mi, j=j):
                            iw = hbst[:, mi * 515:(mi + 1) * 515]
                            v.tensor_scalar(convacc[:, :], iw[:, 0:512],
                                            cw[:, 4 * j:4 * j + 1], None, ALU.mult)
                            for kk in range(1, 4):
                                v.tensor_scalar(convtap[:, :], iw[:, kk:kk + 512],
                                                cw[:, 4 * j + kk:4 * j + kk + 1],
                                                None, ALU.mult)
                                v.tensor_add(convacc[:, :], convacc[:, :], convtap[:, :])
                    with nc.Block() as blk:
                        @blk.scalar
                        def _(s, j=j):
                            if j < 32:
                                dst = hidfm[:, 512 * j:512 * (j + 1)]
                            elif j < 40:
                                dst = bfm[:, 512 * (j - 32):512 * (j - 31)]
                            else:
                                dst = cfm[:, 512 * (j - 40):512 * (j - 39)]
                            s.activation(dst, convacc[:, :], AF.Silu, bias=cb[:, j:j + 1])
            with nc.Block() as blk:
                @blk.gpsimd
                def _(g):
                    pb.dma(g, wband[:, 0:NE * 64].rearrange("p (kt m) -> p kt m", kt=NE),
                           wproj_ap[:, 10240:10304].rearrange("(kt p) m -> p kt m", p=128))
                    pb.wait(g, "ld")
            with nc.Block() as blk:
                @blk.tensor
                def _(t):
                    for kt in range(NE):
                        t.matmul(psD[0:64, 0:256], wband[:, kt * 64:kt * 64 + 64],
                                 xfm[:, 515 * kt + 3:515 * kt + 259],
                                 start=(kt == 0), stop=(kt == NE - 1))
                    for kt in range(NE):
                        t.matmul(psD[0:64, 256:512], wband[:, kt * 64:kt * 64 + 64],
                                 xfm[:, 515 * kt + 259:515 * kt + 515],
                                 start=(kt == 0), stop=(kt == NE - 1))
            with nc.Block() as blk:
                @blk.vector
                def _(v):
                    v.tensor_copy(dtbuf[:, :], psD[0:64, :])
            esW.close()
            nc.leave_named_scope(f"inproj{half}", sc_ip[0], False)

            if half == 0:
                dump("d_gate", gatebuf[:, :])
                dump("d_hid", hidfm[:, :])
                dump("d_bfm", bfm[:, :])
                dump("d_cfm", cfm[:, :])
                dump("d_dt", dtbuf[:, :])
            for ct in range(32):
                with nc.Block() as blk:
                    @blk.scalar
                    def _(s, ct=ct):
                        s.activation(gatebuf[:, 512 * ct:512 * (ct + 1)],
                                     gatebuf[:, 512 * ct:512 * (ct + 1)], AF.Silu)

            with nc.Block() as blk:
                @blk.vector
                def _(v):
                    v.tensor_scalar(dtbuf[:, :], dtbuf[:, :], dtb[:, 0:1], None, ALU.add)
            with nc.Block() as blk:
                @blk.scalar
                def _(s):
                    s.activation(scr64[:, :], dtbuf[:, :], AF.Abs)
            with nc.Block() as blk:
                @blk.scalar
                def _(s):
                    s.activation(scr64[:, :], scr64[:, :], AF.Exp, scale=-1.0)
            with nc.Block() as blk:
                @blk.scalar
                def _(s):
                    s.activation(scr64[:, :], scr64[:, :], AF.Ln, bias=1.0)
                @blk.vector
                def _(v):
                    v.tensor_relu(dt2buf[:, :], dtbuf[:, :])
            with nc.Block() as blk:
                @blk.vector
                def _(v):
                    v.tensor_add(dt2buf[:, :], dt2buf[:, :], scr64[:, :])
                    v.tensor_scalar(dtbuf[:, :], dt2buf[:, :], acol[:, 0:1],
                                    None, ALU.mult)

            if half == 0:
                dump("d_dt2", dt2buf[:, :])
                dump("d_adt", dtbuf[:, :])
            sc_ssd = nc.enter_named_scope(f"ssd{half}", False)
            esM = ExitStack()
            hdt = [pb.sbuf(f"h{half}_hdt{k}", [128, 4096], BF16, es=esM) for k in range(2)]
            gmtbuf = pb.sbuf(f"h{half}_gmt", [128, 8 * 512], BF16, es=esM)
            acumfm = pb.sbuf(f"h{half}_acfm", [64, 256], F32, es=esM)
            acumtm = pb.sbuf(f"h{half}_actm", [128, 128], F32, es=esM)
            acumtm2 = pb.sbuf(f"h{half}_actm2", [128, 128], F32, es=esM)
            dt2tm = pb.sbuf(f"h{half}_dt2tm", [128, 128], F32, es=esM)
            acr8 = [pb.sbuf(f"h{half}_acr8_{i}", [1, 2048], F32, es=esM) for i in range(2)]
            a255row = pb.sbuf(f"h{half}_a255", [1, 64], F32, es=esM)
            s1buf = [pb.sbuf(f"h{half}_s1_{i}", [128, 512], BF16, es=esM) for i in range(2)]
            t1buf = [pb.sbuf(f"h{half}_t1_{i}", [128, 512], BF16, es=esM) for i in range(2)]
            t2buf = [pb.sbuf(f"h{half}_t2_{i}", [128, 512], BF16, es=esM) for i in range(2)]
            mbuf = [pb.sbuf(f"h{half}_m_{i}", [128, 512], BF16, es=esM) for i in range(2)]
            ev64 = pb.sbuf(f"h{half}_ev64", [128, 512], F32, es=esM)
            for c2 in range(2):
                esS = ExitStack()
                psT2 = pb.psum(f"h{half}_psT_{c2}", [128, 128], BF16, es=esS)
                esSf = ExitStack()
                psTf = pb.psum(f"h{half}_psTf_{c2}", [128, 128], F32, es=esSf)
                with nc.Block() as blk:
                    @blk.vector
                    def _(v, c2=c2):
                        v.tensor_tensor_scan(acumfm[:, :], ones64[:, 0:256],
                                             dtbuf[:, 256 * c2:256 * (c2 + 1)],
                                             0.0, ALU.mult, ALU.add)
                for kc in range(2):
                    with nc.Block() as blk:
                        @blk.tensor
                        def _(t, kc=kc):
                            t.transpose(psTf[0:128, 0:64], acumfm[:, 128 * kc:128 * (kc + 1)],
                                        pb.ident_f[0:64, 0:64])
                    with nc.Block() as blk:
                        @blk.vector
                        def _(v, kc=kc):
                            v.tensor_copy(acumtm[:, 64 * kc:64 * (kc + 1)], psTf[0:128, 0:64])
                    with nc.Block() as blk:
                        @blk.tensor
                        def _(t, kc=kc, c2=c2):
                            t.transpose(psTf[0:128, 0:64],
                                        dt2buf[:, 256 * c2 + 128 * kc:256 * c2 + 128 * (kc + 1)],
                                        pb.ident_f[0:64, 0:64])
                    with nc.Block() as blk:
                        @blk.vector
                        def _(v, kc=kc):
                            v.tensor_copy(dt2tm[:, 64 * kc:64 * (kc + 1)], psTf[0:128, 0:64])
                with nc.Block() as blk:
                    @blk.sync
                    def _(sy):
                        pb.dma(sy, acum_dram.ap(), acumfm[:, :], name="st")
                        pb.wait(sy, "st")
                with nc.Block() as blk:
                    @blk.sync
                    def _(sy):
                        pb.dma(sy, a255row[:, :].rearrange("o (h q) -> o h q", h=64),
                               acum_dram.ap()[:, 255:256].rearrange("(o h) q -> o h q", o=1),
                               allow_slow_non_contiguous=True)
                        pb.wait(sy, "ld")
                with nc.Block() as blk:
                    @blk.tensor
                    def _(t):
                        t.matmul(psTf[:, 0:64], pb.ones_row_f[:, :], a255row[:, :],
                                 start=True, stop=True)
                with nc.Block() as blk:
                    @blk.vector
                    def _(v):
                        for kc in range(2):
                            v.tensor_sub(acumtm2[:, 64 * kc:64 * (kc + 1)], psTf[:, 0:64],
                                         acumtm[:, 64 * kc:64 * (kc + 1)])
                if half == 0 and c2 == 0:
                    dump("d_acum", acumfm[:, :])
                    dump("d_actm", acumtm[:, :])
                    dump("d_actm2", acumtm2[:, :])
                esSf.close()
                psT2b = pb.psum(f"h{half}_psT2b_{c2}", [128, 128], BF16, es=esS)
                transpose_seq(pb, [
                    (hidfm[:, 512 * ct + 256 * c2 + 128 * kc:
                           512 * ct + 256 * c2 + 128 * (kc + 1)],
                     hdt[kc][:, 128 * ct:128 * (ct + 1)], pb.ident[:, :])
                    for ct in range(32) for kc in range(2)], [psT2, psT2b])
                with nc.Block() as blk:
                    @blk.vector
                    def _(v):
                        for kc in range(2):
                            for hh in range(64):
                                v.tensor_scalar(hdt[kc][:, 64 * hh:64 * (hh + 1)],
                                                hdt[kc][:, 64 * hh:64 * (hh + 1)],
                                                dt2tm[:, 64 * kc + hh:64 * kc + hh + 1],
                                                None, ALU.mult)
                sT = pb.sem("trT"); sE = pb.sem("trE")
                bT = pb.cnt["trT"]; bE = pb.cnt["trE"]
                with nc.Block() as blk:
                    @blk.tensor
                    def _(t, c2=c2, bT=bT, bE=bE):
                        for g8 in range(8):
                            if g8 >= 2:
                                t.wait_ge(sE, bE + g8 - 1)
                            for kc in range(2):
                                mm = t.matmul(ps512[g8 % 2][:, 256 * kc:256 * (kc + 1)],
                                              bfm[:, 512 * g8 + 256 * c2 + 128 * kc:
                                                  512 * g8 + 256 * c2 + 128 * (kc + 1)],
                                              cfm[:, 512 * g8 + 256 * c2:
                                                  512 * g8 + 256 * c2 + 256],
                                              start=True, stop=True)
                            mm.then_inc(sT, 1)
                    @blk.vector
                    def _(v, bT=bT, bE=bE):
                        for g8 in range(8):
                            v.wait_ge(sT, bT + g8 + 1)
                            v.tensor_copy(gmtbuf[:, 512 * g8:512 * (g8 + 1)],
                                          ps512[g8 % 2][:, :]).then_inc(sE, 1)
                pb.cnt["trT"] += 8
                pb.cnt["trE"] += 8
                if half == 0 and c2 == 0:
                    dump("d_hdt0", hdt[0][:, :])
                    dump("d_hdt1", hdt[1][:, :])
                    dump("d_gmt", gmtbuf[:, :])
                psA = [pb.psum(f"h{half}_psA{c2}_{i}", [128, 256], F32, es=esS)
                       for i in range(2)]
                psY = [pb.psum(f"h{half}_psY{c2}_{i}", [128, 256], F32, es=esS)
                       for i in range(2)]
                sA = pb.sem("ssdA"); bA = pb.cnt["ssdA"]
                sS = pb.sem("ssdS"); bS = pb.cnt["ssdS"]
                sT1 = pb.sem("ssdT1"); bT1 = pb.cnt["ssdT1"]
                sT2 = pb.sem("ssdT2"); bT2 = pb.cnt["ssdT2"]
                sW = pb.sem("ssdW"); bW = pb.cnt["ssdW"]
                sY = pb.sem("ssdY"); bY = pb.cnt["ssdY"]
                sE = pb.sem("ssdE"); bE = pb.cnt["ssdE"]
                sR = pb.sem("ssdR"); bR = pb.cnt["ssdR"]
                with nc.Block() as blk:
                    @blk.sync
                    def _(sy, bA=bA, bR=bR):
                        for oct in range(8):
                            if oct >= 2:
                                sy.wait_ge(sA, bA + 8 * (oct - 1))
                            sy.dma_start(
                                out=acr8[oct % 2][:, :].rearrange("o (h q) -> o h q", h=8),
                                in_=acum_dram.ap()[8 * oct:8 * (oct + 1), :]
                                .rearrange("(o h) q -> o h q", o=1)).then_inc(sR, 16)
                    @blk.tensor
                    def _(t, c2=c2, bA=bA, bS=bS, bT2=bT2, bW=bW, bE=bE, bR=bR):
                        for h in range(65):
                            if h < 64:
                                if h % 8 == 0:
                                    t.wait_ge(sR, bR + 16 * (h // 8 + 1))
                                if h >= 2:
                                    t.wait_ge(sS, bS + h - 1)
                                    t.wait_ge(sT2, bT2 + h - 1)
                                t.matmul(psA[h % 2][:, :], pb.ones_row_f[:, :],
                                         acr8[(h // 8) % 2][0:1, 256 * (h % 8):256 * (h % 8 + 1)],
                                         start=True, stop=True).then_inc(sA, 1)
                            if h >= 1:
                                hh = h - 1
                                pi = hh % 2
                                t.wait_ge(sW, bW + hh + 1)
                                if hh >= 2:
                                    t.wait_ge(sE, bE + hh - 1)
                                ph = 64 * (hh % 2)
                                tp = (0, ph) if ph else None
                                t.matmul(psY[pi][ph:ph + 64, :],
                                         hdt[0][:, 64 * hh:64 * (hh + 1)],
                                         mbuf[pi][:, 0:256], start=True, stop=False,
                                         tile_position=tp)
                                t.matmul(psY[pi][ph:ph + 64, :],
                                         hdt[1][:, 64 * hh:64 * (hh + 1)],
                                         mbuf[pi][:, 256:512], start=False, stop=True,
                                         tile_position=tp).then_inc(sY, 1)
                    @blk.vector
                    def _(v, c2=c2, bA=bA, bS=bS, bT1=bT1, bY=bY):
                        for h in range(65):
                            if h < 64:
                                v.wait_ge(sA, bA + h + 1)
                                if h >= 2:
                                    v.wait_ge(sT1, bT1 + h - 1)
                                pi = h % 2
                                v.tensor_scalar(s1buf[pi][:, 0:256], psA[pi][:, :],
                                                acumtm[:, h:h + 1], 0.0,
                                                ALU.subtract, ALU.min)
                                v.tensor_scalar(s1buf[pi][:, 256:512], psA[pi][:, :],
                                                acumtm[:, 64 + h:64 + h + 1], 0.0,
                                                ALU.subtract, ALU.min).then_inc(sS, 1)
                            if h >= 1:
                                hh = h - 1
                                pi = hh % 2
                                v.wait_ge(sY, bY + hh + 1)
                                ph = 64 * (hh % 2)
                                g2 = hh // 2
                                csl = slice(512 * g2 + 256 * c2, 512 * g2 + 256 * c2 + 256)
                                esl = ev64[ph:ph + 64, 256 * pi:256 * (pi + 1)]
                                v.tensor_scalar(esl, hidfm[ph:ph + 64, csl],
                                                dpair[ph:ph + 64, g2:g2 + 1], None, ALU.mult)
                                v.tensor_add(esl, esl, psY[pi][ph:ph + 64, :])
                                v.tensor_mul(hidfm[ph:ph + 64, csl], esl,
                                             gatebuf[ph:ph + 64, csl]).then_inc(sE, 1)
                    @blk.scalar
                    def _(s, bS=bS, bW=bW):
                        for h in range(64):
                            s.wait_ge(sS, bS + h + 1)
                            if h >= 2:
                                s.wait_ge(sW, bW + h - 1)
                            pi = h % 2
                            s.activation(t1buf[pi][:, :], s1buf[pi][:, :], AF.Exp
                                         ).then_inc(sT1, 1)
                            s.activation(t2buf[pi][:, 0:256], psA[pi][:, :], AF.Exp,
                                         bias=acumtm2[:, h:h + 1])
                            s.activation(t2buf[pi][:, 256:512], psA[pi][:, :], AF.Exp,
                                         bias=acumtm2[:, 64 + h:64 + h + 1]).then_inc(sT2, 1)
                    @blk.gpsimd
                    def _(g, bT1=bT1, bT2=bT2, bY=bY):
                        for h in range(64):
                            g.wait_ge(sT1, bT1 + h + 1)
                            g.wait_ge(sT2, bT2 + h + 1)
                            if h >= 2:
                                g.wait_ge(sY, bY + h - 1)
                            pi = h % 2
                            g.tensor_mul(mbuf[pi][:, :], t1buf[pi][:, :], tri_w[:, :])
                            g.tensor_add(mbuf[pi][:, :], mbuf[pi][:, :], t2buf[pi][:, :])
                            g.tensor_mul(mbuf[pi][:, :], mbuf[pi][:, :],
                                         gmtbuf[:, 512 * (h % 8):512 * (h % 8 + 1)]
                                         ).then_inc(sW, 1)
                for nm_ in ("ssdA", "ssdS", "ssdT1", "ssdT2", "ssdW", "ssdY", "ssdE"):
                    pb.cnt[nm_] += 64
                pb.cnt["ssdR"] += 8 * 16
                esS.close()
            esM.close()
            nc.leave_named_scope(f"ssd{half}", sc_ssd[0], False)

            if half == 0:
                dump("d_yg", hidfm[:, :])
            sc_op = nc.enter_named_scope(f"outproj{half}", False)
            esO = ExitStack()
            sq_ps = pb.psum(f"h{half}_sqps", [1, 512], F32, es=esO)
            arow = pb.sbuf(f"h{half}_arow", [1, 512], F32, es=esO)
            abc512 = pb.sbuf(f"h{half}_abc512", [128, 512], F32, es=esO)
            sqb = pb.sbuf(f"h{half}_sqb", [128, 512], BF16, es=esO)
            for ct in range(32):
                with nc.Block() as blk:
                    @blk.scalar
                    def _(s, ct=ct):
                        s.activation(sqb[:, :], hidfm[:, 512 * ct:512 * (ct + 1)], AF.Square)
                with nc.Block() as blk:
                    @blk.tensor
                    def _(t, ct=ct):
                        t.matmul(sq_ps[:, :], pb.ones_col[:, :], sqb[:, :],
                                 start=(ct == 0), stop=(ct == 31))
            with nc.Block() as blk:
                @blk.vector
                def _(v):
                    v.tensor_scalar(arow[:, :], sq_ps[:, :], 1.0 / INTER, 1e-6,
                                    ALU.mult, ALU.add)
            with nc.Block() as blk:
                @blk.scalar
                def _(s):
                    s.activation(arow[:, :], arow[:, :], AF.Sqrt)
            with nc.Block() as blk:
                @blk.vector
                def _(v):
                    v.reciprocal(arow[:, :], arow[:, :])
            with nc.Block() as blk:
                @blk.tensor
                def _(t):
                    t.matmul(ps512[0][:, :], pb.ones_row_f[:, :], arow[:, :],
                             start=True, stop=True)
            with nc.Block() as blk:
                @blk.vector
                def _(v):
                    v.tensor_copy(abc512[:, :], ps512[0][:, :])
            for ct in range(32):
                with nc.Block() as blk:
                    @blk.vector
                    def _(v, ct=ct):
                        sl = hidfm[:, 512 * ct:512 * (ct + 1)]
                        v.tensor_mul(sl, sl, abc512[:, :])
                        v.tensor_scalar(sl, sl, nw[:, ct:ct + 1], None, ALU.mult)
            wband2 = pb.sbuf(f"h{half}_wband2", [128, 2 * 32 * 256], BF16, es=esO)
            pso = [pb.psum(f"h{half}_pso{i}", [128, 512], F32, es=esO) for i in range(4)]

            def wo_dma(sy, g):
                off = (g % 2) * 32 * 256
                for q4 in range(8):
                    pb.dma(sy, wband2[:, off + q4 * 1024:off + (q4 + 1) * 1024]
                           .rearrange("p (kt m) -> p kt m", kt=4),
                           wout_ap[512 * q4:512 * (q4 + 1), 256 * g:256 * (g + 1)]
                           .rearrange("(kt p) m -> p kt m", p=128), name=f"wo{half}d")

            def wo_mm(t, g):
                off = (g % 2) * 32 * 256
                ps = pso[2 * (g % 2):2 * (g % 2) + 2]
                for mi in range(2):
                    for kt in range(32):
                        wo = off + kt * 256 + mi * 128
                        mm = t.matmul(ps[mi][:, :], wband2[:, wo:wo + 128],
                                      hidfm[:, 512 * kt:512 * (kt + 1)],
                                      start=(kt == 0), stop=(kt == 31))
                return mm

            def wo_evac(v, g):
                ps = pso[2 * (g % 2):2 * (g % 2) + 2]
                for mi in range(2):
                    m0 = 2 * g + mi
                    cp = v.tensor_add(x1buf[:, 512 * m0:512 * (m0 + 1)], ps[mi][:, :],
                                      xfm[:, 515 * m0 + 3:515 * m0 + 515])
                return cp
            pipe_mm(pb, f"wo{half}", 8, wo_dma, wo_mm, wo_evac)
            esO.close()
            nc.leave_named_scope(f"outproj{half}", sc_op[0], False)
            ln_fm(pb, lambda ct: x1buf[:, 512 * ct:512 * (ct + 1)], NE, 512,
                  lambda ct: x1buf[:, 512 * ct:512 * (ct + 1)], tag=f"a{half}")
            with nc.Block() as blk:
                @blk.sync
                def _(sy, half=half):
                    pb.dma(sy, x1_ap[:, 512 * half:512 * (half + 1)]
                           .rearrange("(ct p) t -> p ct t", p=128),
                           x1buf[:, :].rearrange("p (ct t) -> p ct t", ct=NE), name="st")
                    pb.wait(sy, "st")
            esH.close()


def build_l0attn(debug=False):
    nc = bass.Bass("TRN2", target_bir_lowering=False)
    pb = PB(nc)
    x_lo = nc.dram_tensor("x_lo", [515, EMB], F32, kind="ExternalInput").ap()
    x_hi = nc.dram_tensor("x_hi", [515, EMB], F32, kind="ExternalInput").ap()
    wproj = nc.dram_tensor("w_inproj_t", [EMB, PROJ], F32, kind="ExternalInput").ap()
    cw = nc.dram_tensor("conv_w", [128, 192], F32, kind="ExternalInput").ap()
    cb = nc.dram_tensor("conv_b", [128, 48], F32, kind="ExternalInput").ap()
    dtb = nc.dram_tensor("dt_bias", [64, 1], F32, kind="ExternalInput").ap()
    alog = nc.dram_tensor("a_log", [64, 1], F32, kind="ExternalInput").ap()
    dpair = nc.dram_tensor("d_pair", [128, 32], F32, kind="ExternalInput").ap()
    nw = nc.dram_tensor("norm_w", [128, 32], F32, kind="ExternalInput").ap()
    wout = nc.dram_tensor("w_outproj_t", [INTER, EMB], F32, kind="ExternalInput").ap()
    x1 = nc.dram_tensor("x1fm", [EMB, 1024], BF16, kind="ExternalOutput").ap()
    dbg = None
    if debug:
        dbg = {}
        for nm_, shp, dt_ in (("d_gate", [128, 32 * 512], BF16), ("d_hid", [128, 32 * 512], BF16),
                              ("d_bfm", [128, 8 * 512], BF16), ("d_cfm", [128, 8 * 512], BF16),
                              ("d_dt", [64, 512], F32), ("d_dt2", [64, 512], F32),
                              ("d_adt", [64, 512], F32), ("d_acum", [64, 256], F32),
                              ("d_actm", [128, 128], F32), ("d_actm2", [128, 128], F32),
                              ("d_hdt0", [128, 4096], BF16), ("d_hdt1", [128, 4096], BF16),
                              ("d_gmt", [128, 8 * 512], BF16), ("d_yg", [128, 32 * 512], BF16)):
            dbg[nm_] = nc.dram_tensor(nm_, shp, dt_, kind="ExternalOutput").ap()
    with pb.es:
        pb.consts()
        _l0attn_body(pb, (x_lo, x_hi), wproj, cw, cb, dtb, alog, dpair, nw, wout, x1, dbg=dbg)
    return nc



def _l1attn_body(pb, core_blocks_dummy, x2_ap, qkv_ap, oproj_ap, gam_ap, bet_ap,
                 allow_ap, x3_ap):
    nc = pb.nc
    T = 1024
    SC = 1.0 / math.sqrt(2048.0)
    kag_in = nc.dram_tensor("kag_in", [EMB, T], BF16)
    vag_in = nc.dram_tensor("vag_in", [T, EMB], BF16)
    KAG = nc.dram_tensor("KAG", [8 * EMB, T], BF16, addr_space="Shared")
    VAG = nc.dram_tensor("VAG", [8 * T, EMB], BF16, addr_space="Shared")
    esP = ExitStack()
    with esP:
        gam = pb.sbuf("t_gam", [128, 16], F32, es=esP)
        bet = pb.sbuf("t_bet", [128, 16], F32, es=esP)
        xbuf = pb.sbuf("t_x", [128, NE * T], BF16, es=esP)
        qT = pb.sbuf("t_qT", [128, NE * T], BF16, es=esP)
        with nc.Block() as blk:
            @blk.sync
            def _(sy):
                pb.dma(sy, gam[:, :], gam_ap)
                pb.dma(sy, bet[:, :], bet_ap)
                pb.dma(sy, xbuf[:, :].rearrange("p (ct t) -> p ct t", ct=NE),
                       x2_ap.rearrange("(ct p) t -> p ct t", p=128))
                pb.wait(sy, "ld")
        sc_pj = nc.enter_named_scope("l1aproj", False)
        es1 = ExitStack()
        xn = pb.sbuf("t_xn", [128, NE * T], BF16, es=es1)
        ln_fm(pb, lambda ct: xbuf[:, ct * T:(ct + 1) * T], NE, T,
              lambda ct: xn[:, ct * T:(ct + 1) * T], gamma=gam, beta=bet, tag="t1")
        ktl = pb.sbuf("t_ktl", [128, NE * T], BF16, es=es1)
        vt = pb.sbuf("t_vt", [128, 8 * EMB], BF16, es=es1)
        wband = pb.sbuf("t_wband", [128, 2 * NE * 512], BF16, es=es1)
        psq = [pb.psum(f"t_psq{i}", [128, 512], F32, es=es1) for i in range(8)]
        def qk_dma(sy, g):
            which, mg = divmod(g, 8)
            colbase = 2048 * which
            m0 = 2 * mg
            wb = wband[:, (g % 2) * 4096:(g % 2) * 4096 + 4096]
            for q4 in range(4):
                pb.dma(sy, wb[:, q4 * 1024:(q4 + 1) * 1024]
                       .rearrange("p (kt m) -> p kt m", kt=4),
                       qkv_ap[512 * q4:512 * (q4 + 1),
                              colbase + 128 * m0:colbase + 128 * (m0 + 2)]
                       .rearrange("(kt p) m -> p kt m", p=128), name="qkd")

        def qk_mm(t, g):
            wb = wband[:, (g % 2) * 4096:(g % 2) * 4096 + 4096]
            ps = psq[4 * (g % 2):4 * (g % 2) + 4]
            for mi in range(2):
                for th in range(2):
                    for kt in range(NE):
                        off = kt * 256 + mi * 128
                        mm = t.matmul(ps[2 * mi + th][:, :], wb[:, off:off + 128],
                                      xn[:, kt * T + 512 * th:kt * T + 512 * (th + 1)],
                                      start=(kt == 0), stop=(kt == NE - 1))
            return mm

        def qk_evac(v, g):
            which, mg = divmod(g, 8)
            m0 = 2 * mg
            dstbuf = qT if which == 0 else ktl
            ps = psq[4 * (g % 2):4 * (g % 2) + 4]
            for mi in range(2):
                for th in range(2):
                    cp = v.tensor_copy(
                        dstbuf[:, (m0 + mi) * T + 512 * th:(m0 + mi) * T + 512 * (th + 1)],
                        ps[2 * mi + th][:, :])
            return cp
        pipe_mm(pb, "qk", 16, qk_dma, qk_mm, qk_evac)
        def v_dma(sy, g):
            wb = wband[:, (g % 2) * 8192:(g % 2) * 8192 + 8192]
            for q4 in range(4):
                pb.dma(sy, wb[:, q4 * 2048:(q4 + 1) * 2048]
                       .rearrange("p (kt m) -> p kt m", kt=4),
                       qkv_ap[512 * q4:512 * (q4 + 1),
                              4096 + 512 * g:4096 + 512 * (g + 1)]
                       .rearrange("(kt p) m -> p kt m", p=128), name="qkd")

        def v_mm(t, g):
            wb = wband[:, (g % 2) * 8192:(g % 2) * 8192 + 8192]
            for tc in range(8):
                for kt in range(NE):
                    mm = t.matmul(psq[tc][:, :], xn[:, kt * T + 128 * tc:kt * T + 128 * (tc + 1)],
                                  wb[:, kt * 512:(kt + 1) * 512],
                                  start=(kt == 0), stop=(kt == NE - 1))
            return mm

        def v_evac(v, g):
            for tc in range(8):
                cp = v.tensor_copy(vt[:, tc * EMB + 512 * g:tc * EMB + 512 * (g + 1)],
                                   psq[tc][:, :])
            return cp
        pipe_mm(pb, "qk", 4, v_dma, v_mm, v_evac, double_psum=False)
        with nc.Block() as blk:
            @blk.sync
            def _(sy):
                pb.dma(sy, kag_in.ap().rearrange("(ct p) t -> p ct t", p=128),
                       ktl[:, :].rearrange("p (ct t) -> p ct t", ct=NE), name="st")
                pb.dma(sy, vag_in.ap().rearrange("(tc p) m -> p tc m", p=128),
                       vt[:, :].rearrange("p (tc m) -> p tc m", tc=8), name="st")
                pb.wait(sy, "st")
        with nc.Block() as blk:
            @blk.gpsimd
            def _(g):
                ccs = pb.sem("cc")
                for src, dst in ((kag_in, KAG), (vag_in, VAG)):
                    g.collective_compute(
                        "AllGather", ALU.bypass, replica_groups=[list(range(NCORES))],
                        ins=[src.ap().opt()], outs=[dst.ap().opt()]).then_inc(ccs)
                    pb.cnt["cc"] += 1
                g.wait_ge(ccs, pb.cnt["cc"])
        es1.close()
        nc.leave_named_scope("l1aproj", sc_pj[0], False)
        sc_at = nc.enter_named_scope("l1aattn", False)

        OT = pb.sbuf("t_OT", [128, NE * T], BF16, es=esP)
        es2 = ExitStack()
        kbuf = [pb.sbuf(f"t_kb{i}", [128, 16 * 512], BF16, es=es2) for i in range(2)]
        vbuf = [pb.sbuf(f"t_vb{i}", [128, 4 * EMB], BF16, es=es2) for i in range(2)]
        abuf = pb.sbuf("t_ab", [128, 4 * 512], BF16, es=es2)
        pbuf = [pb.sbuf(f"t_pb{i}", [128, 512], BF16, es=es2) for i in range(2)]
        ptbuf = [pb.sbuf(f"t_pt{i}", [128, 512], BF16, es=es2) for i in range(2)]
        obig = [pb.sbuf(f"t_obig{i}", [128, EMB], F32, es=es2) for i in range(2)]
        obuf = pb.sbuf("t_ob", [128, EMB], BF16, es=es2)
        mrow = pb.sbuf("t_mrow", [128, 4 * 20], F32, es=es2)
        lrow = pb.sbuf("t_lrow", [128, 4 * 20], F32, es=es2)
        psS = [pb.psum(f"t_psS{i}", [128, 512], F32, es=es2) for i in range(2)]
        psPT = [pb.psum(f"t_psPT{i}", [128, 512], BF16, es=es2) for i in range(2)]
        psO = [pb.psum(f"t_psO{i}", [128, 512], F32, es=es2) for i in range(4)]
        pb.sem("aK")
        for hb in range(2):
          for sub in range(2):
            nslots = 8 if hb == 0 else 16
            si0 = 32 * hb
            ni = 2 * nslots
            sM = pb.sem("aM"); bM = pb.cnt["aM"]
            sR = pb.sem("aR"); bR = pb.cnt["aR"]
            bKl = pb.cnt["aK"]
            with nc.Block() as blk:
                @blk.sync
                def _(sy, hb=hb, nslots=nslots, bM=bM, sub=sub):
                    for j in range(nslots):
                        if j >= 2:
                            sy.wait_ge(sM, bM + 2 * (j - 1))
                        r = own(j)
                        co = 512 * (0 if j <= 7 else 1)
                        pb.dma(sy, kbuf[j % 2][:, :].rearrange("p (kt f) -> p kt f", kt=NE),
                               KAG.ap()[EMB * r:EMB * (r + 1), co:co + 512]
                               .rearrange("(kt p) f -> p kt f", p=128), name="aK")
                @blk.tensor
                def _(t, hb=hb, ni=ni, bKl=bKl, bM=bM, bR=bR, sub=sub):
                    for i in range(ni):
                        j, qt = divmod(i, 2)
                        if qt == 0:
                            t.wait_ge(pb.sem("aK"), bKl + 16 * (j + 1))
                        if i >= 2:
                            t.wait_ge(sR, bR + i - 1)
                        mq = 4 * hb + 2 * sub + qt
                        for kt in range(NE):
                            mm = t.matmul(psS[i % 2][:, :],
                                          qT[:, kt * T + 128 * mq:kt * T + 128 * (mq + 1)],
                                          kbuf[j % 2][:, kt * 512:(kt + 1) * 512],
                                          start=(kt == 0), stop=(kt == NE - 1))
                        mm.then_inc(sM, 1)
                @blk.vector
                def _(v, ni=ni, bM=bM):
                    for i in range(ni):
                        j, qt = divmod(i, 2)
                        v.wait_ge(sM, bM + i + 1)
                        v.reduce_max(mrow[:, 20 * qt + j:20 * qt + j + 1],
                                     psS[i % 2][:, :], AX.X).then_inc(sR, 1)
            pb.cnt["aM"] += ni
            pb.cnt["aR"] += ni
            with nc.Block() as blk:
                @blk.vector
                def _(v, nslots=nslots):
                    for qt in range(4):
                        pb.guard(v, v.reduce_max(mrow[:, 20 * qt + 16:20 * qt + 17],
                                                 mrow[:, 20 * qt:20 * qt + nslots], AX.X))
                        v.tensor_scalar(mrow[:, 20 * qt + 17:20 * qt + 18],
                                        mrow[:, 20 * qt + 16:20 * qt + 17],
                                        -SC, None, ALU.mult)
            bKl = pb.cnt["aK"]
            bM = pb.cnt["aM"]
            sE = pb.sem("aE"); bE = pb.cnt["aE"]
            sD = pb.sem("aD"); bD = pb.cnt["aD"]
            sPT = pb.sem("aPT"); bPT = pb.cnt["aPT"]
            sV = pb.sem("aV"); bV = pb.cnt["aV"]
            sO = pb.sem("aO"); bO = pb.cnt["aO"]
            sA = pb.sem("aA"); bA = pb.cnt["aA"]
            with nc.Block() as blk:
                @blk.sync
                def _(sy, hb=hb, nslots=nslots, si0=si0, bO=bO, sub=sub):
                    for j in range(nslots):
                        if j >= 2:
                            sy.wait_ge(sO, bO + 2 * (j - 1))
                        r = own(j)
                        co = 512 * (0 if j <= 7 else 1)
                        pb.dma(sy, kbuf[j % 2][:, :].rearrange("p (kt f) -> p kt f", kt=NE),
                               KAG.ap()[EMB * r:EMB * (r + 1), co:co + 512]
                               .rearrange("(kt p) f -> p kt f", p=128), name="aK")
                        pb.dma(sy, vbuf[j % 2][:, :].rearrange("p (kc f) -> p kc f", kc=4),
                               VAG.ap()[T * r + co:T * r + co + 512, :]
                               .rearrange("(kc p) f -> p kc f", p=128), name="aK")
                        for qt in range(2):
                            pb.dma(sy, abuf[:, 512 * (2 * (j % 2) + qt):
                                            512 * (2 * (j % 2) + qt) + 512],
                                   allow_ap[si0 + nslots * (2 * sub + qt) + j, :, :],
                                   name="aK")
                @blk.tensor
                def _(t, hb=hb, ni=ni, bKl=bKl, bE=bE, bD=bD, bV=bV, bA=bA, sub=sub):
                    for i in range(ni + 1):
                        if i < ni:
                            j, qt = divmod(i, 2)
                            if qt == 0:
                                t.wait_ge(pb.sem("aK"), bKl + 64 * (j + 1))
                            if i >= 2:
                                t.wait_ge(sE, bE + i - 1)
                            mq = 4 * hb + 2 * sub + qt
                            for kt in range(NE):
                                mm = t.matmul(psS[i % 2][:, :],
                                              qT[:, kt * T + 128 * mq:kt * T + 128 * (mq + 1)],
                                              kbuf[j % 2][:, kt * 512:(kt + 1) * 512],
                                              start=(kt == 0), stop=(kt == NE - 1))
                            mm.then_inc(sM, 1)
                        if i >= 1:
                            k = i - 1
                            jj = k // 4
                            t.wait_ge(sD, bD + k + 1)
                            if k >= 2:
                                t.wait_ge(sV, bV + k - 1)
                            for kc in range(4):
                                tt = t.transpose(psPT[k % 2][:, 128 * kc:128 * (kc + 1)],
                                                 pbuf[k % 2][:, 128 * kc:128 * (kc + 1)],
                                                 pb.ident[:, :])
                            tt.then_inc(sPT, 1)
                            t.wait_ge(sV, bV + k + 1)
                            if k >= 1:
                                t.wait_ge(sA, bA + k)
                            for kc in range(4):
                                for og in range(4):
                                    mm = t.matmul(psO[og][:, :],
                                                  ptbuf[k % 2][:, 128 * kc:128 * (kc + 1)],
                                                  vbuf[jj % 2][:, kc * EMB + 512 * og:
                                                               kc * EMB + 512 * (og + 1)],
                                                  start=(kc == 0), stop=(kc == 3))
                            mm.then_inc(sO, 1)
                @blk.scalar
                def _(s, ni=ni, bM=bM, bD=bD, bPT=bPT):
                    for i in range(ni):
                        j, qt = divmod(i, 2)
                        s.wait_ge(sM, bM + i + 1)
                        if i >= 2:
                            s.wait_ge(sD, bD + i - 1)
                            s.wait_ge(sPT, bPT + i - 1)
                        s.activation(pbuf[i % 2][:, :], psS[i % 2][:, :], AF.Exp,
                                     bias=mrow[:, 20 * qt + 17:20 * qt + 18],
                                     scale=SC).then_inc(sE, 1)
                @blk.vector
                def _(v, ni=ni, bE=bE, bPT=bPT, bO=bO):
                    for i in range(ni):
                        j, qt = divmod(i, 2)
                        v.wait_ge(sE, bE + i + 1)
                        v.tensor_mul(pbuf[i % 2][:, :], pbuf[i % 2][:, :],
                                     abuf[:, 512 * (2 * (j % 2) + qt):
                                          512 * (2 * (j % 2) + qt) + 512])
                        v.reduce_sum(lrow[:, 20 * qt + j:20 * qt + j + 1],
                                     pbuf[i % 2][:, :], AX.X).then_inc(sD, 1)
                        v.wait_ge(sPT, bPT + i + 1)
                        v.tensor_copy(ptbuf[i % 2][:, :], psPT[i % 2][:, :]).then_inc(sV, 1)
                        v.wait_ge(sO, bO + i + 1)
                        for og in range(4):
                            if i < 2:
                                cp = v.tensor_copy(obig[qt][:, 512 * og:512 * (og + 1)],
                                                   psO[og][:, :])
                            else:
                                cp = v.tensor_add(obig[qt][:, 512 * og:512 * (og + 1)],
                                                  obig[qt][:, 512 * og:512 * (og + 1)],
                                                  psO[og][:, :])
                        cp.then_inc(sA, 1)
            pb.cnt["aM"] += ni
            pb.cnt["aE"] += ni
            pb.cnt["aD"] += ni
            pb.cnt["aPT"] += ni
            pb.cnt["aV"] += ni
            pb.cnt["aO"] += ni
            pb.cnt["aA"] += ni
            for qt in range(2):
                m = 4 * hb + 2 * sub + qt
                with nc.Block() as blk:
                    @blk.vector
                    def _(v, qt=qt, nslots=nslots):
                        pb.guard(v, v.reduce_sum(lrow[:, 20 * qt + 16:20 * qt + 17],
                                                 lrow[:, 20 * qt:20 * qt + nslots], AX.X))
                        pb.guard(v, v.reciprocal(lrow[:, 20 * qt + 17:20 * qt + 18],
                                                 lrow[:, 20 * qt + 16:20 * qt + 17]))
                        for og in range(4):
                            v.tensor_scalar(obuf[:, 512 * og:512 * (og + 1)],
                                            obig[qt][:, 512 * og:512 * (og + 1)],
                                            lrow[:, 20 * qt + 17:20 * qt + 18],
                                            None, ALU.mult)
                transpose_seq(pb, [
                    (obuf[:, 128 * ct:128 * (ct + 1)],
                     OT[:, ct * T + 128 * m:ct * T + 128 * (m + 1)], pb.ident[:, :])
                    for ct in range(NE)], psPT)
        es2.close()
        nc.leave_named_scope("l1aattn", sc_at[0], False)
        sc_oo = nc.enter_named_scope("l1aoproj", False)
        es3 = ExitStack()
        wband2 = pb.sbuf("t_wband2", [128, 2 * 4096], BF16, es=es3)
        pso = [pb.psum(f"t_pso{i}", [128, 512], F32, es=es3) for i in range(8)]
        def op_dma(sy, g):
            wb = wband2[:, (g % 2) * 4096:(g % 2) * 4096 + 4096]
            for q4 in range(4):
                pb.dma(sy, wb[:, q4 * 1024:(q4 + 1) * 1024]
                       .rearrange("p (kt m) -> p kt m", kt=4),
                       oproj_ap[512 * q4:512 * (q4 + 1), 256 * g:256 * (g + 1)]
                       .rearrange("(kt p) m -> p kt m", p=128), name="opd")

        def op_mm(t, g):
            wb = wband2[:, (g % 2) * 4096:(g % 2) * 4096 + 4096]
            ps = pso[4 * (g % 2):4 * (g % 2) + 4]
            for mi in range(2):
                for th in range(2):
                    for kt in range(NE):
                        off = kt * 256 + mi * 128
                        mm = t.matmul(ps[2 * mi + th][:, :], wb[:, off:off + 128],
                                      OT[:, kt * T + 512 * th:kt * T + 512 * (th + 1)],
                                      start=(kt == 0), stop=(kt == NE - 1))
            return mm

        def op_evac(v, g):
            ps = pso[4 * (g % 2):4 * (g % 2) + 4]
            for mi in range(2):
                for th in range(2):
                    sl = slice((2 * g + mi) * T + 512 * th, (2 * g + mi) * T + 512 * (th + 1))
                    cp = v.scalar_tensor_tensor(xbuf[:, sl], xbuf[:, sl], 2.0,
                                                ps[2 * mi + th][:, :], ALU.mult, ALU.add)
            return cp
        pipe_mm(pb, "op", 8, op_dma, op_mm, op_evac)
        es3.close()
        nc.leave_named_scope("l1aoproj", sc_oo[0], False)
        ln_fm(pb, lambda ct: xbuf[:, ct * T:(ct + 1) * T], NE, T,
              lambda ct: xbuf[:, ct * T:(ct + 1) * T], tag="t2")
        with nc.Block() as blk:
            @blk.sync
            def _(sy):
                pb.dma(sy, x3_ap.rearrange("(ct p) t -> p ct t", p=128),
                       xbuf[:, :].rearrange("p (ct t) -> p ct t", ct=NE), name="st")
                pb.wait(sy, "st")


def build_l1attn():
    nc = bass.Bass("TRN2", target_bir_lowering=False)
    pb = PB(nc)
    x2 = nc.dram_tensor("x2fm", [EMB, 1024], BF16, kind="ExternalInput").ap()
    qkv = nc.dram_tensor("qkv1", [EMB, 6144], F32, kind="ExternalInput").ap()
    oproj = nc.dram_tensor("oproj", [EMB, EMB], F32, kind="ExternalInput").ap()
    gam = nc.dram_tensor("gamma", [128, 16], F32, kind="ExternalInput").ap()
    bet = nc.dram_tensor("beta", [128, 16], F32, kind="ExternalInput").ap()
    allow = nc.dram_tensor("allow", [96, 128, 512], BF16, kind="ExternalInput").ap()
    x3 = nc.dram_tensor("x3fm", [EMB, 1024], BF16, kind="ExternalOutput").ap()
    with pb.es:
        pb.consts()
        _l1attn_body(pb, None, x2, qkv, oproj, gam, bet, allow, x3)
    return nc



def _l1ffn_body(pb, x3_ap, wqkv_ap, wgate_ap, out_ap, dbg=None):
    nc = pb.nc

    def dump(name, src_ap):
        if dbg is None or name not in dbg:
            return
        with nc.Block() as blk:
            @blk.sync
            def _(sy):
                pb.dma(sy, dbg[name], src_ap, name="st")
                pb.wait(sy, "st")
    from concourse.aot_env import allow_ici_all_to_all
    T = 1024
    SC2 = 1.0 / math.sqrt(128.0)
    ksh_in = nc.dram_tensor("ksh_in", [128, T], BF16)
    vsh_in = nc.dram_tensor("vsh_in", [T, 128], BF16)
    knm_in = nc.dram_tensor("knm_in", [1, 2], F32)
    KSHg = nc.dram_tensor("KSHg", [8 * 128, T], BF16, addr_space="Shared")
    Vg = nc.dram_tensor("Vg", [8 * T, 128], BF16, addr_space="Shared")
    KNg = nc.dram_tensor("KNg", [8, 2], F32, addr_space="Shared")
    qa_in = nc.dram_tensor("qa_in", [EMB, T], BF16)
    QA = nc.dram_tensor("QA", [EMB, T], BF16)
    ya_in = nc.dram_tensor("ya_in", [8 * T, 256], BF16)
    YA = nc.dram_tensor("YA", [8 * T, 256], BF16)
    esP = ExitStack()
    with esP:
        xbuf = pb.sbuf("u_x", [128, NE * T], BF16, es=esP)
        triv = pb.sbuf("u_triv", [128, 4 * 512], BF16, es=esP)
        knm = pb.sbuf("u_knm", [1, 18], F32, es=esP)
        with nc.Block() as blk:
            @blk.sync
            def _(sy):
                pb.dma(sy, xbuf[:, :].rearrange("p (ct t) -> p ct t", ct=NE),
                       x3_ap.rearrange("(ct p) t -> p ct t", p=128))
                pb.wait(sy, "ld")
            @blk.vector
            def _(v):
                for vv in range(4):
                    v.tensor_scalar(triv[:, 512 * vv:512 * (vv + 1)], pb.iota_row[:, 0:512],
                                    pb.iota_col[:, 0:1], float(128 * vv),
                                    ALU.subtract, ALU.is_ge)

        sc_p2 = nc.enter_named_scope("l1fproj", False)
        es1 = ExitStack()
        qTl = pb.sbuf("u_qT", [128, NE * T], BF16, es=es1)
        ktm = pb.sbuf("u_ktm", [128, 8 * 512], BF16, es=es1)
        vtm = pb.sbuf("u_vtm", [128, 8 * 512], BF16, es=es1)
        kfull = pb.sbuf("u_kfull", [128, 8 * EMB], BF16, es=es1)
        vfull = pb.sbuf("u_vfull", [128, 8 * EMB], BF16, es=es1)
        wband = pb.sbuf("u_wband", [128, 2 * NE * 512], BF16, es=es1)
        glog = pb.sbuf("u_glog", [128, 8 * 16], F32, es=es1)
        kshtm = pb.sbuf("u_kshtm", [128, 8 * 128], BF16, es=es1)
        vshtm = pb.sbuf("u_vshtm", [128, 8 * 128], BF16, es=es1)
        kshT = pb.sbuf("u_kshT", [128, 8 * 128], BF16, es=es1)
        gsc = pb.sbuf("u_gsc", [128, 4], F32, es=es1)
        expg = pb.sbuf("u_expg", [128, 16], F32, es=es1)
        indb = pb.sbuf("u_indb", [128, 16], F32, es=es1)
        gt1b = pb.sbuf("u_gt1b", [128, 16], F32, es=es1)
        tmp128 = pb.sbuf("u_tmp128", [128, 128], BF16, es=es1)
        sq128 = pb.sbuf("u_sq128", [128, 128], F32, es=es1)
        kn2 = pb.sbuf("u_kn2", [128, 12], F32, es=es1)
        wgb = pb.sbuf("u_wgb", [128, NE * 16], F32, es=es1)
        es_pq = ExitStack()
        psq = [pb.psum(f"u_psq{i}", [128, 512], F32, es=es_pq) for i in range(8)]
        def q2_dma(sy, g):
            wb = wband[:, (g % 2) * 4096:(g % 2) * 4096 + 4096]
            for q4 in range(4):
                pb.dma(sy, wb[:, q4 * 1024:(q4 + 1) * 1024]
                       .rearrange("p (kt m) -> p kt m", kt=4),
                       wqkv_ap[512 * q4:512 * (q4 + 1), 256 * g:256 * (g + 1)]
                       .rearrange("(kt p) m -> p kt m", p=128), name="q2d")

        def q2_mm(t, g):
            wb = wband[:, (g % 2) * 4096:(g % 2) * 4096 + 4096]
            ps = psq[4 * (g % 2):4 * (g % 2) + 4]
            for mi in range(2):
                for th in range(2):
                    for kt in range(NE):
                        off = kt * 256 + mi * 128
                        mm = t.matmul(ps[2 * mi + th][:, :], wb[:, off:off + 128],
                                      xbuf[:, kt * T + 512 * th:kt * T + 512 * (th + 1)],
                                      start=(kt == 0), stop=(kt == NE - 1))
            return mm

        def q2_evac(v, g):
            ps = psq[4 * (g % 2):4 * (g % 2) + 4]
            for mi in range(2):
                for th in range(2):
                    cp = v.tensor_copy(
                        qTl[:, (2 * g + mi) * T + 512 * th:(2 * g + mi) * T + 512 * (th + 1)],
                        ps[2 * mi + th][:, :])
            return cp
        pipe_mm(pb, "q2", 8, q2_dma, q2_mm, q2_evac)
        def kv_dma(sy, g):
            wb = wband[:, (g % 2) * 8192:(g % 2) * 8192 + 8192]
            colbase = 2048 + 2048 * (g // 4)
            og = g % 4
            for q4 in range(4):
                pb.dma(sy, wb[:, q4 * 2048:(q4 + 1) * 2048]
                       .rearrange("p (kt m) -> p kt m", kt=4),
                       wqkv_ap[512 * q4:512 * (q4 + 1),
                               colbase + 512 * og:colbase + 512 * (og + 1)]
                       .rearrange("(kt p) m -> p kt m", p=128), name="q2d")

        def kv_mm(t, g):
            wb = wband[:, (g % 2) * 8192:(g % 2) * 8192 + 8192]
            for tc in range(8):
                for kt in range(NE):
                    mm = t.matmul(psq[tc][:, :],
                                  xbuf[:, kt * T + 128 * tc:kt * T + 128 * (tc + 1)],
                                  wb[:, kt * 512:(kt + 1) * 512],
                                  start=(kt == 0), stop=(kt == NE - 1))
            return mm

        def kv_evac(v, g):
            dstbuf = kfull if g < 4 else vfull
            og = g % 4
            for tc in range(8):
                cp = v.tensor_copy(dstbuf[:, tc * EMB + 512 * og:tc * EMB + 512 * (og + 1)],
                                   psq[tc][:, :])
            return cp
        pipe_mm(pb, "q2", 8, kv_dma, kv_mm, kv_evac, double_psum=False)
        xg32 = pb.sbuf("u_xg32", [128, T], F32, es=es1)
        with nc.Block() as blk:
            @blk.sync
            def _(sy):
                pb.dma(sy, wgb[:, :].rearrange("p (kt m) -> p kt m", kt=NE),
                       wgate_ap.rearrange("(kt p) m -> p kt m", p=128))
                pb.wait(sy, "ld")
        for kt in range(NE):
            with nc.Block() as blk:
                @blk.vector
                def _(v, kt=kt):
                    v.tensor_copy(xg32[:, :], xbuf[:, kt * T:(kt + 1) * T])
            with nc.Block() as blk:
                @blk.tensor
                def _(t, kt=kt):
                    for tc in range(8):
                        t.matmul(psq[tc][:, 0:16], xg32[:, 128 * tc:128 * (tc + 1)],
                                 wgb[:, kt * 16:(kt + 1) * 16],
                                 start=(kt == 0), stop=(kt == NE - 1))
        with nc.Block() as blk:
            @blk.vector
            def _(v):
                for tc in range(8):
                    v.tensor_copy(glog[:, 16 * tc:16 * (tc + 1)], psq[tc][:, 0:16])
        es_pq.close()
        dump("d_glog", glog[:, :])
        dump("d_vfull", vfull[:, 0:2048])
        dump("d_kfull", kfull[:, 0:2048])
        for tc in range(8):
            gsl = glog[:, 16 * tc:16 * (tc + 1)]
            with nc.Block() as blk:
                @blk.vector
                def _(v, gsl=gsl):
                    pb.guard(v, v.reduce_max(gsc[:, 0:1], gsl, AX.X))
                    v.tensor_scalar(gsc[:, 1:2], gsc[:, 0:1], -1.0, None, ALU.mult)
            with nc.Block() as blk:
                @blk.scalar
                def _(s, gsl=gsl):
                    s.activation(expg[:, :], gsl, AF.Exp, bias=gsc[:, 1:2])
            with nc.Block() as blk:
                @blk.vector
                def _(v, gsl=gsl):
                    v.reduce_sum(gsc[:, 2:3], expg[:, :], AX.X)
            with nc.Block() as blk:
                @blk.vector
                def _(v):
                    v.reciprocal(gsc[:, 3:4], gsc[:, 2:3])
            with nc.Block() as blk:
                @blk.vector
                def _(v, gsl=gsl):
                    v.tensor_scalar(indb[:, :], gsl, gsc[:, 0:1], None, ALU.is_equal)
                    v.tensor_scalar(gt1b[:, :], indb[:, :], gsc[:, 3:4], None, ALU.mult)
            with nc.Block() as blk:
                @blk.vector
                def _(v, tc=tc):
                    ksl = kshtm[:, 128 * tc:128 * (tc + 1)]
                    vsl = vshtm[:, 128 * tc:128 * (tc + 1)]
                    v.tensor_scalar(ksl, kfull[:, tc * EMB:tc * EMB + 128],
                                    indb[:, 0:1], None, ALU.mult)
                    v.tensor_scalar(vsl, vfull[:, tc * EMB:tc * EMB + 128],
                                    gt1b[:, 0:1], None, ALU.mult)
                    for h in range(1, 16):
                        v.tensor_scalar(tmp128[:, :],
                                        kfull[:, tc * EMB + 128 * h:tc * EMB + 128 * (h + 1)],
                                        indb[:, h:h + 1], None, ALU.mult)
                        v.tensor_add(ksl, ksl, tmp128[:, :])
                        v.tensor_scalar(tmp128[:, :],
                                        vfull[:, tc * EMB + 128 * h:tc * EMB + 128 * (h + 1)],
                                        gt1b[:, h:h + 1], None, ALU.mult)
                        v.tensor_add(vsl, vsl, tmp128[:, :])
            with nc.Block() as blk:
                @blk.vector
                def _(v, tc=tc):
                    ksl = kshtm[:, 128 * tc:128 * (tc + 1)]
                    pb.guard(v, v.tensor_mul(sq128[:, :], ksl, ksl))
                    v.reduce_sum(kn2[:, tc:tc + 1], sq128[:, :], AX.X)
        esT = ExitStack()
        psTk = pb.psum("u_psTk", [128, 128], BF16, es=esT)
        psTn = pb.psum("u_psTn", [1, 128], F32, es=esT)
        for hh in range(2):
            with nc.Block() as blk:
                @blk.vector
                def _(v, hh=hh):
                    v.reduce_max(kn2[:, 8 + hh:9 + hh], kn2[:, 4 * hh:4 * (hh + 1)], AX.X)
            with nc.Block() as blk:
                @blk.tensor
                def _(t, hh=hh):
                    t.transpose(psTn[0:1, 0:128], kn2[:, 8 + hh:9 + hh], pb.ident_f[:, :])
            with nc.Block() as blk:
                @blk.vector
                def _(v, hh=hh):
                    v.reduce_max(knm[0:1, hh:hh + 1], psTn[0:1, 0:128], AX.X)
        psTk2 = pb.psum("u_psTk2", [128, 128], BF16, es=esT)
        transpose_seq(pb, [
            (kshtm[:, 128 * tc:128 * (tc + 1)], kshT[:, 128 * tc:128 * (tc + 1)],
             pb.ident[:, :]) for tc in range(8)], [psTk, psTk2])
        esT.close()
        dump("d_gt1", gt1b[:, :])
        dump("d_ind", indb[:, :])
        dump("d_top1", gsc[:, :])
        dump("d_ksh", kshtm[:, :])
        dump("d_vsh", vshtm[:, :])
        dump("d_kshT", kshT[:, :])
        dump("d_qT", qTl[:, 0:2048])
        with nc.Block() as blk:
            @blk.sync
            def _(sy):
                pb.dma(sy, ksh_in.ap(), kshT[:, :], name="st")
                pb.dma(sy, vsh_in.ap().rearrange("(tc p) m -> p tc m", p=128),
                       vshtm[:, :].rearrange("p (tc m) -> p tc m", tc=8), name="st")
                pb.dma(sy, knm_in.ap(), knm[0:1, 0:2], name="st")
                pb.dma(sy, qa_in.ap().rearrange("(ct p) t -> p ct t", p=128),
                       qTl[:, :].rearrange("p (ct t) -> p ct t", ct=NE), name="st")
                pb.wait(sy, "st")
        with nc.Block() as blk:
            @blk.gpsimd
            def _(g):
                ccs = pb.sem("cc")
                for src, dst in ((ksh_in, KSHg), (vsh_in, Vg), (knm_in, KNg)):
                    g.collective_compute(
                        "AllGather", ALU.bypass, replica_groups=[list(range(NCORES))],
                        ins=[src.ap().opt()], outs=[dst.ap().opt()]).then_inc(ccs)
                    pb.cnt["cc"] += 1
                with allow_ici_all_to_all():
                    g.collective_compute(
                        "AllToAll", ALU.bypass, replica_groups=[list(range(NCORES))],
                        ins=[qa_in.ap().opt()], outs=[QA.ap().opt()]).then_inc(ccs)
                    pb.cnt["cc"] += 1
                g.wait_ge(ccs, pb.cnt["cc"])
        es1.close()
        nc.leave_named_scope("l1fproj", sc_p2[0], False)

        with nc.Block() as blk:
            @blk.sync
            def _(sy):
                pb.dma(sy, knm[0:1, 0:16].rearrange("o (r h) -> o r h", r=8),
                       KNg.ap().rearrange("(o r) h -> o r h", o=1))
                pb.wait(sy, "ld")
        with nc.Block() as blk:
            @blk.vector
            def _(v):
                v.reduce_max(knm[0:1, 16:17], knm[0:1, 0:16], AX.X)
        dump("d_knm", knm[0:1, 0:18])

        sc_a2 = nc.enter_named_scope("l1fattn", False)
        es2 = ExitStack()
        kshbuf = pb.sbuf("u_kshb", [128, 16 * 512], BF16, es=es2)
        vshbuf = pb.sbuf("u_vshb", [128, 64 * 128], BF16, es=es2)
        qbuf2 = [pb.sbuf(f"u_qbuf{i}", [128, 1024], BF16, es=es2) for i in range(2)]
        sqv = pb.sbuf("u_sqv", [128, 512], F32, es=es2)
        brow = pb.sbuf("u_brow", [1, 8], F32, es=es2)
        nbcol = pb.sbuf("u_nbcol", [128, 2], F32, es=es2)
        ptile = [pb.sbuf(f"u_pt{i}", [128, 512], BF16, es=es2) for i in range(2)]
        lacc = pb.sbuf("u_lacc", [128, 512], F32, es=es2)
        otbuf = pb.sbuf("u_otb", [128, 512], BF16, es=es2)
        ybuf = [pb.sbuf(f"u_ybuf{i}", [128, 1024], BF16, es=es2) for i in range(2)]
        lrow = pb.sbuf("u_lrow", [1, 512], F32, es=es2)
        linv = pb.sbuf("u_linv", [128, 8], F32, es=es2)
        psS = [pb.psum(f"u_psS{i}", [128, 512], F32, es=es2) for i in range(2)]
        psOT = pb.psum("u_psOT", [128, 512], F32, es=es2)
        psL = pb.psum("u_psL", [1, 512], F32, es=es2)
        psB = pb.psum("u_psB", [128, 512], F32, es=es2)
        psTy = [pb.psum(f"u_psTy{i}", [128, 512], BF16, es=es2) for i in range(2)]
        kvd = pb.sem("l1fkv")
        yss = pb.sem("l1fys")
        sPr = pb.sem("l1fpr")
        sPoD = pb.sem("l1fpd")
        sPoT = pb.sem("l1fpt")
        sM = pb.sem("l1fM")
        sE = pb.sem("l1fE")
        sD = pb.sem("l1fD")
        sV = pb.sem("l1fV")
        sA = pb.sem("l1fA")

        def bMEV(qb, hl):
            return 4 * qb * (qb + 1) + hl * 4 * (qb + 1)

        def kv_emit(sy, j):
            co = 512 * (0 if j <= 7 else 1)
            r = own(j)
            pb.dma(sy, kshbuf[:, 512 * j:512 * (j + 1)],
                   KSHg.ap()[128 * r:128 * (r + 1), co:co + 512], name="l1fkv")
            pb.dma(sy, vshbuf[:, 512 * j:512 * (j + 1)]
                   .rearrange("p (kc m) -> p kc m", kc=4),
                   Vg.ap()[T * r + co:T * r + co + 512, :]
                   .rearrange("(kc p) m -> p kc m", p=128), name="l1fkv")
            for hl in range(2):
                pb.dma(sy, qbuf2[j % 2][:, 512 * hl:512 * (hl + 1)],
                       QA.ap()[256 * r + 128 * hl:256 * r + 128 * (hl + 1), co:co + 512],
                       name="l1fkv")

        with nc.Block() as blk:
            @blk.sync
            def _(sy):
                kv_emit(sy, 0)
        for qb in range(16):
            nkc = 4 * (qb + 1)
            kvneed = 64 * (qb + 1)
            with nc.Block() as blk:
                @blk.sync
                def _(sy, qb=qb):
                    if qb < 15:
                        kv_emit(sy, qb + 1)
                    dd = own(qb)
                    row0 = T * dd + 512 * (0 if qb <= 7 else 1)
                    for hl in range(2):
                        sy.wait_ge(sPoD, 3 * (2 * qb + hl) + 3)
                        pb.dma(sy, ya_in.ap()[row0:row0 + 512, 128 * hl:128 * (hl + 1)]
                               .rearrange("(qt p) m -> p qt m", p=128),
                               ybuf[qb % 2][:, 512 * hl:512 * (hl + 1)]
                               .rearrange("p (qt m) -> p qt m", qt=4), name="l1fys")

                @blk.tensor
                def _(t, qb=qb, nkc=nkc, kvneed=kvneed):
                    for hl in range(2):
                        ih = 2 * qb + hl
                        bPr = 7 * ih
                        bM = bE = bV = bMEV(qb, hl)
                        bD = 4 * ih
                        qsl = qbuf2[qb % 2][:, 512 * hl:512 * (hl + 1)]
                        t.wait_ge(sPr, bPr + 1)
                        t.matmul(psL[:, :], pb.ones_col_f[:, :], sqv[:, :],
                                 start=True, stop=True).then_inc(sPr, 1)
                        t.wait_ge(sPr, bPr + 5)
                        t.matmul(psB[:, 8 + hl:9 + hl], pb.ones_row_f[:, :],
                                 brow[0:1, 3:4], start=True, stop=True
                                 ).then_inc(sPr, 1)
                        for i in range(nkc + 1):
                            if i < nkc:
                                if i == 0:
                                    t.wait_ge(kvd, kvneed)
                                if i >= 2:
                                    t.wait_ge(sE, bE + i - 1)
                                t.matmul(psS[i % 2][:, :],
                                         kshbuf[:, 128 * i:128 * (i + 1)], qsl,
                                         start=True, stop=True).then_inc(sM, 1)
                            if i >= 1:
                                k = i - 1
                                t.wait_ge(sE, bE + k + 1)
                                if k >= 4 * qb:
                                    t.wait_ge(sD, bD + (k - 4 * qb) + 1)
                                t.matmul(psOT[:, :], vshbuf[:, 128 * k:128 * (k + 1)],
                                         ptile[k % 2][:, :], start=(k == 0),
                                         stop=(k == nkc - 1)).then_inc(sV, 1)
                        t.wait_ge(sA, bM + nkc)
                        t.matmul(psL[:, :], pb.ones_col_f[:, :], lacc[:, :],
                                 start=True, stop=True).then_inc(sPoT, 1)
                        t.wait_ge(sPoD, 3 * ih + 2)
                        for qt in range(4):
                            lt = t.transpose(psB[:, 4 * hl + qt:4 * hl + qt + 1],
                                             lrow[0:1, 128 * qt:128 * (qt + 1)],
                                             pb.ident_f[0:1, 0:1])
                        lt.then_inc(sPoT, 1)
                        for qt in range(4):
                            yt = t.transpose(psTy[hl][:, 128 * qt:128 * (qt + 1)],
                                             otbuf[:, 128 * qt:128 * (qt + 1)],
                                             pb.ident[:, :])
                        yt.then_inc(sPoT, 1)

                @blk.scalar
                def _(s, qb=qb, nkc=nkc):
                    for hl in range(2):
                        ih = 2 * qb + hl
                        bPr = 7 * ih
                        bM = bV = bA = bMEV(qb, hl)
                        s.wait_ge(sPr, bPr + 3)
                        s.activation(brow[0:1, 2:3], brow[0:1, 1:2], AF.Sqrt
                                     ).then_inc(sPr, 1)
                        s.wait_ge(sPr, bPr + 7)
                        for k in range(nkc):
                            s.wait_ge(sM, bM + k + 1)
                            if k >= 2:
                                s.wait_ge(sV, bV + k - 1)
                                s.wait_ge(sA, bA + k - 1)
                            s.activation(ptile[k % 2][:, :], psS[k % 2][:, :], AF.Exp,
                                         bias=nbcol[:, hl:hl + 1], scale=SC2
                                         ).then_inc(sE, 1)

                @blk.vector
                def _(v, qb=qb, nkc=nkc, kvneed=kvneed):
                    for hl in range(2):
                        ih = 2 * qb + hl
                        bPr = 7 * ih
                        bE = bV = bMEV(qb, hl)
                        qsl = qbuf2[qb % 2][:, 512 * hl:512 * (hl + 1)]
                        if hl == 0:
                            v.wait_ge(kvd, kvneed)
                        v.tensor_mul(sqv[:, :], qsl, qsl).then_inc(sPr, 1)
                        v.wait_ge(sPr, bPr + 2)
                        pb.guard(v, v.reduce_max(brow[0:1, 0:1], psL[:, :], AX.X))
                        v.tensor_mul(brow[0:1, 1:2], brow[0:1, 0:1], knm[0:1, 16:17]
                                     ).then_inc(sPr, 1)
                        v.wait_ge(sPr, bPr + 4)
                        v.tensor_scalar(brow[0:1, 3:4], brow[0:1, 2:3], -SC2, None,
                                        ALU.mult).then_inc(sPr, 1)
                        v.wait_ge(sPr, bPr + 6)
                        v.tensor_copy(nbcol[:, hl:hl + 1], psB[:, 8 + hl:9 + hl]
                                      ).then_inc(sPr, 1)
                        for k in range(nkc):
                            v.wait_ge(sE, bE + k + 1)
                            if k >= 4 * qb:
                                v.tensor_mul(ptile[k % 2][:, :], ptile[k % 2][:, :],
                                             triv[:, 512 * (k % 4):512 * (k % 4 + 1)]
                                             ).then_inc(sD, 1)
                            if k == 0:
                                ac = v.tensor_copy(lacc[:, :], ptile[0][:, :])
                            else:
                                ac = v.tensor_add(lacc[:, :], lacc[:, :],
                                                  ptile[k % 2][:, :])
                            ac.then_inc(sA, 1)
                        v.wait_ge(sV, bV + nkc)
                        v.tensor_copy(otbuf[:, :], psOT[:, :]).then_inc(sPoD, 1)
                        v.wait_ge(sPoT, 3 * ih + 1)
                        v.tensor_copy(lrow[0:1, :], psL[:, :]).then_inc(sPoD, 1)
                        v.wait_ge(sPoT, 3 * ih + 2)
                        pb.guard(v, v.reciprocal(linv[:, 4 * hl:4 * hl + 4],
                                                 psB[:, 4 * hl:4 * hl + 4]))
                        v.wait_ge(sPoT, 3 * ih + 3)
                        if qb >= 2:
                            v.wait_ge(yss, 16 * (2 * (qb - 2) + hl + 1))
                        for qt in range(4):
                            yb = v.tensor_scalar(
                                ybuf[qb % 2][:, 512 * hl + 128 * qt:512 * hl + 128 * (qt + 1)],
                                psTy[hl][:, 128 * qt:128 * (qt + 1)],
                                linv[:, 4 * hl + qt:4 * hl + qt + 1], None, ALU.mult)
                        yb.then_inc(sPoD, 1)
        pb.cnt["l1fM"] = pb.cnt["l1fE"] = pb.cnt["l1fV"] = pb.cnt["l1fA"] = bMEV(16, 0)
        pb.cnt["l1fD"] = 128
        pb.cnt["l1fpr"] = 7 * 32
        pb.cnt["l1fpd"] = pb.cnt["l1fpt"] = 3 * 32
        es2.close()
        nc.leave_named_scope("l1fattn", sc_a2[0], False)
        if dbg is not None and "d_yin" in dbg:
            with nc.Block() as blk:
                @blk.sync
                def _(sy):
                    pb.wait(sy, "l1fys")
                    pb.dma(sy, dbg["d_yin"], ya_in.ap(), name="st")
                    pb.wait(sy, "st")
        with nc.Block() as blk:
            @blk.gpsimd
            def _(g):
                pb.wait(g, "l1fys")
                ccs = pb.sem("cc")
                with allow_ici_all_to_all():
                    g.collective_compute(
                        "AllToAll", ALU.bypass, replica_groups=[list(range(NCORES))],
                        ins=[ya_in.ap().opt()], outs=[YA.ap().opt()]).then_inc(ccs)
                    pb.cnt["cc"] += 1
                g.wait_ge(ccs, pb.cnt["cc"])

        sc_fn = nc.enter_named_scope("l1ffinal", False)
        es3 = ExitStack()
        ytm = pb.sbuf("u_ytm", [128, EMB], BF16, es=es3)
        x3tm = pb.sbuf("u_x3tm", [128, EMB], BF16, es=es3)
        tfin = pb.sbuf("u_tfin", [128, EMB], F32, es=es3)
        sqf = pb.sbuf("u_sqf", [128, EMB], F32, es=es3)
        stf = pb.sbuf("u_stf", [128, 6], F32, es=es3)
        psTz = pb.psum("u_psTz", [128, 128], BF16, es=es3)
        psTz2 = pb.psum("u_psTz2", [128, 128], BF16, es=es3)
        for tc in range(8):
            with nc.Block() as blk:
                @blk.sync
                def _(sy, tc=tc):
                    pb.dma(sy, ytm[:, :].rearrange("p (s m) -> p s m", s=8),
                           YA.ap()[:, :].rearrange("(s q p) m -> p s q m", p=128, q=8)
                           [:, :, tc, :])
                    pb.wait(sy, "ld")
            transpose_seq(pb, [
                (xbuf[:, ct * T + 128 * tc:ct * T + 128 * (tc + 1)],
                 x3tm[:, 128 * ct:128 * (ct + 1)], pb.ident[:, :])
                for ct in range(NE)], [psTz, psTz2])
            with nc.Block() as blk:
                @blk.vector
                def _(v):
                    v.tensor_add(tfin[:, :], x3tm[:, :], ytm[:, :])
            with nc.Block() as blk:
                @blk.vector
                def _(v):
                    pb.guard(v, v.reduce_sum(stf[:, 0:1], tfin[:, :], AX.X))
                    v.tensor_scalar(stf[:, 0:1], stf[:, 0:1], 1.0 / EMB, None, ALU.mult)
                @blk.scalar
                def _(s):
                    s.activation(sqf[:, :], tfin[:, :], AF.Square)
            with nc.Block() as blk:
                @blk.vector
                def _(v):
                    pb.guard(v, v.reduce_sum(stf[:, 1:2], sqf[:, :], AX.X))
                    v.tensor_scalar(stf[:, 1:2], stf[:, 1:2], 1.0 / EMB, None, ALU.mult)
                    v.tensor_mul(stf[:, 2:3], stf[:, 0:1], stf[:, 0:1])
                    v.tensor_sub(stf[:, 1:2], stf[:, 1:2], stf[:, 2:3])
                    v.tensor_scalar(stf[:, 1:2], stf[:, 1:2], 1e-5, None, ALU.add)
            with nc.Block() as blk:
                @blk.scalar
                def _(s):
                    s.activation(stf[:, 3:4], stf[:, 1:2], AF.Sqrt)
            with nc.Block() as blk:
                @blk.vector
                def _(v):
                    v.reciprocal(stf[:, 4:5], stf[:, 3:4])
            with nc.Block() as blk:
                @blk.vector
                def _(v):
                    v.tensor_scalar(tfin[:, :], tfin[:, :], stf[:, 0:1], stf[:, 4:5],
                                    ALU.subtract, ALU.mult)
            with nc.Block() as blk:
                @blk.sync
                def _(sy, tc=tc):
                    pb.dma(sy, out_ap[128 * tc:128 * (tc + 1), :], tfin[:, :], name="st")
                    pb.wait(sy, "st")
        es3.close()
        nc.leave_named_scope("l1ffinal", sc_fn[0], False)


def build_l1ffn(debug=False):
    nc = bass.Bass("TRN2", target_bir_lowering=False)
    pb = PB(nc)
    x3 = nc.dram_tensor("x3fm", [EMB, 1024], BF16, kind="ExternalInput").ap()
    wqkv = nc.dram_tensor("wqkv2_t", [EMB, 6144], F32, kind="ExternalInput").ap()
    wgate = nc.dram_tensor("wgate2_t", [EMB, 16], F32, kind="ExternalInput").ap()
    out = nc.dram_tensor("outrows", [1024, EMB], F32, kind="ExternalOutput").ap()
    dbg = None
    if debug:
        dbg = {}
        for nm_, shp, dt_ in (("d_glog", [128, 128], F32), ("d_ksh", [128, 1024], BF16),
                              ("d_vsh", [128, 1024], BF16), ("d_kshT", [128, 1024], BF16),
                              ("d_qT", [128, 2048], BF16), ("d_knm", [1, 18], F32),
                              ("d_yin", [8 * 1024, 256], BF16),
                              ("d_vfull", [128, 2048], BF16), ("d_kfull", [128, 2048], BF16),
                              ("d_gt1", [128, 16], F32), ("d_ind", [128, 16], F32),
                              ("d_top1", [128, 4], F32)):
            dbg[nm_] = nc.dram_tensor(nm_, shp, dt_, kind="ExternalOutput").ap()
    with pb.es:
        pb.consts()
        _l1ffn_body(pb, x3, wqkv, wgate, out, dbg=dbg)
    return nc


def build_fused(dbg_mid=False, nphases=4):
    nc = bass.Bass("TRN2", target_bir_lowering=False)
    pb = PB(nc)
    ap = lambda *a, **k: nc.dram_tensor(*a, kind="ExternalInput", **k).ap()
    x_lo = ap("x_lo", [515, EMB], F32)
    x_hi = ap("x_hi", [515, EMB], F32)
    wproj = ap("w_inproj_t", [EMB, PROJ], F32)
    cw = ap("conv_w", [128, 192], F32)
    cb = ap("conv_b", [128, 48], F32)
    dtb = ap("dt_bias", [64, 1], F32)
    alog = ap("a_log", [64, 1], F32)
    dpair = ap("d_pair", [128, 32], F32)
    nw = ap("norm_w", [128, 32], F32)
    wout = ap("w_outproj_t", [INTER, EMB], F32)
    lut = ap("lut", [1, 16], F32)
    gidx = ap("gidx_sl", [FFN_I, 256], I32)
    uidx = ap("uidx_sl", [FFN_I, 256], I32)
    didx = ap("didx_sl", [EMB, 688], I32)
    qkv = ap("qkv1", [EMB, 6144], F32)
    oproj = ap("oproj", [EMB, EMB], F32)
    gam = ap("gamma", [128, 16], F32)
    bet = ap("beta", [128, 16], F32)
    allow = ap("allow", [96, 128, 512], BF16)
    wqkv = ap("wqkv2_t", [EMB, 6144], F32)
    wgate = ap("wgate2_t", [EMB, 16], F32)
    out = nc.dram_tensor("outrows", [1024, EMB], F32, kind="ExternalOutput").ap()
    kind = "ExternalOutput" if dbg_mid else "Internal"
    x1 = nc.dram_tensor("x1fm", [EMB, 1024], BF16, kind=kind).ap()
    if dbg_mid:
        nc._dbg_wgt = True
    x2 = nc.dram_tensor("x2fm", [EMB, 1024], BF16, kind=kind).ap()
    x3 = nc.dram_tensor("x3fm", [EMB, 1024], BF16, kind=kind).ap()
    with pb.es:
        pb.consts()
        _l0attn_body(pb, (x_lo, x_hi), wproj, cw, cb, dtb, alog, dpair, nw, wout, x1)
        if nphases >= 2:
            _l0ffn_body(pb, x1, lut, gidx, uidx, didx, x2)
        if nphases >= 3:
            _l1attn_body(pb, None, x2, qkv, oproj, gam, bet, allow, x3)
        if nphases >= 4:
            _l1ffn_body(pb, x3, wqkv, wgate, out)
    return nc


def _prep_host(ins):
    x0 = np.ascontiguousarray(ins["L0_attn_input_states"][0])
    A = {
        "w_inproj_t": np.ascontiguousarray(ins["L0_attn_in_proj_w"].T),
        "conv_w": np.ascontiguousarray(
            ins["L0_attn_conv_w"][:, 0, :].reshape(48, 128, 4).transpose(1, 0, 2)
            .reshape(128, 192)),
        "conv_b": np.ascontiguousarray(ins["L0_attn_conv_b"].reshape(48, 128).T),
        "dt_bias": ins["L0_attn_dt_bias"].reshape(64, 1).copy(),
        "a_log": ins["L0_attn_A_log"].reshape(64, 1).copy(),
        "d_pair": np.ascontiguousarray(
            ins["L0_attn_D"].reshape(32, 2).repeat(64, axis=1).reshape(32, 128).T),
        "norm_w": np.ascontiguousarray(ins["L0_attn_norm_w"].reshape(32, 128).T),
        "w_outproj_t": np.ascontiguousarray(ins["L0_attn_out_proj_w"].T),
    }
    B = {"lut": np.ascontiguousarray(ins["L0_ffn_lut"].reshape(1, 16))}
    C = {
        "qkv1": np.ascontiguousarray(ins["L1_attn_qkv"]),
        "oproj": np.ascontiguousarray(ins["L1_attn_o_proj"]),
        "gamma": np.ascontiguousarray(ins["L1_attn_gamma"].reshape(16, 128).T),
        "beta": np.ascontiguousarray(ins["L1_attn_beta"].reshape(16, 128).T),
    }
    D = {
        "wqkv2_t": np.ascontiguousarray(ins["L1_ffn_W_qkv"].T),
        "wgate2_t": np.ascontiguousarray(ins["L1_ffn_W_gate"].T),
    }
    mapsA, mapsB, mapsC, mapsD = [], [], [], []
    for c in range(NCORES):
        mA = dict(A)
        for key, b in zip(("x_lo", "x_hi"), blocks_of(c)):
            xe = np.zeros((515, EMB), np.float32)
            xe[3:] = x0[512 * b:512 * (b + 1)]
            if b > 0:
                xe[:3] = x0[512 * b - 3:512 * b]
            mA[key] = xe
        mapsA.append(mA)
        mB = dict(B)
        mB["gidx_sl"] = np.ascontiguousarray(ins["L0_ffn_gate_idx"][:, 256 * c:256 * (c + 1)])
        mB["uidx_sl"] = np.ascontiguousarray(ins["L0_ffn_up_idx"][:, 256 * c:256 * (c + 1)])
        mB["didx_sl"] = np.ascontiguousarray(ins["L0_ffn_down_idx"][:, 688 * c:688 * (c + 1)])
        mapsB.append(mB)
        mC = dict(C)
        allow = np.zeros((96, 128, 512), np.float32)
        i = 0
        for half, b in enumerate(blocks_of(c)):
            nslots = 8 if half == 0 else 16
            for qt in range(4):
                qglob = 512 * b + 128 * qt + np.arange(128)
                for j in range(nslots):
                    if j < b:
                        allow[i] = 1.0
                    elif j == b:
                        kglob = 512 * j + np.arange(512)
                        allow[i] = (kglob[None, :] <= qglob[:, None])
                    i += 1
        mC["allow"] = allow.astype(BF)
        mapsC.append(mC)
        mapsD.append(dict(D))
    return mapsA, mapsB, mapsC, mapsD


def kernel(**inputs):
    mapsA, mapsB, mapsC, mapsD = _prep_host(inputs)
    cores = list(range(NCORES))
    maps = [{**mapsA[c], **mapsB[c], **mapsC[c], **mapsD[c]} for c in range(NCORES)]
    res = run_bass_kernel_spmd(build_fused(), maps, core_ids=cores)
    out = np.zeros((1, SEQ, EMB), np.float32)
    for c in range(NCORES):
        rows = res.results[c]["outrows"]
        b0, b1 = blocks_of(c)
        out[0, 512 * b0:512 * (b0 + 1)] = rows[0:512]
        out[0, 512 * b1:512 * (b1 + 1)] = rows[512:1024]
    return out


# version 18 (final)
# speedup vs baseline: 1.0966x; 1.0325x over previous
import math
from contextlib import ExitStack

import numpy as np
import ml_dtypes

import concourse.bass as bass
import concourse.mybir as mybir
from concourse.bass_utils import run_bass_kernel_spmd

F32 = mybir.dt.float32
BF16 = mybir.dt.bfloat16
I32 = mybir.dt.int32
AF = mybir.ActivationFunctionType
ALU = mybir.AluOpType
AX = mybir.AxisListType
BF = ml_dtypes.bfloat16

NCORES = 8
SEQ = 8192
EMB = 2048
NE = EMB // 128
H, PDIM, NST, GRP, KCONV, CS = 64, 64, 128, 8, 4, 256
INTER = H * PDIM
CONV_DIM = INTER + 2 * GRP * NST
PROJ = INTER + CONV_DIM + H
FFN_I = 5504
NFI = 43

def blocks_of(c):
    return (c, 15 - c)

def own(j):
    return j if j <= 7 else 15 - j


class PB:
    def __init__(self, nc):
        self.nc = nc
        self.es = ExitStack()
        self.sems = {}
        self.cnt = {}

    def sem(self, name):
        if name not in self.sems:
            self.sems[name] = self.es.enter_context(self.nc.semaphore(name))
            self.cnt[name] = 0
        return self.sems[name]

    def dma(self, eng, out, in_, name="ld", **kw):
        h = self.sem(name)
        eng.dma_start(out=out, in_=in_, **kw).then_inc(h, 16)
        self.cnt[name] += 16

    def wait(self, eng, name):
        eng.wait_ge(self.sem(name), self.cnt[name])

    def guard(self, eng, instr):
        h = self.sem("gdve")
        instr.then_inc(h, 1)
        self.cnt["gdve"] += 1
        eng.wait_ge(h, self.cnt["gdve"])

    def sbuf(self, name, shape, dtype, es=None):
        return (es or self.es).enter_context(self.nc.sbuf_tensor(name, shape, dtype))

    def psum(self, name, shape, dtype=F32, es=None):
        return (es or self.es).enter_context(self.nc.psum_tensor(name, shape, dtype))

    def consts(self):
        nc = self.nc
        self.ident = self.sbuf("c_ident", [128, 128], BF16)
        self.ident_f = self.sbuf("c_ident_f", [128, 128], F32)
        self.ones_row = self.sbuf("c_ones_row", [1, 128], BF16)
        self.ones_col = self.sbuf("c_ones_col", [128, 1], BF16)
        self.ones_row_f = self.sbuf("c_ones_row_f", [1, 128], F32)
        self.ones_col_f = self.sbuf("c_ones_col_f", [128, 1], F32)
        self.iota_row = self.sbuf("c_iota_row", [128, 512], F32)
        self.iota_col = self.sbuf("c_iota_col", [128, 1], F32)
        with nc.Block() as blk:
            @blk.gpsimd
            def _(g):
                g.iota(self.iota_row[:, :], [[1, 512]], channel_multiplier=0,
                       allow_small_or_imprecise_dtypes=True)
                g.iota(self.iota_col[:, :], [[0, 1]], channel_multiplier=1,
                       allow_small_or_imprecise_dtypes=True)
        with nc.Block() as blk:
            @blk.vector
            def _(v):
                v.tensor_scalar(self.ident[:, :], self.iota_row[:, 0:128],
                                self.iota_col[:, 0:1], None, ALU.is_equal)
                v.tensor_scalar(self.ident_f[:, :], self.iota_row[:, 0:128],
                                self.iota_col[:, 0:1], None, ALU.is_equal)
                v.memset(self.ones_row[:, :], 1.0)
                v.memset(self.ones_col[:, :], 1.0)
                v.memset(self.ones_row_f[:, :], 1.0)
                v.memset(self.ones_col_f[:, :], 1.0)




def pipe_mm(pb, tag, ng, dma_emit, mm_emit, evac_emit, double_psum=True,
            act_emit=None, dma_eng="gpsimd"):
    nc = pb.nc
    sMM = pb.sem(tag + "m")
    sEv = pb.sem(tag + "e")
    sDn = pb.sem(tag + "d")
    bM = pb.cnt[tag + "m"]
    bE = pb.cnt[tag + "e"]
    dma_counts = []
    with nc.Block() as blk:
        def _dma_body(sy):
            for g in range(ng):
                if g >= 2:
                    sy.wait_ge(sMM, bM + g - 1)
                dma_emit(sy, g)
                dma_counts.append(pb.cnt[tag + "d"])
        if dma_eng == "gpsimd":
            blk.gpsimd(_dma_body)
        else:
            blk.sync(_dma_body)
        @blk.tensor
        def _(t):
            for g in range(ng):
                t.wait_ge(sDn, dma_counts[g])
                lag = 2 if double_psum else 1
                if g >= lag:
                    t.wait_ge(sEv, bE + g - lag + 1)
                mm_emit(t, g).then_inc(sMM, 1)
        @blk.vector
        def _(v):
            for g in range(ng):
                v.wait_ge(sMM, bM + g + 1)
                evac_emit(v, g).then_inc(sEv, 1)
        if act_emit is not None:
            @blk.scalar
            def _(sc):
                for g in range(ng):
                    sc.wait_ge(sEv, bE + g + 1)
                    act_emit(sc, g)
    pb.cnt[tag + "m"] += ng
    pb.cnt[tag + "e"] += ng


def transpose_seq(pb, items, ps2):
    nc = pb.nc
    sT = pb.sem("trT")
    sE = pb.sem("trE")
    bT = pb.cnt["trT"]
    bE = pb.cnt["trE"]
    n = len(items)
    with nc.Block() as blk:
        @blk.tensor
        def _(t):
            for i, (src, dst, idn) in enumerate(items):
                if i >= 2:
                    t.wait_ge(sE, bE + i - 1)
                pp, ff = src.shape
                t.transpose(ps2[i % 2][0:ff, 0:pp], src, idn).then_inc(sT, 1)
        @blk.vector
        def _(v):
            for i, (src, dst, idn) in enumerate(items):
                v.wait_ge(sT, bT + i + 1)
                pp, ff = src.shape
                v.tensor_copy(dst, ps2[i % 2][0:ff, 0:pp]).then_inc(sE, 1)
    pb.cnt["trT"] += n
    pb.cnt["trE"] += n


def ln_fm(pb, get_tile, nt, T, out_tile, gamma=None, beta=None, eps=1e-5,
          pre_scale=1.0, tag=""):
    nc = pb.nc
    nth = T // 512
    es = ExitStack()
    with es:
        sum_ps = [pb.psum(f"ln{tag}_s{i}", [1, 512], F32, es=es) for i in range(nth)]
        sq_ps = [pb.psum(f"ln{tag}_q{i}", [1, 512], F32, es=es) for i in range(nth)]
        sq = pb.sbuf(f"ln{tag}_sq", [128, T], F32, es=es)
        sqb = [pb.sbuf(f"ln{tag}_sqb{i}", [128, T], BF16, es=es) for i in range(2)]
        ab = pb.sbuf(f"ln{tag}_ab", [1, 2 * T], F32, es=es)
        abc = pb.sbuf(f"ln{tag}_abc", [128, 2 * T], F32, es=es)
        pbc = [pb.psum(f"ln{tag}_pbc{i}", [128, 512], F32, es=es) for i in range(2)]
        sQ = pb.sem("lnQ")
        sC = pb.sem("lnC")
        bQ = pb.cnt["lnQ"]
        bC = pb.cnt["lnC"]
        with nc.Block() as blk:
            @blk.scalar
            def _(s):
                for ct in range(NE):
                    if ct >= 2:
                        s.wait_ge(sC, bC + ct - 1)
                    s.activation(sqb[ct % 2][:, :], get_tile(ct), AF.Square
                                 ).then_inc(sQ, 1)
            @blk.tensor
            def _(t):
                for ct in range(NE):
                    for th in range(nth):
                        t.matmul(sum_ps[th][:, :], pb.ones_col[:, :],
                                 get_tile(ct)[:, 512 * th:512 * (th + 1)],
                                 start=(ct == 0), stop=(ct == NE - 1))
                    t.wait_ge(sQ, bQ + ct + 1)
                    for th in range(nth):
                        mm = t.matmul(sq_ps[th][:, :], pb.ones_col[:, :],
                                      sqb[ct % 2][:, 512 * th:512 * (th + 1)],
                                      start=(ct == 0), stop=(ct == NE - 1))
                        if th == nth - 1:
                            mm.then_inc(sC, 1)
        pb.cnt["lnQ"] += NE
        pb.cnt["lnC"] += NE
        mean = ab[0:1, 0:T]
        avar = ab[0:1, T:2 * T]
        sG = pb.sem("lnG")
        bG = pb.cnt["lnG"]
        bQ2 = pb.cnt["lnQ"]
        bC2 = pb.cnt["lnC"]
        nbc = 2 * nth
        with nc.Block() as blk:
            @blk.vector
            def _(v):
                for th in range(nth):
                    v.tensor_scalar(ab[0:1, 512 * th:512 * (th + 1)], sum_ps[th][:, :],
                                    1.0 / EMB, None, ALU.mult)
                    v.tensor_scalar(ab[0:1, T + 512 * th:T + 512 * (th + 1)],
                                    sq_ps[th][:, :], 1.0 / EMB, None, ALU.mult)
                v.tensor_mul(sq[0:1, 0:T], mean, mean)
                v.tensor_sub(avar, avar, sq[0:1, 0:T])
                v.tensor_scalar(avar, avar, float(eps), None, ALU.add
                                ).then_inc(sG, 1)
                v.wait_ge(sG, bG + 2)
                pb.guard(v, v.reciprocal(avar, avar))
                v.tensor_mul(mean, mean, avar)
                pb.guard(v, v.tensor_scalar(mean, mean, -1.0, None, ALU.mult))
                v.engine_nop().then_inc(sG, 1)
                for i in range(nbc):
                    v.wait_ge(sQ, bQ2 + i + 1)
                    half, th = divmod(i, nth)
                    v.tensor_copy(abc[:, half * T + 512 * th:half * T + 512 * (th + 1)],
                                  pbc[i % 2][:, :]).then_inc(sC, 1)
            @blk.scalar
            def _(s):
                s.wait_ge(sG, bG + 1)
                s.activation(avar, avar, AF.Sqrt).then_inc(sG, 1)
            @blk.tensor
            def _(t):
                t.wait_ge(sG, bG + 3)
                for i in range(nbc):
                    half, th = divmod(i, nth)
                    if i >= 2:
                        t.wait_ge(sC, bC2 + i - 1)
                    src = (1 - half) * T
                    t.matmul(pbc[i % 2][:, :], pb.ones_row_f[:, :],
                             ab[0:1, src + 512 * th:src + 512 * (th + 1)],
                             start=True, stop=True).then_inc(sQ, 1)
        pb.cnt["lnG"] = bG + 3
        pb.cnt["lnQ"] += nbc
        pb.cnt["lnC"] += nbc
        with nc.Block() as blk:
            @blk.vector
            def _(v):
                for ct in range(NE):
                    v.tensor_copy(sq[:, :], get_tile(ct))
                    v.tensor_mul(sq[:, :], sq[:, :], abc[:, 0:T])
                    v.tensor_add(sq[:, :], sq[:, :], abc[:, T:2 * T])
                    if gamma is not None:
                        v.tensor_scalar(sq[:, :], sq[:, :], gamma[:, ct:ct + 1],
                                        None, ALU.mult)
                    if beta is not None:
                        v.tensor_scalar(sq[:, :], sq[:, :], beta[:, ct:ct + 1],
                                        None, ALU.add)
                    v.tensor_copy(out_tile(ct), sq[:, :])


def _l0ffn_body(pb, x1_ap, lut_ap, gidx_ap, uidx_ap, didx_ap, x2_ap):
    nc = pb.nc
    T = 1024
    wg_sl = nc.dram_tensor("wg_sl", [256, FFN_I], BF16)
    wu_sl = nc.dram_tensor("wu_sl", [256, FFN_I], BF16)
    wd_sl = nc.dram_tensor("wd_sl", [688, EMB], BF16)
    WgT = nc.dram_tensor("WgT", [EMB, FFN_I], BF16, addr_space="Shared")
    WuT = nc.dram_tensor("WuT", [EMB, FFN_I], BF16, addr_space="Shared")
    WdT = nc.dram_tensor("WdT", [FFN_I, EMB], BF16, addr_space="Shared")
    if getattr(nc, "_dbg_wgt", False):
        dWg = nc.dram_tensor("dWgT", [256, FFN_I], BF16, kind="ExternalOutput")
        dWu = nc.dram_tensor("dWuT", [256, FFN_I], BF16, kind="ExternalOutput")
        dWd = nc.dram_tensor("dWdT", [688, EMB], BF16, kind="ExternalOutput")

    sc_gw = nc.enter_named_scope("ghostw", False)
    es2 = ExitStack()
    with es2:
        lutc = pb.sbuf("lutc", [128, 16], F32, es=es2)
        lutrow = pb.sbuf("lutrow", [1, 16], F32, es=es2)
        p_bc16 = pb.psum("p_bc16", [128, 16], F32, es=es2)
        with nc.Block() as blk:
            @blk.sync
            def _(sy):
                pb.dma(sy, lutrow[:, :], lut_ap)
                pb.wait(sy, "ld")
        with nc.Block() as blk:
            @blk.tensor
            def _(t):
                t.matmul(p_bc16[:, :], pb.ones_row_f[:, :], lutrow[:, :], start=True, stop=True)
        with nc.Block() as blk:
            @blk.vector
            def _(v):
                v.tensor_copy(lutc[:, :], p_bc16[:, :])

        idxs = pb.sbuf("g_idxs", [128, 768], I32, es=es2)
        idxb = pb.sbuf("g_idxb", [128, 768], BF16, es=es2)
        acc = pb.sbuf("g_acc", [128, 768], BF16, es=es2)
        tmp = pb.sbuf("g_tmp", [128, 768], BF16, es=es2)
        p_tr = pb.psum("g_ptr", [128, 128], BF16, es=es2)
        stA = pb.sbuf("g_stA", [128, FFN_I], BF16, es=es2)
        stB = pb.sbuf("g_stB", [128, FFN_I], BF16, es=es2)
        stD = [pb.sbuf(f"g_stD{j}", [128 if j < 5 else 48, EMB], BF16, es=es2) for j in range(6)]

        def convert_chunk(width, idx_src, out_tiles):
            with nc.Block() as blk:
                @blk.sync
                def _(sy):
                    pb.dma(sy, idxs[:, 0:width], idx_src)
                @blk.vector
                def _(v):
                    pb.wait(v, "ld")
                    v.tensor_copy(idxb[:, 0:width], idxs[:, 0:width])
                    v.memset(acc[:, 0:width], 0.0)
                    for k in range(16):
                        v.tensor_scalar(tmp[:, 0:width], idxb[:, 0:width],
                                        float(k), lutc[:, k:k + 1], ALU.is_equal, ALU.mult)
                        v.tensor_add(acc[:, 0:width], acc[:, 0:width], tmp[:, 0:width])
            ncol = (width + 127) // 128
            for j in range(ncol):
                w = min(128, width - 128 * j)
                with nc.Block() as blk:
                    @blk.tensor
                    def _(t, j=j, w=w):
                        t.transpose(p_tr[0:w, 0:128], acc[:, 128 * j:128 * j + w],
                                    pb.ident[:, :])
                with nc.Block() as blk:
                    @blk.vector
                    def _(v, j=j, w=w):
                        v.tensor_copy(out_tiles[j][0:w, :], p_tr[0:w, 0:128])

        for idx_ap, dst in ((gidx_ap, wg_sl), (uidx_ap, wu_sl)):
            for i in range(NFI):
                convert_chunk(256, idx_ap[128 * i:128 * (i + 1), :],
                              [stA[:, 128 * i:128 * (i + 1)],
                               stB[:, 128 * i:128 * (i + 1)]])
            with nc.Block() as blk:
                @blk.sync
                def _(sy, dst=dst):
                    pb.dma(sy, dst.ap()[0:128, :], stA[:, :])
                    pb.dma(sy, dst.ap()[128:256, :], stB[:, :])
                    pb.wait(sy, "ld")
        for i in range(NE):
            convert_chunk(688, didx_ap[128 * i:128 * (i + 1), :],
                          [stD[j][:, 128 * i:128 * (i + 1)] for j in range(6)])
        with nc.Block() as blk:
            @blk.sync
            def _(sy):
                for j in range(6):
                    hh = 128 if j < 5 else 48
                    pb.dma(sy, wd_sl.ap()[128 * j:128 * j + hh, :], stD[j][0:hh, :])
                pb.wait(sy, "ld")
        with nc.Block() as blk:
            @blk.gpsimd
            def _(g):
                ccs = pb.sem("cc")
                for src, dsts in ((wg_sl, WgT), (wu_sl, WuT), (wd_sl, WdT)):
                    g.collective_compute(
                        "AllGather", ALU.bypass, replica_groups=[list(range(NCORES))],
                        ins=[src.ap().opt()], outs=[dsts.ap().opt()]).then_inc(ccs)
                    pb.cnt["cc"] += 1
                g.wait_ge(ccs, pb.cnt["cc"])

    nc.leave_named_scope("ghostw", sc_gw[0], False)
    sc_fm = nc.enter_named_scope("ffnmm", False)
    es3 = ExitStack()
    with es3:
        xfm = pb.sbuf("f_xfm", [128, NE * T], BF16, es=es3)
        sgt = pb.sbuf("f_sg", [128, 4 * 512], BF16, es=es3)
        es_h = ExitStack()
        es_w = ExitStack()
        hbuf = pb.sbuf("f_h", [128, NFI * 1024], BF16, es=es_h)
        wband_g = pb.sbuf("f_wg", [128, NE * 256], BF16, es=es_w)
        wband_u = pb.sbuf("f_wu", [128, NE * 256], BF16, es=es_w)
        with nc.Block() as blk:
            @blk.sync
            def _(sy):
                pb.dma(sy, xfm[:, :].rearrange("p (ct t) -> p ct t", ct=NE),
                       x1_ap.rearrange("(ct p) t -> p ct t", p=128))
                pb.wait(sy, "ld")
        es_gu = ExitStack()
        pg = [pb.psum(f"f_pg{i}", [128, 512], F32, es=es_gu) for i in range(4)]
        pu = [pb.psum(f"f_pu{i}", [128, 512], F32, es=es_gu) for i in range(4)]
        def gu_dma(sy, g):
            off = (g % 2) * NE * 128
            for q4 in range(4):
                pb.dma(sy, wband_g[:, off + q4 * 512:off + (q4 + 1) * 512]
                       .rearrange("p (kt m) -> p kt m", kt=4),
                       WgT.ap()[512 * q4:512 * (q4 + 1), 128 * g:128 * (g + 1)]
                       .rearrange("(kt p) m -> p kt m", p=128), name="gud")
                pb.dma(sy, wband_u[:, off + q4 * 512:off + (q4 + 1) * 512]
                       .rearrange("p (kt m) -> p kt m", kt=4),
                       WuT.ap()[512 * q4:512 * (q4 + 1), 128 * g:128 * (g + 1)]
                       .rearrange("(kt p) m -> p kt m", p=128), name="gud")

        def gu_mm(t, g):
            off = (g % 2) * NE * 128
            pgs = pg[2 * (g % 2):2 * (g % 2) + 2]
            pus = pu[2 * (g % 2):2 * (g % 2) + 2]
            for th in range(2):
                for kt in range(NE):
                    xsl = xfm[:, kt * T + 512 * th:kt * T + 512 * (th + 1)]
                    t.matmul(pgs[th][:, :], wband_g[:, off + kt * 128:off + (kt + 1) * 128],
                             xsl, start=(kt == 0), stop=(kt == NE - 1))
                    mm = t.matmul(pus[th][:, :], wband_u[:, off + kt * 128:off + (kt + 1) * 128],
                                  xsl, start=(kt == 0), stop=(kt == NE - 1))
            return mm

        def gu_evac(v, g):
            pus = pu[2 * (g % 2):2 * (g % 2) + 2]
            for th in range(2):
                cp = v.tensor_mul(
                    hbuf[:, g * T + 512 * th:g * T + 512 * (th + 1)],
                    sgt[:, (g % 2) * 1024 + 512 * th:(g % 2) * 1024 + 512 * (th + 1)],
                    pus[th][:, :])
            return cp

        def gu_act(sc, g):
            pgs = pg[2 * (g % 2):2 * (g % 2) + 2]
            for th in range(2):
                sc.activation(sgt[:, (g % 2) * 1024 + 512 * th:(g % 2) * 1024 + 512 * (th + 1)],
                              pgs[th][:, :], AF.Silu)
        sGU = pb.sem("gus")
        bGU = pb.cnt["gus"]
        sGM = pb.sem("gum")
        bGM = pb.cnt["gum"]
        sGE = pb.sem("gue")
        bGE = pb.cnt["gue"]
        dcnt = []
        with nc.Block() as blk:
            @blk.sync
            def _(sy):
                for g in range(NFI):
                    if g >= 2:
                        sy.wait_ge(sGM, bGM + g - 1)
                    gu_dma(sy, g)
                    dcnt.append(pb.cnt["gud"])
            @blk.tensor
            def _(t):
                for g in range(NFI):
                    t.wait_ge(pb.sem("gud"), dcnt[g])
                    if g >= 2:
                        t.wait_ge(sGE, bGE + g - 1)
                    gu_mm(t, g).then_inc(sGM, 1)
            @blk.scalar
            def _(sc):
                for g in range(NFI):
                    sc.wait_ge(sGM, bGM + g + 1)
                    if g >= 2:
                        sc.wait_ge(sGE, bGE + g - 1)
                    for th in range(2):
                        aa = sc.activation(
                            sgt[:, (g % 2) * 1024 + 512 * th:(g % 2) * 1024 + 512 * (th + 1)],
                            pg[2 * (g % 2) + th][:, :], AF.Silu)
                    aa.then_inc(sGU, 1)
            @blk.vector
            def _(v):
                for g in range(NFI):
                    v.wait_ge(sGU, bGU + g + 1)
                    gu_evac(v, g).then_inc(sGE, 1)
        pb.cnt["gus"] += NFI
        pb.cnt["gum"] += NFI
        pb.cnt["gue"] += NFI
        es_gu.close()
        es_w.close()
        es_d = ExitStack()
        wband_d = pb.sbuf("f_wd", [128, 2 * NFI * 256], BF16, es=es_d)
        pd = [pb.psum(f"f_pd{i}", [128, 512], F32, es=es_d) for i in range(8)]
        tfm = xfm
        def dn_dma(sy, g):
            off = (g % 2) * NFI * 256
            for q6 in range(6):
                r0 = 1024 * q6
                rr = min(1024, FFN_I - r0)
                pb.dma(sy, wband_d[:, off + (r0 // 128) * 256:off + ((r0 + rr) // 128) * 256]
                       .rearrange("p (kt m) -> p kt m", m=256),
                       WdT.ap()[r0:r0 + rr, 256 * g:256 * (g + 1)]
                       .rearrange("(kt p) m -> p kt m", p=128), name="dnd")

        def dn_mm(t, g):
            off = (g % 2) * NFI * 256
            ps = pd[4 * (g % 2):4 * (g % 2) + 4]
            for mi in range(2):
                for th in range(2):
                    for kt in range(NFI):
                        wo = off + kt * 256 + mi * 128
                        mm = t.matmul(ps[2 * mi + th][:, :], wband_d[:, wo:wo + 128],
                                      hbuf[:, kt * T + 512 * th:kt * T + 512 * (th + 1)],
                                      start=(kt == 0), stop=(kt == NFI - 1))
            return mm

        def dn_evac(v, g):
            ps = pd[4 * (g % 2):4 * (g % 2) + 4]
            for mi in range(2):
                for th in range(2):
                    sl = slice((2 * g + mi) * T + 512 * th, (2 * g + mi) * T + 512 * (th + 1))
                    cp = v.tensor_add(tfm[:, sl], tfm[:, sl], ps[2 * mi + th][:, :])
            return cp
        pipe_mm(pb, "dn", 8, dn_dma, dn_mm, dn_evac, dma_eng="sync")
        es_d.close()
        es_h.close()
        nc.leave_named_scope("ffnmm", sc_fm[0], False)
        ln_fm(pb, lambda ct: tfm[:, ct * T:(ct + 1) * T], NE, T,
              lambda ct: tfm[:, ct * T:(ct + 1) * T], tag="f")
        with nc.Block() as blk:
            @blk.sync
            def _(sy):
                pb.dma(sy, x2_ap.rearrange("(ct p) t -> p ct t", p=128),
                       tfm[:, :].rearrange("p (ct t) -> p ct t", ct=NE), name="st")
                pb.wait(sy, "st")


def build_l0ffn():
    nc = bass.Bass("TRN2", target_bir_lowering=False)
    pb = PB(nc)
    x1 = nc.dram_tensor("x1fm", [EMB, 1024], BF16, kind="ExternalInput").ap()
    lut = nc.dram_tensor("lut", [1, 16], F32, kind="ExternalInput").ap()
    gidx = nc.dram_tensor("gidx_sl", [FFN_I, 256], I32, kind="ExternalInput").ap()
    uidx = nc.dram_tensor("uidx_sl", [FFN_I, 256], I32, kind="ExternalInput").ap()
    didx = nc.dram_tensor("didx_sl", [EMB, 688], I32, kind="ExternalInput").ap()
    x2 = nc.dram_tensor("x2fm", [EMB, 1024], BF16, kind="ExternalOutput").ap()
    with pb.es:
        pb.consts()
        _l0ffn_body(pb, x1, lut, gidx, uidx, didx, x2)
    return nc



def _l0attn_body(pb, x_aps, wproj_ap, cw_ap, cb_ap, dtb_ap, alog_ap, dpair_ap,
                 nw_ap, wout_ap, x1_ap, dbg=None):
    nc = pb.nc
    esP = ExitStack()

    def dump(name, src_ap):
        if dbg is None or name not in dbg:
            return
        with nc.Block() as blk:
            @blk.sync
            def _(sy):
                pb.dma(sy, dbg[name], src_ap, name="st")
                pb.wait(sy, "st")
    with esP:
        cw = pb.sbuf("a_cw", [128, 192], F32, es=esP)
        cb = pb.sbuf("a_cb", [128, 48], F32, es=esP)
        dtb = pb.sbuf("a_dtb", [64, 1], F32, es=esP)
        alog = pb.sbuf("a_alog", [64, 1], F32, es=esP)
        acol = pb.sbuf("a_acol", [64, 1], F32, es=esP)
        dpair = pb.sbuf("a_dpair", [128, 32], F32, es=esP)
        nw = pb.sbuf("a_nw", [128, 32], F32, es=esP)
        tri_w = pb.sbuf("a_tri", [128, 512], BF16, es=esP)
        ones64 = pb.sbuf("a_ones64", [64, 256], F32, es=esP)
        with nc.Block() as blk:
            @blk.sync
            def _(sy):
                pb.dma(sy, cw[:, :], cw_ap)
                pb.dma(sy, cb[:, :], cb_ap)
                pb.dma(sy, dtb[:, :], dtb_ap)
                pb.dma(sy, alog[:, :], alog_ap)
                pb.dma(sy, dpair[:, :], dpair_ap)
                pb.dma(sy, nw[:, :], nw_ap)
                pb.wait(sy, "ld")
        with nc.Block() as blk:
            @blk.scalar
            def _(s):
                s.activation(acol[:, :], alog[:, :], AF.Exp)
            @blk.vector
            def _(v):
                v.tensor_scalar(tri_w[:, 0:256], pb.iota_row[:, 0:256],
                                pb.iota_col[:, 0:1], None, ALU.is_ge)
                v.tensor_scalar(tri_w[:, 256:512], pb.iota_row[:, 0:256],
                                pb.iota_col[:, 0:1], 128.0, ALU.subtract, ALU.is_ge)
                v.memset(ones64[:, :], 1.0)
        with nc.Block() as blk:
            @blk.vector
            def _(v):
                v.tensor_scalar(acol[:, :], acol[:, :], -1.0, None, ALU.mult)
        ps512 = [pb.psum(f"a_ps512_{i}", [128, 512], F32, es=esP) for i in range(2)]
        acum_dram = nc.dram_tensor("a_acum_dram", [64, 256], F32)

        for half in range(2):
            x_ap = x_aps[half]
            esH = ExitStack()
            xfm = pb.sbuf(f"h{half}_xfm", [128, NE * 515], BF16, es=esH)
            gatebuf = pb.sbuf(f"h{half}_gate", [128, 32 * 512], BF16, es=esH)
            hidfm = pb.sbuf(f"h{half}_hid", [128, 32 * 512], BF16, es=esH)
            bfm = pb.sbuf(f"h{half}_bfm", [128, 8 * 512], BF16, es=esH)
            cfm = pb.sbuf(f"h{half}_cfm", [128, 8 * 512], BF16, es=esH)
            dtbuf = pb.sbuf(f"h{half}_dt", [64, 512], F32, es=esH)
            dt2buf = pb.sbuf(f"h{half}_dt2", [64, 512], F32, es=esH)
            scr64 = pb.sbuf(f"h{half}_scr64", [64, 512], F32, es=esH)
            x1buf = pb.sbuf(f"h{half}_x1", [128, NE * 512], BF16, es=esH)

            sc_xT = nc.enter_named_scope(f"xT{half}", False)
            esX = ExitStack()
            xrow = [pb.sbuf(f"h{half}_xrow{i}", [128, 2048], BF16, es=esX) for i in range(2)]
            psTx = [pb.psum(f"h{half}_psTx{i}", [128, 128], BF16, es=esX) for i in range(2)]
            for rt in range(5):
                r0 = 128 * rt
                rr = min(128, 515 - r0)
                with nc.Block() as blk:
                    @blk.gpsimd
                    def _(g, r0=r0, rr=rr, rt=rt):
                        pb.dma(g, xrow[rt % 2][0:rr, :].rearrange("p (ct f) -> p ct f", ct=NE),
                               x_ap[r0:r0 + rr, :].rearrange("p (ct f) -> p ct f", ct=NE))
                        pb.wait(g, "ld")
                transpose_seq(pb, [
                    (xrow[rt % 2][0:rr, 128 * ct:128 * (ct + 1)],
                     xfm[:, 515 * ct + r0:515 * ct + r0 + rr],
                     pb.ident[0:rr, 0:rr]) for ct in range(NE)], psTx)
            esX.close()
            nc.leave_named_scope(f"xT{half}", sc_xT[0], False)

            sc_ip = nc.enter_named_scope(f"inproj{half}", False)
            esW = ExitStack()
            wband = pb.sbuf(f"h{half}_wband", [128, NE * 512], BF16, es=esW)
            hbst = pb.sbuf(f"h{half}_hbst", [128, 2 * 515], BF16, es=esW)
            convtap = pb.sbuf(f"h{half}_ctap", [128, 512], F32, es=esW)
            convacc = pb.sbuf(f"h{half}_cacc", [128, 512], F32, es=esW)
            psH = pb.psum(f"h{half}_psH", [128, 16], F32, es=esW)
            psD = pb.psum(f"h{half}_psD", [64, 512], F32, es=esW)

            def inproj_group(c0, dst_write, is_hbc):
                nm = 2
                with nc.Block() as blk:
                    @blk.gpsimd
                    def _(g, c0=c0):
                        for q4 in range(4):
                            pb.dma(g, wband[:, q4 * 4 * 128 * nm:(q4 + 1) * 4 * 128 * nm]
                                   .rearrange("p (kt m) -> p kt m", kt=4),
                                   wproj_ap[512 * q4:512 * (q4 + 1), c0:c0 + 128 * nm]
                                   .rearrange("(kt p) m -> p kt m", p=128))
                        pb.wait(g, "ld")
                with nc.Block() as blk:
                    @blk.tensor
                    def _(t, c0=c0, is_hbc=is_hbc):
                        for mi in range(nm):
                            for kt in range(NE):
                                off = kt * 128 * nm + mi * 128
                                if is_hbc:
                                    t.matmul(ps512[mi][:, :], wband[:, off:off + 128],
                                             xfm[:, 515 * kt:515 * kt + 512],
                                             start=(kt == 0), stop=(kt == NE - 1))
                                    t.matmul(psH[:, 4 * mi:4 * mi + 3], wband[:, off:off + 128],
                                             xfm[:, 515 * kt + 512:515 * kt + 515],
                                             start=(kt == 0), stop=(kt == NE - 1))
                                else:
                                    t.matmul(ps512[mi][:, :], wband[:, off:off + 128],
                                             xfm[:, 515 * kt + 3:515 * kt + 515],
                                             start=(kt == 0), stop=(kt == NE - 1))
                with nc.Block() as blk:
                    @blk.vector
                    def _(v):
                        dst_write(v)

            for g2 in range(16):
                def wr(v, g2=g2):
                    for mi in range(2):
                        v.tensor_copy(gatebuf[:, (2 * g2 + mi) * 512:(2 * g2 + mi + 1) * 512],
                                      ps512[mi][:, :])
                inproj_group(256 * g2, wr, False)
            for g2 in range(24):
                def wr(v, g2=g2):
                    for mi in range(2):
                        v.tensor_copy(hbst[:, mi * 515:mi * 515 + 512], ps512[mi][:, :])
                        v.tensor_copy(hbst[:, mi * 515 + 512:mi * 515 + 515],
                                      psH[:, 4 * mi:4 * mi + 3])
                inproj_group(4096 + 256 * g2, wr, True)
                for mi in range(2):
                    j = 2 * g2 + mi
                    with nc.Block() as blk:
                        @blk.vector
                        def _(v, mi=mi, j=j):
                            iw = hbst[:, mi * 515:(mi + 1) * 515]
                            v.tensor_scalar(convacc[:, :], iw[:, 0:512],
                                            cw[:, 4 * j:4 * j + 1], None, ALU.mult)
                            for kk in range(1, 4):
                                v.tensor_scalar(convtap[:, :], iw[:, kk:kk + 512],
                                                cw[:, 4 * j + kk:4 * j + kk + 1],
                                                None, ALU.mult)
                                v.tensor_add(convacc[:, :], convacc[:, :], convtap[:, :])
                    with nc.Block() as blk:
                        @blk.scalar
                        def _(s, j=j):
                            if j < 32:
                                dst = hidfm[:, 512 * j:512 * (j + 1)]
                            elif j < 40:
                                dst = bfm[:, 512 * (j - 32):512 * (j - 31)]
                            else:
                                dst = cfm[:, 512 * (j - 40):512 * (j - 39)]
                            s.activation(dst, convacc[:, :], AF.Silu, bias=cb[:, j:j + 1])
            with nc.Block() as blk:
                @blk.gpsimd
                def _(g):
                    pb.dma(g, wband[:, 0:NE * 64].rearrange("p (kt m) -> p kt m", kt=NE),
                           wproj_ap[:, 10240:10304].rearrange("(kt p) m -> p kt m", p=128))
                    pb.wait(g, "ld")
            with nc.Block() as blk:
                @blk.tensor
                def _(t):
                    for kt in range(NE):
                        t.matmul(psD[0:64, 0:256], wband[:, kt * 64:kt * 64 + 64],
                                 xfm[:, 515 * kt + 3:515 * kt + 259],
                                 start=(kt == 0), stop=(kt == NE - 1))
                    for kt in range(NE):
                        t.matmul(psD[0:64, 256:512], wband[:, kt * 64:kt * 64 + 64],
                                 xfm[:, 515 * kt + 259:515 * kt + 515],
                                 start=(kt == 0), stop=(kt == NE - 1))
            with nc.Block() as blk:
                @blk.vector
                def _(v):
                    v.tensor_copy(dtbuf[:, :], psD[0:64, :])
            esW.close()
            nc.leave_named_scope(f"inproj{half}", sc_ip[0], False)

            if half == 0:
                dump("d_gate", gatebuf[:, :])
                dump("d_hid", hidfm[:, :])
                dump("d_bfm", bfm[:, :])
                dump("d_cfm", cfm[:, :])
                dump("d_dt", dtbuf[:, :])
            for ct in range(32):
                with nc.Block() as blk:
                    @blk.scalar
                    def _(s, ct=ct):
                        s.activation(gatebuf[:, 512 * ct:512 * (ct + 1)],
                                     gatebuf[:, 512 * ct:512 * (ct + 1)], AF.Silu)

            with nc.Block() as blk:
                @blk.vector
                def _(v):
                    v.tensor_scalar(dtbuf[:, :], dtbuf[:, :], dtb[:, 0:1], None, ALU.add)
            with nc.Block() as blk:
                @blk.scalar
                def _(s):
                    s.activation(scr64[:, :], dtbuf[:, :], AF.Abs)
            with nc.Block() as blk:
                @blk.scalar
                def _(s):
                    s.activation(scr64[:, :], scr64[:, :], AF.Exp, scale=-1.0)
            with nc.Block() as blk:
                @blk.scalar
                def _(s):
                    s.activation(scr64[:, :], scr64[:, :], AF.Ln, bias=1.0)
                @blk.vector
                def _(v):
                    v.tensor_relu(dt2buf[:, :], dtbuf[:, :])
            with nc.Block() as blk:
                @blk.vector
                def _(v):
                    v.tensor_add(dt2buf[:, :], dt2buf[:, :], scr64[:, :])
                    v.tensor_scalar(dtbuf[:, :], dt2buf[:, :], acol[:, 0:1],
                                    None, ALU.mult)

            if half == 0:
                dump("d_dt2", dt2buf[:, :])
                dump("d_adt", dtbuf[:, :])
            sc_ssd = nc.enter_named_scope(f"ssd{half}", False)
            esM = ExitStack()
            hdt = [pb.sbuf(f"h{half}_hdt{k}", [128, 4096], BF16, es=esM) for k in range(2)]
            gmtbuf = pb.sbuf(f"h{half}_gmt", [128, 8 * 512], BF16, es=esM)
            acumfm = pb.sbuf(f"h{half}_acfm", [64, 256], F32, es=esM)
            acumtm = pb.sbuf(f"h{half}_actm", [128, 128], F32, es=esM)
            acumtm2 = pb.sbuf(f"h{half}_actm2", [128, 128], F32, es=esM)
            dt2tm = pb.sbuf(f"h{half}_dt2tm", [128, 128], F32, es=esM)
            acr8 = [pb.sbuf(f"h{half}_acr8_{i}", [1, 2048], F32, es=esM) for i in range(2)]
            a255row = pb.sbuf(f"h{half}_a255", [1, 64], F32, es=esM)
            s1buf = [pb.sbuf(f"h{half}_s1_{i}", [128, 512], BF16, es=esM) for i in range(2)]
            t1buf = [pb.sbuf(f"h{half}_t1_{i}", [128, 512], BF16, es=esM) for i in range(2)]
            t2buf = [pb.sbuf(f"h{half}_t2_{i}", [128, 512], BF16, es=esM) for i in range(2)]
            mbuf = [pb.sbuf(f"h{half}_m_{i}", [128, 512], BF16, es=esM) for i in range(2)]
            ev64 = pb.sbuf(f"h{half}_ev64", [128, 512], F32, es=esM)
            for c2 in range(2):
                esS = ExitStack()
                psT2 = pb.psum(f"h{half}_psT_{c2}", [128, 128], BF16, es=esS)
                esSf = ExitStack()
                psTf = pb.psum(f"h{half}_psTf_{c2}", [128, 128], F32, es=esSf)
                with nc.Block() as blk:
                    @blk.vector
                    def _(v, c2=c2):
                        v.tensor_tensor_scan(acumfm[:, :], ones64[:, 0:256],
                                             dtbuf[:, 256 * c2:256 * (c2 + 1)],
                                             0.0, ALU.mult, ALU.add)
                for kc in range(2):
                    with nc.Block() as blk:
                        @blk.tensor
                        def _(t, kc=kc):
                            t.transpose(psTf[0:128, 0:64], acumfm[:, 128 * kc:128 * (kc + 1)],
                                        pb.ident_f[0:64, 0:64])
                    with nc.Block() as blk:
                        @blk.vector
                        def _(v, kc=kc):
                            v.tensor_copy(acumtm[:, 64 * kc:64 * (kc + 1)], psTf[0:128, 0:64])
                    with nc.Block() as blk:
                        @blk.tensor
                        def _(t, kc=kc, c2=c2):
                            t.transpose(psTf[0:128, 0:64],
                                        dt2buf[:, 256 * c2 + 128 * kc:256 * c2 + 128 * (kc + 1)],
                                        pb.ident_f[0:64, 0:64])
                    with nc.Block() as blk:
                        @blk.vector
                        def _(v, kc=kc):
                            v.tensor_copy(dt2tm[:, 64 * kc:64 * (kc + 1)], psTf[0:128, 0:64])
                with nc.Block() as blk:
                    @blk.sync
                    def _(sy):
                        pb.dma(sy, acum_dram.ap(), acumfm[:, :], name="st")
                        pb.wait(sy, "st")
                with nc.Block() as blk:
                    @blk.sync
                    def _(sy):
                        pb.dma(sy, a255row[:, :].rearrange("o (h q) -> o h q", h=64),
                               acum_dram.ap()[:, 255:256].rearrange("(o h) q -> o h q", o=1),
                               allow_slow_non_contiguous=True)
                        pb.wait(sy, "ld")
                with nc.Block() as blk:
                    @blk.tensor
                    def _(t):
                        t.matmul(psTf[:, 0:64], pb.ones_row_f[:, :], a255row[:, :],
                                 start=True, stop=True)
                with nc.Block() as blk:
                    @blk.vector
                    def _(v):
                        for kc in range(2):
                            v.tensor_sub(acumtm2[:, 64 * kc:64 * (kc + 1)], psTf[:, 0:64],
                                         acumtm[:, 64 * kc:64 * (kc + 1)])
                if half == 0 and c2 == 0:
                    dump("d_acum", acumfm[:, :])
                    dump("d_actm", acumtm[:, :])
                    dump("d_actm2", acumtm2[:, :])
                esSf.close()
                psT2b = pb.psum(f"h{half}_psT2b_{c2}", [128, 128], BF16, es=esS)
                transpose_seq(pb, [
                    (hidfm[:, 512 * ct + 256 * c2 + 128 * kc:
                           512 * ct + 256 * c2 + 128 * (kc + 1)],
                     hdt[kc][:, 128 * ct:128 * (ct + 1)], pb.ident[:, :])
                    for ct in range(32) for kc in range(2)], [psT2, psT2b])
                with nc.Block() as blk:
                    @blk.vector
                    def _(v):
                        for kc in range(2):
                            for hh in range(64):
                                v.tensor_scalar(hdt[kc][:, 64 * hh:64 * (hh + 1)],
                                                hdt[kc][:, 64 * hh:64 * (hh + 1)],
                                                dt2tm[:, 64 * kc + hh:64 * kc + hh + 1],
                                                None, ALU.mult)
                sT = pb.sem("trT"); sE = pb.sem("trE")
                bT = pb.cnt["trT"]; bE = pb.cnt["trE"]
                with nc.Block() as blk:
                    @blk.tensor
                    def _(t, c2=c2, bT=bT, bE=bE):
                        for g8 in range(8):
                            if g8 >= 2:
                                t.wait_ge(sE, bE + g8 - 1)
                            for kc in range(2):
                                mm = t.matmul(ps512[g8 % 2][:, 256 * kc:256 * (kc + 1)],
                                              bfm[:, 512 * g8 + 256 * c2 + 128 * kc:
                                                  512 * g8 + 256 * c2 + 128 * (kc + 1)],
                                              cfm[:, 512 * g8 + 256 * c2:
                                                  512 * g8 + 256 * c2 + 256],
                                              start=True, stop=True)
                            mm.then_inc(sT, 1)
                    @blk.vector
                    def _(v, bT=bT, bE=bE):
                        for g8 in range(8):
                            v.wait_ge(sT, bT + g8 + 1)
                            v.tensor_copy(gmtbuf[:, 512 * g8:512 * (g8 + 1)],
                                          ps512[g8 % 2][:, :]).then_inc(sE, 1)
                pb.cnt["trT"] += 8
                pb.cnt["trE"] += 8
                if half == 0 and c2 == 0:
                    dump("d_hdt0", hdt[0][:, :])
                    dump("d_hdt1", hdt[1][:, :])
                    dump("d_gmt", gmtbuf[:, :])
                psA = [pb.psum(f"h{half}_psA{c2}_{i}", [128, 256], F32, es=esS)
                       for i in range(2)]
                psY = [pb.psum(f"h{half}_psY{c2}_{i}", [128, 256], F32, es=esS)
                       for i in range(2)]
                sA = pb.sem("ssdA"); bA = pb.cnt["ssdA"]
                sS = pb.sem("ssdS"); bS = pb.cnt["ssdS"]
                sT1 = pb.sem("ssdT1"); bT1 = pb.cnt["ssdT1"]
                sT2 = pb.sem("ssdT2"); bT2 = pb.cnt["ssdT2"]
                sW = pb.sem("ssdW"); bW = pb.cnt["ssdW"]
                sY = pb.sem("ssdY"); bY = pb.cnt["ssdY"]
                sE = pb.sem("ssdE"); bE = pb.cnt["ssdE"]
                sR = pb.sem("ssdR"); bR = pb.cnt["ssdR"]
                with nc.Block() as blk:
                    @blk.sync
                    def _(sy, bA=bA, bR=bR):
                        for oct in range(8):
                            if oct >= 2:
                                sy.wait_ge(sA, bA + 8 * (oct - 1))
                            sy.dma_start(
                                out=acr8[oct % 2][:, :].rearrange("o (h q) -> o h q", h=8),
                                in_=acum_dram.ap()[8 * oct:8 * (oct + 1), :]
                                .rearrange("(o h) q -> o h q", o=1)).then_inc(sR, 16)
                    @blk.tensor
                    def _(t, c2=c2, bA=bA, bS=bS, bT2=bT2, bW=bW, bE=bE, bR=bR):
                        for h in range(65):
                            if h < 64:
                                if h % 8 == 0:
                                    t.wait_ge(sR, bR + 16 * (h // 8 + 1))
                                if h >= 2:
                                    t.wait_ge(sS, bS + h - 1)
                                    t.wait_ge(sT2, bT2 + h - 1)
                                t.matmul(psA[h % 2][:, :], pb.ones_row_f[:, :],
                                         acr8[(h // 8) % 2][0:1, 256 * (h % 8):256 * (h % 8 + 1)],
                                         start=True, stop=True).then_inc(sA, 1)
                            if h >= 1:
                                hh = h - 1
                                pi = hh % 2
                                t.wait_ge(sW, bW + hh + 1)
                                if hh >= 2:
                                    t.wait_ge(sE, bE + hh - 1)
                                ph = 64 * (hh % 2)
                                tp = (0, ph) if ph else None
                                t.matmul(psY[pi][ph:ph + 64, :],
                                         hdt[0][:, 64 * hh:64 * (hh + 1)],
                                         mbuf[pi][:, 0:256], start=True, stop=False,
                                         tile_position=tp)
                                t.matmul(psY[pi][ph:ph + 64, :],
                                         hdt[1][:, 64 * hh:64 * (hh + 1)],
                                         mbuf[pi][:, 256:512], start=False, stop=True,
                                         tile_position=tp).then_inc(sY, 1)
                    @blk.vector
                    def _(v, c2=c2, bA=bA, bS=bS, bT1=bT1, bY=bY):
                        for h in range(65):
                            if h < 64:
                                v.wait_ge(sA, bA + h + 1)
                                if h >= 2:
                                    v.wait_ge(sT1, bT1 + h - 1)
                                pi = h % 2
                                v.tensor_scalar(s1buf[pi][:, 0:256], psA[pi][:, :],
                                                acumtm[:, h:h + 1], 0.0,
                                                ALU.subtract, ALU.min)
                                v.tensor_scalar(s1buf[pi][:, 256:512], psA[pi][:, :],
                                                acumtm[:, 64 + h:64 + h + 1], 0.0,
                                                ALU.subtract, ALU.min).then_inc(sS, 1)
                            if h >= 1:
                                hh = h - 1
                                pi = hh % 2
                                v.wait_ge(sY, bY + hh + 1)
                                ph = 64 * (hh % 2)
                                g2 = hh // 2
                                csl = slice(512 * g2 + 256 * c2, 512 * g2 + 256 * c2 + 256)
                                esl = ev64[ph:ph + 64, 256 * pi:256 * (pi + 1)]
                                v.tensor_scalar(esl, hidfm[ph:ph + 64, csl],
                                                dpair[ph:ph + 64, g2:g2 + 1], None, ALU.mult)
                                v.tensor_add(esl, esl, psY[pi][ph:ph + 64, :])
                                v.tensor_mul(hidfm[ph:ph + 64, csl], esl,
                                             gatebuf[ph:ph + 64, csl]).then_inc(sE, 1)
                    @blk.scalar
                    def _(s, bS=bS, bW=bW):
                        for h in range(64):
                            s.wait_ge(sS, bS + h + 1)
                            if h >= 2:
                                s.wait_ge(sW, bW + h - 1)
                            pi = h % 2
                            s.activation(t1buf[pi][:, :], s1buf[pi][:, :], AF.Exp
                                         ).then_inc(sT1, 1)
                            s.activation(t2buf[pi][:, 0:256], psA[pi][:, :], AF.Exp,
                                         bias=acumtm2[:, h:h + 1])
                            s.activation(t2buf[pi][:, 256:512], psA[pi][:, :], AF.Exp,
                                         bias=acumtm2[:, 64 + h:64 + h + 1]).then_inc(sT2, 1)
                    @blk.gpsimd
                    def _(g, bT1=bT1, bT2=bT2, bY=bY):
                        for h in range(64):
                            g.wait_ge(sT1, bT1 + h + 1)
                            g.wait_ge(sT2, bT2 + h + 1)
                            if h >= 2:
                                g.wait_ge(sY, bY + h - 1)
                            pi = h % 2
                            g.tensor_mul(mbuf[pi][:, :], t1buf[pi][:, :], tri_w[:, :])
                            g.tensor_add(mbuf[pi][:, :], mbuf[pi][:, :], t2buf[pi][:, :])
                            g.tensor_mul(mbuf[pi][:, :], mbuf[pi][:, :],
                                         gmtbuf[:, 512 * (h % 8):512 * (h % 8 + 1)]
                                         ).then_inc(sW, 1)
                for nm_ in ("ssdA", "ssdS", "ssdT1", "ssdT2", "ssdW", "ssdY", "ssdE"):
                    pb.cnt[nm_] += 64
                pb.cnt["ssdR"] += 8 * 16
                esS.close()
            esM.close()
            nc.leave_named_scope(f"ssd{half}", sc_ssd[0], False)

            if half == 0:
                dump("d_yg", hidfm[:, :])
            sc_op = nc.enter_named_scope(f"outproj{half}", False)
            esO = ExitStack()
            sq_ps = pb.psum(f"h{half}_sqps", [1, 512], F32, es=esO)
            arow = pb.sbuf(f"h{half}_arow", [1, 512], F32, es=esO)
            abc512 = pb.sbuf(f"h{half}_abc512", [128, 512], F32, es=esO)
            sqb = pb.sbuf(f"h{half}_sqb", [128, 512], BF16, es=esO)
            for ct in range(32):
                with nc.Block() as blk:
                    @blk.scalar
                    def _(s, ct=ct):
                        s.activation(sqb[:, :], hidfm[:, 512 * ct:512 * (ct + 1)], AF.Square)
                with nc.Block() as blk:
                    @blk.tensor
                    def _(t, ct=ct):
                        t.matmul(sq_ps[:, :], pb.ones_col[:, :], sqb[:, :],
                                 start=(ct == 0), stop=(ct == 31))
            with nc.Block() as blk:
                @blk.vector
                def _(v):
                    v.tensor_scalar(arow[:, :], sq_ps[:, :], 1.0 / INTER, 1e-6,
                                    ALU.mult, ALU.add)
            with nc.Block() as blk:
                @blk.scalar
                def _(s):
                    s.activation(arow[:, :], arow[:, :], AF.Sqrt)
            with nc.Block() as blk:
                @blk.vector
                def _(v):
                    v.reciprocal(arow[:, :], arow[:, :])
            with nc.Block() as blk:
                @blk.tensor
                def _(t):
                    t.matmul(ps512[0][:, :], pb.ones_row_f[:, :], arow[:, :],
                             start=True, stop=True)
            with nc.Block() as blk:
                @blk.vector
                def _(v):
                    v.tensor_copy(abc512[:, :], ps512[0][:, :])
            for ct in range(32):
                with nc.Block() as blk:
                    @blk.vector
                    def _(v, ct=ct):
                        sl = hidfm[:, 512 * ct:512 * (ct + 1)]
                        v.tensor_mul(sl, sl, abc512[:, :])
                        v.tensor_scalar(sl, sl, nw[:, ct:ct + 1], None, ALU.mult)
            wband2 = pb.sbuf(f"h{half}_wband2", [128, 2 * 32 * 256], BF16, es=esO)
            pso = [pb.psum(f"h{half}_pso{i}", [128, 512], F32, es=esO) for i in range(4)]

            def wo_dma(sy, g):
                off = (g % 2) * 32 * 256
                for q4 in range(8):
                    pb.dma(sy, wband2[:, off + q4 * 1024:off + (q4 + 1) * 1024]
                           .rearrange("p (kt m) -> p kt m", kt=4),
                           wout_ap[512 * q4:512 * (q4 + 1), 256 * g:256 * (g + 1)]
                           .rearrange("(kt p) m -> p kt m", p=128), name=f"wo{half}d")

            def wo_mm(t, g):
                off = (g % 2) * 32 * 256
                ps = pso[2 * (g % 2):2 * (g % 2) + 2]
                for mi in range(2):
                    for kt in range(32):
                        wo = off + kt * 256 + mi * 128
                        mm = t.matmul(ps[mi][:, :], wband2[:, wo:wo + 128],
                                      hidfm[:, 512 * kt:512 * (kt + 1)],
                                      start=(kt == 0), stop=(kt == 31))
                return mm

            def wo_evac(v, g):
                ps = pso[2 * (g % 2):2 * (g % 2) + 2]
                for mi in range(2):
                    m0 = 2 * g + mi
                    cp = v.tensor_add(x1buf[:, 512 * m0:512 * (m0 + 1)], ps[mi][:, :],
                                      xfm[:, 515 * m0 + 3:515 * m0 + 515])
                return cp
            pipe_mm(pb, f"wo{half}", 8, wo_dma, wo_mm, wo_evac)
            esO.close()
            nc.leave_named_scope(f"outproj{half}", sc_op[0], False)
            ln_fm(pb, lambda ct: x1buf[:, 512 * ct:512 * (ct + 1)], NE, 512,
                  lambda ct: x1buf[:, 512 * ct:512 * (ct + 1)], tag=f"a{half}")
            with nc.Block() as blk:
                @blk.sync
                def _(sy, half=half):
                    pb.dma(sy, x1_ap[:, 512 * half:512 * (half + 1)]
                           .rearrange("(ct p) t -> p ct t", p=128),
                           x1buf[:, :].rearrange("p (ct t) -> p ct t", ct=NE), name="st")
                    pb.wait(sy, "st")
            esH.close()


def build_l0attn(debug=False):
    nc = bass.Bass("TRN2", target_bir_lowering=False)
    pb = PB(nc)
    x_lo = nc.dram_tensor("x_lo", [515, EMB], F32, kind="ExternalInput").ap()
    x_hi = nc.dram_tensor("x_hi", [515, EMB], F32, kind="ExternalInput").ap()
    wproj = nc.dram_tensor("w_inproj_t", [EMB, PROJ], F32, kind="ExternalInput").ap()
    cw = nc.dram_tensor("conv_w", [128, 192], F32, kind="ExternalInput").ap()
    cb = nc.dram_tensor("conv_b", [128, 48], F32, kind="ExternalInput").ap()
    dtb = nc.dram_tensor("dt_bias", [64, 1], F32, kind="ExternalInput").ap()
    alog = nc.dram_tensor("a_log", [64, 1], F32, kind="ExternalInput").ap()
    dpair = nc.dram_tensor("d_pair", [128, 32], F32, kind="ExternalInput").ap()
    nw = nc.dram_tensor("norm_w", [128, 32], F32, kind="ExternalInput").ap()
    wout = nc.dram_tensor("w_outproj_t", [INTER, EMB], F32, kind="ExternalInput").ap()
    x1 = nc.dram_tensor("x1fm", [EMB, 1024], BF16, kind="ExternalOutput").ap()
    dbg = None
    if debug:
        dbg = {}
        for nm_, shp, dt_ in (("d_gate", [128, 32 * 512], BF16), ("d_hid", [128, 32 * 512], BF16),
                              ("d_bfm", [128, 8 * 512], BF16), ("d_cfm", [128, 8 * 512], BF16),
                              ("d_dt", [64, 512], F32), ("d_dt2", [64, 512], F32),
                              ("d_adt", [64, 512], F32), ("d_acum", [64, 256], F32),
                              ("d_actm", [128, 128], F32), ("d_actm2", [128, 128], F32),
                              ("d_hdt0", [128, 4096], BF16), ("d_hdt1", [128, 4096], BF16),
                              ("d_gmt", [128, 8 * 512], BF16), ("d_yg", [128, 32 * 512], BF16)):
            dbg[nm_] = nc.dram_tensor(nm_, shp, dt_, kind="ExternalOutput").ap()
    with pb.es:
        pb.consts()
        _l0attn_body(pb, (x_lo, x_hi), wproj, cw, cb, dtb, alog, dpair, nw, wout, x1, dbg=dbg)
    return nc



def _l1attn_body(pb, core_blocks_dummy, x2_ap, qkv_ap, oproj_ap, gam_ap, bet_ap,
                 allow_ap, x3_ap):
    nc = pb.nc
    T = 1024
    SC = 1.0 / math.sqrt(2048.0)
    kag_in = nc.dram_tensor("kag_in", [EMB, T], BF16)
    vag_in = nc.dram_tensor("vag_in", [T, EMB], BF16)
    KAG = nc.dram_tensor("KAG", [8 * EMB, T], BF16, addr_space="Shared")
    VAG = nc.dram_tensor("VAG", [8 * T, EMB], BF16, addr_space="Shared")
    kn_in = nc.dram_tensor("kn_in", [1, 1], F32)
    KNAg = nc.dram_tensor("KNAg", [8, 1], F32, addr_space="Shared")
    esP = ExitStack()
    with esP:
        gam = pb.sbuf("t_gam", [128, 16], F32, es=esP)
        bet = pb.sbuf("t_bet", [128, 16], F32, es=esP)
        xbuf = pb.sbuf("t_x", [128, NE * T], BF16, es=esP)
        qT = pb.sbuf("t_qT", [128, NE * T], BF16, es=esP)
        qkr = pb.sbuf("t_qkr", [1, 1040], F32, es=esP)
        with nc.Block() as blk:
            @blk.sync
            def _(sy):
                pb.dma(sy, gam[:, :], gam_ap)
                pb.dma(sy, bet[:, :], bet_ap)
                pb.dma(sy, xbuf[:, :].rearrange("p (ct t) -> p ct t", ct=NE),
                       x2_ap.rearrange("(ct p) t -> p ct t", p=128))
                pb.wait(sy, "ld")
        sc_pj = nc.enter_named_scope("l1aproj", False)
        es1 = ExitStack()
        xn = pb.sbuf("t_xn", [128, NE * T], BF16, es=es1)
        ln_fm(pb, lambda ct: xbuf[:, ct * T:(ct + 1) * T], NE, T,
              lambda ct: xn[:, ct * T:(ct + 1) * T], gamma=gam, beta=bet, tag="t1")
        ktl = pb.sbuf("t_ktl", [128, NE * T], BF16, es=es1)
        vt = pb.sbuf("t_vt", [128, 8 * EMB], BF16, es=es1)
        wband = pb.sbuf("t_wband", [128, 2 * NE * 512], BF16, es=es1)
        psq = [pb.psum(f"t_psq{i}", [128, 512], F32, es=es1) for i in range(8)]
        def qk_dma(sy, g):
            which, mg = divmod(g, 8)
            colbase = 2048 * which
            m0 = 2 * mg
            wb = wband[:, (g % 2) * 4096:(g % 2) * 4096 + 4096]
            for q4 in range(4):
                pb.dma(sy, wb[:, q4 * 1024:(q4 + 1) * 1024]
                       .rearrange("p (kt m) -> p kt m", kt=4),
                       qkv_ap[512 * q4:512 * (q4 + 1),
                              colbase + 128 * m0:colbase + 128 * (m0 + 2)]
                       .rearrange("(kt p) m -> p kt m", p=128), name="qkd")

        def qk_mm(t, g):
            wb = wband[:, (g % 2) * 4096:(g % 2) * 4096 + 4096]
            ps = psq[4 * (g % 2):4 * (g % 2) + 4]
            for mi in range(2):
                for th in range(2):
                    for kt in range(NE):
                        off = kt * 256 + mi * 128
                        mm = t.matmul(ps[2 * mi + th][:, :], wb[:, off:off + 128],
                                      xn[:, kt * T + 512 * th:kt * T + 512 * (th + 1)],
                                      start=(kt == 0), stop=(kt == NE - 1))
            return mm

        def qk_evac(v, g):
            which, mg = divmod(g, 8)
            m0 = 2 * mg
            dstbuf = qT if which == 0 else ktl
            ps = psq[4 * (g % 2):4 * (g % 2) + 4]
            for mi in range(2):
                for th in range(2):
                    cp = v.tensor_copy(
                        dstbuf[:, (m0 + mi) * T + 512 * th:(m0 + mi) * T + 512 * (th + 1)],
                        ps[2 * mi + th][:, :])
            return cp
        pipe_mm(pb, "qk", 16, qk_dma, qk_mm, qk_evac)
        def v_dma(sy, g):
            wb = wband[:, (g % 2) * 8192:(g % 2) * 8192 + 8192]
            for q4 in range(4):
                pb.dma(sy, wb[:, q4 * 2048:(q4 + 1) * 2048]
                       .rearrange("p (kt m) -> p kt m", kt=4),
                       qkv_ap[512 * q4:512 * (q4 + 1),
                              4096 + 512 * g:4096 + 512 * (g + 1)]
                       .rearrange("(kt p) m -> p kt m", p=128), name="qkd")

        def v_mm(t, g):
            wb = wband[:, (g % 2) * 8192:(g % 2) * 8192 + 8192]
            for tc in range(8):
                for kt in range(NE):
                    mm = t.matmul(psq[tc][:, :], xn[:, kt * T + 128 * tc:kt * T + 128 * (tc + 1)],
                                  wb[:, kt * 512:(kt + 1) * 512],
                                  start=(kt == 0), stop=(kt == NE - 1))
            return mm

        def v_evac(v, g):
            for tc in range(8):
                cp = v.tensor_copy(vt[:, tc * EMB + 512 * g:tc * EMB + 512 * (g + 1)],
                                   psq[tc][:, :])
            return cp
        pipe_mm(pb, "qk", 4, v_dma, v_mm, v_evac, double_psum=False)
        sQN = pb.sem("qnS")
        sQM = pb.sem("qnM")
        with nc.Block() as blk:
            @blk.vector
            def _(v):
                n = 0
                for src in (qT, ktl):
                    for ct in range(NE):
                        if n >= 2:
                            v.wait_ge(sQM, n - 1)
                        sq = wband[:, (n % 2) * 1024:(n % 2) * 1024 + 1024]
                        v.tensor_mul(sq, src[:, ct * T:(ct + 1) * T],
                                     src[:, ct * T:(ct + 1) * T]).then_inc(sQN, 1)
                        n += 1
            @blk.tensor
            def _(t):
                n = 0
                for which in range(2):
                    for ct in range(NE):
                        t.wait_ge(sQN, n + 1)
                        sq = wband[:, (n % 2) * 1024:(n % 2) * 1024 + 1024]
                        for th in range(2):
                            mm = t.matmul(psq[2 * which + th][0:1, :],
                                          pb.ones_col[:, :], sq[:, 512 * th:512 * (th + 1)],
                                          start=(ct == 0), stop=(ct == NE - 1))
                        mm.then_inc(sQM, 1)
                        n += 1
        with nc.Block() as blk:
            @blk.scalar
            def _(sc):
                sc.activation(qkr[0:1, 0:512], psq[0][0:1, :], AF.Sqrt)
                sc.activation(qkr[0:1, 512:1024], psq[1][0:1, :], AF.Sqrt)
            @blk.vector
            def _(v):
                v.reduce_max(qkr[0:1, 1024:1025], psq[2][0:1, :], AX.X)
                v.reduce_max(qkr[0:1, 1025:1026], psq[3][0:1, :], AX.X)
        with nc.Block() as blk:
            @blk.vector
            def _(v):
                v.reduce_max(qkr[0:1, 1026:1027], qkr[0:1, 1024:1026], AX.X)
        with nc.Block() as blk:
            @blk.sync
            def _(sy):
                pb.dma(sy, kag_in.ap().rearrange("(ct p) t -> p ct t", p=128),
                       ktl[:, :].rearrange("p (ct t) -> p ct t", ct=NE), name="st")
                pb.dma(sy, vag_in.ap().rearrange("(tc p) m -> p tc m", p=128),
                       vt[:, :].rearrange("p (tc m) -> p tc m", tc=8), name="st")
                pb.dma(sy, kn_in.ap(), qkr[0:1, 1026:1027], name="st")
                pb.wait(sy, "st")
        with nc.Block() as blk:
            @blk.gpsimd
            def _(g):
                ccs = pb.sem("cc")
                for src, dst in ((kag_in, KAG), (vag_in, VAG), (kn_in, KNAg)):
                    g.collective_compute(
                        "AllGather", ALU.bypass, replica_groups=[list(range(NCORES))],
                        ins=[src.ap().opt()], outs=[dst.ap().opt()]).then_inc(ccs)
                    pb.cnt["cc"] += 1
                g.wait_ge(ccs, pb.cnt["cc"])
        es1.close()
        nc.leave_named_scope("l1aproj", sc_pj[0], False)
        sc_at = nc.enter_named_scope("l1aattn", False)

        OT = pb.sbuf("t_OT", [128, NE * T], BF16, es=esP)
        es2 = ExitStack()
        kbuf = [pb.sbuf(f"t_kb{i}", [128, 16 * 512], BF16, es=es2) for i in range(2)]
        vbuf = [pb.sbuf(f"t_vb{i}", [128, 4 * EMB], BF16, es=es2) for i in range(2)]
        abuf = pb.sbuf("t_ab", [128, 4 * 512], BF16, es=es2)
        pbuf = [pb.sbuf(f"t_pb{i}", [128, 512], BF16, es=es2) for i in range(2)]
        ptbuf = [pb.sbuf(f"t_pt{i}", [128, 512], BF16, es=es2) for i in range(2)]
        obig = [pb.sbuf(f"t_obig{i}", [128, EMB], F32, es=es2) for i in range(2)]
        obuf = pb.sbuf("t_ob", [128, EMB], BF16, es=es2)
        nbq = pb.sbuf("t_nbq", [128, 8], F32, es=es2)
        mkcol = pb.sbuf("t_mkcol", [128, 1], F32, es=es2)
        lrow = pb.sbuf("t_lrow", [128, 4 * 20], F32, es=es2)
        psS = [pb.psum(f"t_psS{i}", [128, 512], F32, es=es2) for i in range(2)]
        psPT = [pb.psum(f"t_psPT{i}", [128, 512], BF16, es=es2) for i in range(2)]
        psO = [pb.psum(f"t_psO{i}", [128, 512], F32, es=es2) for i in range(4)]
        with nc.Block() as blk:
            @blk.sync
            def _(sy):
                pb.dma(sy, qkr[0:1, 1028:1036].rearrange("o (r h) -> o r h", r=8),
                       KNAg.ap().rearrange("(o r) h -> o r h", o=1))
                pb.wait(sy, "ld")
        with nc.Block() as blk:
            @blk.vector
            def _(v):
                v.reduce_max(qkr[0:1, 1036:1037], qkr[0:1, 1028:1036], AX.X)
        with nc.Block() as blk:
            @blk.scalar
            def _(sc):
                sc.activation(qkr[0:1, 1037:1038], qkr[0:1, 1036:1037], AF.Sqrt)
        with nc.Block() as blk:
            @blk.vector
            def _(v):
                v.tensor_scalar(qkr[0:1, 1038:1039], qkr[0:1, 1037:1038], -SC,
                                None, ALU.mult)
        with nc.Block() as blk:
            @blk.tensor
            def _(t):
                for m in range(8):
                    t.transpose(psS[0][:, m:m + 1], qkr[0:1, 128 * m:128 * (m + 1)],
                                pb.ident_f[0:1, 0:1])
                t.matmul(psS[1][:, 0:1], pb.ones_row_f[:, :], qkr[0:1, 1038:1039],
                         start=True, stop=True)
        with nc.Block() as blk:
            @blk.vector
            def _(v):
                v.tensor_copy(mkcol[:, :], psS[1][:, 0:1])
                v.tensor_scalar(nbq[:, 0:8], psS[0][:, 0:8], mkcol[:, 0:1],
                                None, ALU.mult)
        pb.sem("aK")
        for hb in range(2):
          for sub in range(2):
            nslots = 8 if hb == 0 else 16
            si0 = 32 * hb
            ni = 2 * nslots
            sM = pb.sem("aM")
            bKl = pb.cnt["aK"]
            bM = pb.cnt["aM"]
            sE = pb.sem("aE"); bE = pb.cnt["aE"]
            sD = pb.sem("aD"); bD = pb.cnt["aD"]
            sPT = pb.sem("aPT"); bPT = pb.cnt["aPT"]
            sV = pb.sem("aV"); bV = pb.cnt["aV"]
            sO = pb.sem("aO"); bO = pb.cnt["aO"]
            sA = pb.sem("aA"); bA = pb.cnt["aA"]
            with nc.Block() as blk:
                @blk.sync
                def _(sy, hb=hb, nslots=nslots, si0=si0, bO=bO, sub=sub):
                    for j in range(nslots):
                        if j >= 2:
                            sy.wait_ge(sO, bO + 2 * (j - 1))
                        r = own(j)
                        co = 512 * (0 if j <= 7 else 1)
                        pb.dma(sy, kbuf[j % 2][:, :].rearrange("p (kt f) -> p kt f", kt=NE),
                               KAG.ap()[EMB * r:EMB * (r + 1), co:co + 512]
                               .rearrange("(kt p) f -> p kt f", p=128), name="aK")
                        pb.dma(sy, vbuf[j % 2][:, :].rearrange("p (kc f) -> p kc f", kc=4),
                               VAG.ap()[T * r + co:T * r + co + 512, :]
                               .rearrange("(kc p) f -> p kc f", p=128), name="aK")
                        for qt in range(2):
                            pb.dma(sy, abuf[:, 512 * (2 * (j % 2) + qt):
                                            512 * (2 * (j % 2) + qt) + 512],
                                   allow_ap[si0 + nslots * (2 * sub + qt) + j, :, :],
                                   name="aK")
                @blk.tensor
                def _(t, hb=hb, ni=ni, bKl=bKl, bE=bE, bD=bD, bV=bV, bA=bA, sub=sub):
                    for i in range(ni + 1):
                        if i < ni:
                            j, qt = divmod(i, 2)
                            if qt == 0:
                                t.wait_ge(pb.sem("aK"), bKl + 64 * (j + 1))
                            if i >= 2:
                                t.wait_ge(sE, bE + i - 1)
                            mq = 4 * hb + 2 * sub + qt
                            for kt in range(NE):
                                mm = t.matmul(psS[i % 2][:, :],
                                              qT[:, kt * T + 128 * mq:kt * T + 128 * (mq + 1)],
                                              kbuf[j % 2][:, kt * 512:(kt + 1) * 512],
                                              start=(kt == 0), stop=(kt == NE - 1))
                            mm.then_inc(sM, 1)
                        if i >= 1:
                            k = i - 1
                            jj = k // 4
                            t.wait_ge(sD, bD + k + 1)
                            if k >= 2:
                                t.wait_ge(sV, bV + k - 1)
                            for kc in range(4):
                                tt = t.transpose(psPT[k % 2][:, 128 * kc:128 * (kc + 1)],
                                                 pbuf[k % 2][:, 128 * kc:128 * (kc + 1)],
                                                 pb.ident[:, :])
                            tt.then_inc(sPT, 1)
                            t.wait_ge(sV, bV + k + 1)
                            if k >= 1:
                                t.wait_ge(sA, bA + k)
                            for kc in range(4):
                                for og in range(4):
                                    mm = t.matmul(psO[og][:, :],
                                                  ptbuf[k % 2][:, 128 * kc:128 * (kc + 1)],
                                                  vbuf[jj % 2][:, kc * EMB + 512 * og:
                                                               kc * EMB + 512 * (og + 1)],
                                                  start=(kc == 0), stop=(kc == 3))
                            mm.then_inc(sO, 1)
                @blk.scalar
                def _(s, ni=ni, bM=bM, bD=bD, bPT=bPT, hb=hb, sub=sub):
                    for i in range(ni):
                        j, qt = divmod(i, 2)
                        s.wait_ge(sM, bM + i + 1)
                        if i >= 2:
                            s.wait_ge(sD, bD + i - 1)
                            s.wait_ge(sPT, bPT + i - 1)
                        mq = 4 * hb + 2 * sub + qt
                        s.activation(pbuf[i % 2][:, :], psS[i % 2][:, :], AF.Exp,
                                     bias=nbq[:, mq:mq + 1],
                                     scale=SC).then_inc(sE, 1)
                @blk.vector
                def _(v, ni=ni, bE=bE, bPT=bPT, bO=bO):
                    for i in range(ni):
                        j, qt = divmod(i, 2)
                        v.wait_ge(sE, bE + i + 1)
                        v.tensor_mul(pbuf[i % 2][:, :], pbuf[i % 2][:, :],
                                     abuf[:, 512 * (2 * (j % 2) + qt):
                                          512 * (2 * (j % 2) + qt) + 512])
                        v.reduce_sum(lrow[:, 20 * qt + j:20 * qt + j + 1],
                                     pbuf[i % 2][:, :], AX.X).then_inc(sD, 1)
                        v.wait_ge(sPT, bPT + i + 1)
                        v.tensor_copy(ptbuf[i % 2][:, :], psPT[i % 2][:, :]).then_inc(sV, 1)
                        v.wait_ge(sO, bO + i + 1)
                        for og in range(4):
                            if i < 2:
                                cp = v.tensor_copy(obig[qt][:, 512 * og:512 * (og + 1)],
                                                   psO[og][:, :])
                            else:
                                cp = v.tensor_add(obig[qt][:, 512 * og:512 * (og + 1)],
                                                  obig[qt][:, 512 * og:512 * (og + 1)],
                                                  psO[og][:, :])
                        cp.then_inc(sA, 1)
            pb.cnt["aM"] += ni
            pb.cnt["aE"] += ni
            pb.cnt["aD"] += ni
            pb.cnt["aPT"] += ni
            pb.cnt["aV"] += ni
            pb.cnt["aO"] += ni
            pb.cnt["aA"] += ni
            for qt in range(2):
                m = 4 * hb + 2 * sub + qt
                with nc.Block() as blk:
                    @blk.vector
                    def _(v, qt=qt, nslots=nslots):
                        pb.guard(v, v.reduce_sum(lrow[:, 20 * qt + 16:20 * qt + 17],
                                                 lrow[:, 20 * qt:20 * qt + nslots], AX.X))
                        pb.guard(v, v.reciprocal(lrow[:, 20 * qt + 17:20 * qt + 18],
                                                 lrow[:, 20 * qt + 16:20 * qt + 17]))
                        for og in range(4):
                            v.tensor_scalar(obuf[:, 512 * og:512 * (og + 1)],
                                            obig[qt][:, 512 * og:512 * (og + 1)],
                                            lrow[:, 20 * qt + 17:20 * qt + 18],
                                            None, ALU.mult)
                transpose_seq(pb, [
                    (obuf[:, 128 * ct:128 * (ct + 1)],
                     OT[:, ct * T + 128 * m:ct * T + 128 * (m + 1)], pb.ident[:, :])
                    for ct in range(NE)], psPT)
        es2.close()
        nc.leave_named_scope("l1aattn", sc_at[0], False)
        sc_oo = nc.enter_named_scope("l1aoproj", False)
        es3 = ExitStack()
        wband2 = pb.sbuf("t_wband2", [128, 2 * 4096], BF16, es=es3)
        pso = [pb.psum(f"t_pso{i}", [128, 512], F32, es=es3) for i in range(8)]
        def op_dma(sy, g):
            wb = wband2[:, (g % 2) * 4096:(g % 2) * 4096 + 4096]
            for q4 in range(4):
                pb.dma(sy, wb[:, q4 * 1024:(q4 + 1) * 1024]
                       .rearrange("p (kt m) -> p kt m", kt=4),
                       oproj_ap[512 * q4:512 * (q4 + 1), 256 * g:256 * (g + 1)]
                       .rearrange("(kt p) m -> p kt m", p=128), name="opd")

        def op_mm(t, g):
            wb = wband2[:, (g % 2) * 4096:(g % 2) * 4096 + 4096]
            ps = pso[4 * (g % 2):4 * (g % 2) + 4]
            for mi in range(2):
                for th in range(2):
                    for kt in range(NE):
                        off = kt * 256 + mi * 128
                        mm = t.matmul(ps[2 * mi + th][:, :], wb[:, off:off + 128],
                                      OT[:, kt * T + 512 * th:kt * T + 512 * (th + 1)],
                                      start=(kt == 0), stop=(kt == NE - 1))
            return mm

        def op_evac(v, g):
            ps = pso[4 * (g % 2):4 * (g % 2) + 4]
            for mi in range(2):
                for th in range(2):
                    sl = slice((2 * g + mi) * T + 512 * th, (2 * g + mi) * T + 512 * (th + 1))
                    cp = v.scalar_tensor_tensor(xbuf[:, sl], xbuf[:, sl], 2.0,
                                                ps[2 * mi + th][:, :], ALU.mult, ALU.add)
            return cp
        pipe_mm(pb, "op", 8, op_dma, op_mm, op_evac)
        es3.close()
        nc.leave_named_scope("l1aoproj", sc_oo[0], False)
        ln_fm(pb, lambda ct: xbuf[:, ct * T:(ct + 1) * T], NE, T,
              lambda ct: xbuf[:, ct * T:(ct + 1) * T], tag="t2")
        with nc.Block() as blk:
            @blk.sync
            def _(sy):
                pb.dma(sy, x3_ap.rearrange("(ct p) t -> p ct t", p=128),
                       xbuf[:, :].rearrange("p (ct t) -> p ct t", ct=NE), name="st")
                pb.wait(sy, "st")


def build_l1attn():
    nc = bass.Bass("TRN2", target_bir_lowering=False)
    pb = PB(nc)
    x2 = nc.dram_tensor("x2fm", [EMB, 1024], BF16, kind="ExternalInput").ap()
    qkv = nc.dram_tensor("qkv1", [EMB, 6144], F32, kind="ExternalInput").ap()
    oproj = nc.dram_tensor("oproj", [EMB, EMB], F32, kind="ExternalInput").ap()
    gam = nc.dram_tensor("gamma", [128, 16], F32, kind="ExternalInput").ap()
    bet = nc.dram_tensor("beta", [128, 16], F32, kind="ExternalInput").ap()
    allow = nc.dram_tensor("allow", [96, 128, 512], BF16, kind="ExternalInput").ap()
    x3 = nc.dram_tensor("x3fm", [EMB, 1024], BF16, kind="ExternalOutput").ap()
    with pb.es:
        pb.consts()
        _l1attn_body(pb, None, x2, qkv, oproj, gam, bet, allow, x3)
    return nc



def _l1ffn_body(pb, x3_ap, wqkv_ap, wgate_ap, out_ap, dbg=None):
    nc = pb.nc

    def dump(name, src_ap):
        if dbg is None or name not in dbg:
            return
        with nc.Block() as blk:
            @blk.sync
            def _(sy):
                pb.dma(sy, dbg[name], src_ap, name="st")
                pb.wait(sy, "st")
    from concourse.aot_env import allow_ici_all_to_all
    T = 1024
    SC2 = 1.0 / math.sqrt(128.0)
    ksh_in = nc.dram_tensor("ksh_in", [128, T], BF16)
    vsh_in = nc.dram_tensor("vsh_in", [T, 128], BF16)
    knm_in = nc.dram_tensor("knm_in", [1, 2], F32)
    KSHg = nc.dram_tensor("KSHg", [8 * 128, T], BF16, addr_space="Shared")
    Vg = nc.dram_tensor("Vg", [8 * T, 128], BF16, addr_space="Shared")
    KNg = nc.dram_tensor("KNg", [8, 2], F32, addr_space="Shared")
    qa_in = nc.dram_tensor("qa_in", [EMB, T], BF16)
    QA = nc.dram_tensor("QA", [EMB, T], BF16)
    ya_in = nc.dram_tensor("ya_in", [8 * T, 256], BF16)
    YA = nc.dram_tensor("YA", [8 * T, 256], BF16)
    esP = ExitStack()
    with esP:
        xbuf = pb.sbuf("u_x", [128, NE * T], BF16, es=esP)
        triv = pb.sbuf("u_triv", [128, 4 * 512], BF16, es=esP)
        knm = pb.sbuf("u_knm", [1, 18], F32, es=esP)
        with nc.Block() as blk:
            @blk.sync
            def _(sy):
                pb.dma(sy, xbuf[:, :].rearrange("p (ct t) -> p ct t", ct=NE),
                       x3_ap.rearrange("(ct p) t -> p ct t", p=128))
                pb.wait(sy, "ld")
            @blk.vector
            def _(v):
                for vv in range(4):
                    v.tensor_scalar(triv[:, 512 * vv:512 * (vv + 1)], pb.iota_row[:, 0:512],
                                    pb.iota_col[:, 0:1], float(128 * vv),
                                    ALU.subtract, ALU.is_ge)

        sc_p2 = nc.enter_named_scope("l1fproj", False)
        es1 = ExitStack()
        qTl = pb.sbuf("u_qT", [128, NE * T], BF16, es=es1)
        ktm = pb.sbuf("u_ktm", [128, 8 * 512], BF16, es=es1)
        vtm = pb.sbuf("u_vtm", [128, 8 * 512], BF16, es=es1)
        kfull = pb.sbuf("u_kfull", [128, 8 * EMB], BF16, es=es1)
        vfull = pb.sbuf("u_vfull", [128, 8 * EMB], BF16, es=es1)
        wband = pb.sbuf("u_wband", [128, 2 * NE * 512], BF16, es=es1)
        glog = pb.sbuf("u_glog", [128, 8 * 16], F32, es=es1)
        kshtm = pb.sbuf("u_kshtm", [128, 8 * 128], BF16, es=es1)
        vshtm = pb.sbuf("u_vshtm", [128, 8 * 128], BF16, es=es1)
        kshT = pb.sbuf("u_kshT", [128, 8 * 128], BF16, es=es1)
        gsc = pb.sbuf("u_gsc", [128, 4], F32, es=es1)
        expg = pb.sbuf("u_expg", [128, 16], F32, es=es1)
        indb = pb.sbuf("u_indb", [128, 16], F32, es=es1)
        gt1b = pb.sbuf("u_gt1b", [128, 16], F32, es=es1)
        tmp128 = pb.sbuf("u_tmp128", [128, 128], BF16, es=es1)
        sq128 = pb.sbuf("u_sq128", [128, 128], F32, es=es1)
        kn2 = pb.sbuf("u_kn2", [128, 12], F32, es=es1)
        wgb = pb.sbuf("u_wgb", [128, NE * 16], F32, es=es1)
        es_pq = ExitStack()
        psq = [pb.psum(f"u_psq{i}", [128, 512], F32, es=es_pq) for i in range(8)]
        def q2_dma(sy, g):
            wb = wband[:, (g % 2) * 4096:(g % 2) * 4096 + 4096]
            for q4 in range(4):
                pb.dma(sy, wb[:, q4 * 1024:(q4 + 1) * 1024]
                       .rearrange("p (kt m) -> p kt m", kt=4),
                       wqkv_ap[512 * q4:512 * (q4 + 1), 256 * g:256 * (g + 1)]
                       .rearrange("(kt p) m -> p kt m", p=128), name="q2d")

        def q2_mm(t, g):
            wb = wband[:, (g % 2) * 4096:(g % 2) * 4096 + 4096]
            ps = psq[4 * (g % 2):4 * (g % 2) + 4]
            for mi in range(2):
                for th in range(2):
                    for kt in range(NE):
                        off = kt * 256 + mi * 128
                        mm = t.matmul(ps[2 * mi + th][:, :], wb[:, off:off + 128],
                                      xbuf[:, kt * T + 512 * th:kt * T + 512 * (th + 1)],
                                      start=(kt == 0), stop=(kt == NE - 1))
            return mm

        def q2_evac(v, g):
            ps = psq[4 * (g % 2):4 * (g % 2) + 4]
            for mi in range(2):
                for th in range(2):
                    cp = v.tensor_copy(
                        qTl[:, (2 * g + mi) * T + 512 * th:(2 * g + mi) * T + 512 * (th + 1)],
                        ps[2 * mi + th][:, :])
            return cp
        pipe_mm(pb, "q2", 8, q2_dma, q2_mm, q2_evac)
        def kv_dma(sy, g):
            wb = wband[:, (g % 2) * 8192:(g % 2) * 8192 + 8192]
            colbase = 2048 + 2048 * (g // 4)
            og = g % 4
            for q4 in range(4):
                pb.dma(sy, wb[:, q4 * 2048:(q4 + 1) * 2048]
                       .rearrange("p (kt m) -> p kt m", kt=4),
                       wqkv_ap[512 * q4:512 * (q4 + 1),
                               colbase + 512 * og:colbase + 512 * (og + 1)]
                       .rearrange("(kt p) m -> p kt m", p=128), name="q2d")

        def kv_mm(t, g):
            wb = wband[:, (g % 2) * 8192:(g % 2) * 8192 + 8192]
            for tc in range(8):
                for kt in range(NE):
                    mm = t.matmul(psq[tc][:, :],
                                  xbuf[:, kt * T + 128 * tc:kt * T + 128 * (tc + 1)],
                                  wb[:, kt * 512:(kt + 1) * 512],
                                  start=(kt == 0), stop=(kt == NE - 1))
            return mm

        def kv_evac(v, g):
            dstbuf = kfull if g < 4 else vfull
            og = g % 4
            for tc in range(8):
                cp = v.tensor_copy(dstbuf[:, tc * EMB + 512 * og:tc * EMB + 512 * (og + 1)],
                                   psq[tc][:, :])
            return cp
        pipe_mm(pb, "q2", 8, kv_dma, kv_mm, kv_evac, double_psum=False)
        xg32 = pb.sbuf("u_xg32", [128, T], F32, es=es1)
        with nc.Block() as blk:
            @blk.sync
            def _(sy):
                pb.dma(sy, wgb[:, :].rearrange("p (kt m) -> p kt m", kt=NE),
                       wgate_ap.rearrange("(kt p) m -> p kt m", p=128))
                pb.wait(sy, "ld")
        for kt in range(NE):
            with nc.Block() as blk:
                @blk.vector
                def _(v, kt=kt):
                    v.tensor_copy(xg32[:, :], xbuf[:, kt * T:(kt + 1) * T])
            with nc.Block() as blk:
                @blk.tensor
                def _(t, kt=kt):
                    for tc in range(8):
                        t.matmul(psq[tc][:, 0:16], xg32[:, 128 * tc:128 * (tc + 1)],
                                 wgb[:, kt * 16:(kt + 1) * 16],
                                 start=(kt == 0), stop=(kt == NE - 1))
        with nc.Block() as blk:
            @blk.vector
            def _(v):
                for tc in range(8):
                    v.tensor_copy(glog[:, 16 * tc:16 * (tc + 1)], psq[tc][:, 0:16])
        es_pq.close()
        dump("d_glog", glog[:, :])
        dump("d_vfull", vfull[:, 0:2048])
        dump("d_kfull", kfull[:, 0:2048])
        for tc in range(8):
            gsl = glog[:, 16 * tc:16 * (tc + 1)]
            with nc.Block() as blk:
                @blk.vector
                def _(v, gsl=gsl):
                    pb.guard(v, v.reduce_max(gsc[:, 0:1], gsl, AX.X))
                    v.tensor_scalar(gsc[:, 1:2], gsc[:, 0:1], -1.0, None, ALU.mult)
            with nc.Block() as blk:
                @blk.scalar
                def _(s, gsl=gsl):
                    s.activation(expg[:, :], gsl, AF.Exp, bias=gsc[:, 1:2])
            with nc.Block() as blk:
                @blk.vector
                def _(v, gsl=gsl):
                    v.reduce_sum(gsc[:, 2:3], expg[:, :], AX.X)
            with nc.Block() as blk:
                @blk.vector
                def _(v):
                    v.reciprocal(gsc[:, 3:4], gsc[:, 2:3])
            with nc.Block() as blk:
                @blk.vector
                def _(v, gsl=gsl):
                    v.tensor_scalar(indb[:, :], gsl, gsc[:, 0:1], None, ALU.is_equal)
                    v.tensor_scalar(gt1b[:, :], indb[:, :], gsc[:, 3:4], None, ALU.mult)
            with nc.Block() as blk:
                @blk.vector
                def _(v, tc=tc):
                    ksl = kshtm[:, 128 * tc:128 * (tc + 1)]
                    vsl = vshtm[:, 128 * tc:128 * (tc + 1)]
                    v.tensor_scalar(ksl, kfull[:, tc * EMB:tc * EMB + 128],
                                    indb[:, 0:1], None, ALU.mult)
                    v.tensor_scalar(vsl, vfull[:, tc * EMB:tc * EMB + 128],
                                    gt1b[:, 0:1], None, ALU.mult)
                    for h in range(1, 16):
                        v.tensor_scalar(tmp128[:, :],
                                        kfull[:, tc * EMB + 128 * h:tc * EMB + 128 * (h + 1)],
                                        indb[:, h:h + 1], None, ALU.mult)
                        v.tensor_add(ksl, ksl, tmp128[:, :])
                        v.tensor_scalar(tmp128[:, :],
                                        vfull[:, tc * EMB + 128 * h:tc * EMB + 128 * (h + 1)],
                                        gt1b[:, h:h + 1], None, ALU.mult)
                        v.tensor_add(vsl, vsl, tmp128[:, :])
            with nc.Block() as blk:
                @blk.vector
                def _(v, tc=tc):
                    ksl = kshtm[:, 128 * tc:128 * (tc + 1)]
                    pb.guard(v, v.tensor_mul(sq128[:, :], ksl, ksl))
                    v.reduce_sum(kn2[:, tc:tc + 1], sq128[:, :], AX.X)
        esT = ExitStack()
        psTk = pb.psum("u_psTk", [128, 128], BF16, es=esT)
        psTn = pb.psum("u_psTn", [1, 128], F32, es=esT)
        for hh in range(2):
            with nc.Block() as blk:
                @blk.vector
                def _(v, hh=hh):
                    v.reduce_max(kn2[:, 8 + hh:9 + hh], kn2[:, 4 * hh:4 * (hh + 1)], AX.X)
            with nc.Block() as blk:
                @blk.tensor
                def _(t, hh=hh):
                    t.transpose(psTn[0:1, 0:128], kn2[:, 8 + hh:9 + hh], pb.ident_f[:, :])
            with nc.Block() as blk:
                @blk.vector
                def _(v, hh=hh):
                    v.reduce_max(knm[0:1, hh:hh + 1], psTn[0:1, 0:128], AX.X)
        psTk2 = pb.psum("u_psTk2", [128, 128], BF16, es=esT)
        transpose_seq(pb, [
            (kshtm[:, 128 * tc:128 * (tc + 1)], kshT[:, 128 * tc:128 * (tc + 1)],
             pb.ident[:, :]) for tc in range(8)], [psTk, psTk2])
        esT.close()
        dump("d_gt1", gt1b[:, :])
        dump("d_ind", indb[:, :])
        dump("d_top1", gsc[:, :])
        dump("d_ksh", kshtm[:, :])
        dump("d_vsh", vshtm[:, :])
        dump("d_kshT", kshT[:, :])
        dump("d_qT", qTl[:, 0:2048])
        with nc.Block() as blk:
            @blk.sync
            def _(sy):
                pb.dma(sy, ksh_in.ap(), kshT[:, :], name="st")
                pb.dma(sy, vsh_in.ap().rearrange("(tc p) m -> p tc m", p=128),
                       vshtm[:, :].rearrange("p (tc m) -> p tc m", tc=8), name="st")
                pb.dma(sy, knm_in.ap(), knm[0:1, 0:2], name="st")
                pb.dma(sy, qa_in.ap().rearrange("(ct p) t -> p ct t", p=128),
                       qTl[:, :].rearrange("p (ct t) -> p ct t", ct=NE), name="st")
                pb.wait(sy, "st")
        with nc.Block() as blk:
            @blk.gpsimd
            def _(g):
                ccs = pb.sem("cc")
                for src, dst in ((ksh_in, KSHg), (vsh_in, Vg), (knm_in, KNg)):
                    g.collective_compute(
                        "AllGather", ALU.bypass, replica_groups=[list(range(NCORES))],
                        ins=[src.ap().opt()], outs=[dst.ap().opt()]).then_inc(ccs)
                    pb.cnt["cc"] += 1
                with allow_ici_all_to_all():
                    g.collective_compute(
                        "AllToAll", ALU.bypass, replica_groups=[list(range(NCORES))],
                        ins=[qa_in.ap().opt()], outs=[QA.ap().opt()]).then_inc(ccs)
                    pb.cnt["cc"] += 1
                g.wait_ge(ccs, pb.cnt["cc"])
        es1.close()
        nc.leave_named_scope("l1fproj", sc_p2[0], False)

        with nc.Block() as blk:
            @blk.sync
            def _(sy):
                pb.dma(sy, knm[0:1, 0:16].rearrange("o (r h) -> o r h", r=8),
                       KNg.ap().rearrange("(o r) h -> o r h", o=1))
                pb.wait(sy, "ld")
        with nc.Block() as blk:
            @blk.vector
            def _(v):
                v.reduce_max(knm[0:1, 16:17], knm[0:1, 0:16], AX.X)
        dump("d_knm", knm[0:1, 0:18])

        sc_a2 = nc.enter_named_scope("l1fattn", False)
        es2 = ExitStack()
        kshbuf = pb.sbuf("u_kshb", [128, 16 * 512], BF16, es=es2)
        vshbuf = pb.sbuf("u_vshb", [128, 64 * 128], BF16, es=es2)
        qbuf2 = [pb.sbuf(f"u_qbuf{i}", [128, 1024], BF16, es=es2) for i in range(2)]
        sqv = pb.sbuf("u_sqv", [128, 512], F32, es=es2)
        brow = pb.sbuf("u_brow", [1, 8], F32, es=es2)
        nbcol = pb.sbuf("u_nbcol", [128, 2], F32, es=es2)
        ptile = [pb.sbuf(f"u_pt{i}", [128, 512], BF16, es=es2) for i in range(2)]
        lacc = pb.sbuf("u_lacc", [128, 512], F32, es=es2)
        otbuf = pb.sbuf("u_otb", [128, 512], BF16, es=es2)
        ybuf = [pb.sbuf(f"u_ybuf{i}", [128, 1024], BF16, es=es2) for i in range(2)]
        lrow = pb.sbuf("u_lrow", [1, 512], F32, es=es2)
        linv = pb.sbuf("u_linv", [128, 8], F32, es=es2)
        psS = [pb.psum(f"u_psS{i}", [128, 512], F32, es=es2) for i in range(2)]
        psOT = pb.psum("u_psOT", [128, 512], F32, es=es2)
        psL = pb.psum("u_psL", [1, 512], F32, es=es2)
        psB = pb.psum("u_psB", [128, 512], F32, es=es2)
        psTy = [pb.psum(f"u_psTy{i}", [128, 512], BF16, es=es2) for i in range(2)]
        kvd = pb.sem("l1fkv")
        yss = pb.sem("l1fys")
        sPr = pb.sem("l1fpr")
        sPoD = pb.sem("l1fpd")
        sPoT = pb.sem("l1fpt")
        sM = pb.sem("l1fM")
        sE = pb.sem("l1fE")
        sD = pb.sem("l1fD")
        sV = pb.sem("l1fV")
        sA = pb.sem("l1fA")

        def bMEV(qb, hl):
            return 4 * qb * (qb + 1) + hl * 4 * (qb + 1)

        def kv_emit(sy, j):
            co = 512 * (0 if j <= 7 else 1)
            r = own(j)
            pb.dma(sy, kshbuf[:, 512 * j:512 * (j + 1)],
                   KSHg.ap()[128 * r:128 * (r + 1), co:co + 512], name="l1fkv")
            pb.dma(sy, vshbuf[:, 512 * j:512 * (j + 1)]
                   .rearrange("p (kc m) -> p kc m", kc=4),
                   Vg.ap()[T * r + co:T * r + co + 512, :]
                   .rearrange("(kc p) m -> p kc m", p=128), name="l1fkv")
            for hl in range(2):
                pb.dma(sy, qbuf2[j % 2][:, 512 * hl:512 * (hl + 1)],
                       QA.ap()[256 * r + 128 * hl:256 * r + 128 * (hl + 1), co:co + 512],
                       name="l1fkv")

        with nc.Block() as blk:
            @blk.sync
            def _(sy):
                kv_emit(sy, 0)
        for qb in range(16):
            nkc = 4 * (qb + 1)
            kvneed = 64 * (qb + 1)
            with nc.Block() as blk:
                @blk.sync
                def _(sy, qb=qb):
                    if qb < 15:
                        kv_emit(sy, qb + 1)
                    dd = own(qb)
                    row0 = T * dd + 512 * (0 if qb <= 7 else 1)
                    for hl in range(2):
                        sy.wait_ge(sPoD, 3 * (2 * qb + hl) + 3)
                        pb.dma(sy, ya_in.ap()[row0:row0 + 512, 128 * hl:128 * (hl + 1)]
                               .rearrange("(qt p) m -> p qt m", p=128),
                               ybuf[qb % 2][:, 512 * hl:512 * (hl + 1)]
                               .rearrange("p (qt m) -> p qt m", qt=4), name="l1fys")

                @blk.tensor
                def _(t, qb=qb, nkc=nkc, kvneed=kvneed):
                    for hl in range(2):
                        ih = 2 * qb + hl
                        bPr = 7 * ih
                        bM = bE = bV = bMEV(qb, hl)
                        bD = 4 * ih
                        qsl = qbuf2[qb % 2][:, 512 * hl:512 * (hl + 1)]
                        t.wait_ge(sPr, bPr + 1)
                        t.matmul(psL[:, :], pb.ones_col_f[:, :], sqv[:, :],
                                 start=True, stop=True).then_inc(sPr, 1)
                        t.wait_ge(sPr, bPr + 5)
                        t.matmul(psB[:, 8 + hl:9 + hl], pb.ones_row_f[:, :],
                                 brow[0:1, 3:4], start=True, stop=True
                                 ).then_inc(sPr, 1)
                        for i in range(nkc + 1):
                            if i < nkc:
                                if i == 0:
                                    t.wait_ge(kvd, kvneed)
                                if i >= 2:
                                    t.wait_ge(sE, bE + i - 1)
                                t.matmul(psS[i % 2][:, :],
                                         kshbuf[:, 128 * i:128 * (i + 1)], qsl,
                                         start=True, stop=True).then_inc(sM, 1)
                            if i >= 1:
                                k = i - 1
                                t.wait_ge(sE, bE + k + 1)
                                if k >= 4 * qb:
                                    t.wait_ge(sD, bD + (k - 4 * qb) + 1)
                                t.matmul(psOT[:, :], vshbuf[:, 128 * k:128 * (k + 1)],
                                         ptile[k % 2][:, :], start=(k == 0),
                                         stop=(k == nkc - 1)).then_inc(sV, 1)
                        t.wait_ge(sA, bM + nkc)
                        t.matmul(psL[:, :], pb.ones_col_f[:, :], lacc[:, :],
                                 start=True, stop=True).then_inc(sPoT, 1)
                        t.wait_ge(sPoD, 3 * ih + 2)
                        for qt in range(4):
                            lt = t.transpose(psB[:, 4 * hl + qt:4 * hl + qt + 1],
                                             lrow[0:1, 128 * qt:128 * (qt + 1)],
                                             pb.ident_f[0:1, 0:1])
                        lt.then_inc(sPoT, 1)
                        for qt in range(4):
                            yt = t.transpose(psTy[hl][:, 128 * qt:128 * (qt + 1)],
                                             otbuf[:, 128 * qt:128 * (qt + 1)],
                                             pb.ident[:, :])
                        yt.then_inc(sPoT, 1)

                @blk.scalar
                def _(s, qb=qb, nkc=nkc):
                    for hl in range(2):
                        ih = 2 * qb + hl
                        bPr = 7 * ih
                        bM = bV = bA = bMEV(qb, hl)
                        s.wait_ge(sPr, bPr + 3)
                        s.activation(brow[0:1, 2:3], brow[0:1, 1:2], AF.Sqrt
                                     ).then_inc(sPr, 1)
                        s.wait_ge(sPr, bPr + 7)
                        for k in range(nkc):
                            s.wait_ge(sM, bM + k + 1)
                            if k >= 2:
                                s.wait_ge(sV, bV + k - 1)
                                s.wait_ge(sA, bA + k - 1)
                            s.activation(ptile[k % 2][:, :], psS[k % 2][:, :], AF.Exp,
                                         bias=nbcol[:, hl:hl + 1], scale=SC2
                                         ).then_inc(sE, 1)

                @blk.vector
                def _(v, qb=qb, nkc=nkc, kvneed=kvneed):
                    for hl in range(2):
                        ih = 2 * qb + hl
                        bPr = 7 * ih
                        bE = bV = bMEV(qb, hl)
                        qsl = qbuf2[qb % 2][:, 512 * hl:512 * (hl + 1)]
                        if hl == 0:
                            v.wait_ge(kvd, kvneed)
                        v.tensor_mul(sqv[:, :], qsl, qsl).then_inc(sPr, 1)
                        v.wait_ge(sPr, bPr + 2)
                        pb.guard(v, v.reduce_max(brow[0:1, 0:1], psL[:, :], AX.X))
                        v.tensor_mul(brow[0:1, 1:2], brow[0:1, 0:1], knm[0:1, 16:17]
                                     ).then_inc(sPr, 1)
                        v.wait_ge(sPr, bPr + 4)
                        v.tensor_scalar(brow[0:1, 3:4], brow[0:1, 2:3], -SC2, None,
                                        ALU.mult).then_inc(sPr, 1)
                        v.wait_ge(sPr, bPr + 6)
                        v.tensor_copy(nbcol[:, hl:hl + 1], psB[:, 8 + hl:9 + hl]
                                      ).then_inc(sPr, 1)
                        for k in range(nkc):
                            v.wait_ge(sE, bE + k + 1)
                            if k >= 4 * qb:
                                v.tensor_mul(ptile[k % 2][:, :], ptile[k % 2][:, :],
                                             triv[:, 512 * (k % 4):512 * (k % 4 + 1)]
                                             ).then_inc(sD, 1)
                            if k == 0:
                                ac = v.tensor_copy(lacc[:, :], ptile[0][:, :])
                            else:
                                ac = v.tensor_add(lacc[:, :], lacc[:, :],
                                                  ptile[k % 2][:, :])
                            ac.then_inc(sA, 1)
                        v.wait_ge(sV, bV + nkc)
                        v.tensor_copy(otbuf[:, :], psOT[:, :]).then_inc(sPoD, 1)
                        v.wait_ge(sPoT, 3 * ih + 1)
                        v.tensor_copy(lrow[0:1, :], psL[:, :]).then_inc(sPoD, 1)
                        v.wait_ge(sPoT, 3 * ih + 2)
                        pb.guard(v, v.reciprocal(linv[:, 4 * hl:4 * hl + 4],
                                                 psB[:, 4 * hl:4 * hl + 4]))
                        v.wait_ge(sPoT, 3 * ih + 3)
                        if qb >= 2:
                            v.wait_ge(yss, 16 * (2 * (qb - 2) + hl + 1))
                        for qt in range(4):
                            yb = v.tensor_scalar(
                                ybuf[qb % 2][:, 512 * hl + 128 * qt:512 * hl + 128 * (qt + 1)],
                                psTy[hl][:, 128 * qt:128 * (qt + 1)],
                                linv[:, 4 * hl + qt:4 * hl + qt + 1], None, ALU.mult)
                        yb.then_inc(sPoD, 1)
        pb.cnt["l1fM"] = pb.cnt["l1fE"] = pb.cnt["l1fV"] = pb.cnt["l1fA"] = bMEV(16, 0)
        pb.cnt["l1fD"] = 128
        pb.cnt["l1fpr"] = 7 * 32
        pb.cnt["l1fpd"] = pb.cnt["l1fpt"] = 3 * 32
        es2.close()
        nc.leave_named_scope("l1fattn", sc_a2[0], False)
        if dbg is not None and "d_yin" in dbg:
            with nc.Block() as blk:
                @blk.sync
                def _(sy):
                    pb.wait(sy, "l1fys")
                    pb.dma(sy, dbg["d_yin"], ya_in.ap(), name="st")
                    pb.wait(sy, "st")
        with nc.Block() as blk:
            @blk.gpsimd
            def _(g):
                pb.wait(g, "l1fys")
                ccs = pb.sem("cc")
                with allow_ici_all_to_all():
                    g.collective_compute(
                        "AllToAll", ALU.bypass, replica_groups=[list(range(NCORES))],
                        ins=[ya_in.ap().opt()], outs=[YA.ap().opt()]).then_inc(ccs)
                    pb.cnt["cc"] += 1
                g.wait_ge(ccs, pb.cnt["cc"])

        sc_fn = nc.enter_named_scope("l1ffinal", False)
        es3 = ExitStack()
        ytm = pb.sbuf("u_ytm", [128, EMB], BF16, es=es3)
        x3tm = pb.sbuf("u_x3tm", [128, EMB], BF16, es=es3)
        tfin = pb.sbuf("u_tfin", [128, EMB], F32, es=es3)
        sqf = pb.sbuf("u_sqf", [128, EMB], F32, es=es3)
        stf = pb.sbuf("u_stf", [128, 6], F32, es=es3)
        psTz = pb.psum("u_psTz", [128, 128], BF16, es=es3)
        psTz2 = pb.psum("u_psTz2", [128, 128], BF16, es=es3)
        for tc in range(8):
            with nc.Block() as blk:
                @blk.sync
                def _(sy, tc=tc):
                    pb.dma(sy, ytm[:, :].rearrange("p (s m) -> p s m", s=8),
                           YA.ap()[:, :].rearrange("(s q p) m -> p s q m", p=128, q=8)
                           [:, :, tc, :])
                    pb.wait(sy, "ld")
            transpose_seq(pb, [
                (xbuf[:, ct * T + 128 * tc:ct * T + 128 * (tc + 1)],
                 x3tm[:, 128 * ct:128 * (ct + 1)], pb.ident[:, :])
                for ct in range(NE)], [psTz, psTz2])
            with nc.Block() as blk:
                @blk.vector
                def _(v):
                    v.tensor_add(tfin[:, :], x3tm[:, :], ytm[:, :])
            with nc.Block() as blk:
                @blk.vector
                def _(v):
                    pb.guard(v, v.reduce_sum(stf[:, 0:1], tfin[:, :], AX.X))
                    v.tensor_scalar(stf[:, 0:1], stf[:, 0:1], 1.0 / EMB, None, ALU.mult)
                @blk.scalar
                def _(s):
                    s.activation(sqf[:, :], tfin[:, :], AF.Square)
            with nc.Block() as blk:
                @blk.vector
                def _(v):
                    pb.guard(v, v.reduce_sum(stf[:, 1:2], sqf[:, :], AX.X))
                    v.tensor_scalar(stf[:, 1:2], stf[:, 1:2], 1.0 / EMB, None, ALU.mult)
                    v.tensor_mul(stf[:, 2:3], stf[:, 0:1], stf[:, 0:1])
                    v.tensor_sub(stf[:, 1:2], stf[:, 1:2], stf[:, 2:3])
                    v.tensor_scalar(stf[:, 1:2], stf[:, 1:2], 1e-5, None, ALU.add)
            with nc.Block() as blk:
                @blk.scalar
                def _(s):
                    s.activation(stf[:, 3:4], stf[:, 1:2], AF.Sqrt)
            with nc.Block() as blk:
                @blk.vector
                def _(v):
                    v.reciprocal(stf[:, 4:5], stf[:, 3:4])
            with nc.Block() as blk:
                @blk.vector
                def _(v):
                    v.tensor_scalar(tfin[:, :], tfin[:, :], stf[:, 0:1], stf[:, 4:5],
                                    ALU.subtract, ALU.mult)
            with nc.Block() as blk:
                @blk.sync
                def _(sy, tc=tc):
                    pb.dma(sy, out_ap[128 * tc:128 * (tc + 1), :], tfin[:, :], name="st")
                    pb.wait(sy, "st")
        es3.close()
        nc.leave_named_scope("l1ffinal", sc_fn[0], False)


def build_l1ffn(debug=False):
    nc = bass.Bass("TRN2", target_bir_lowering=False)
    pb = PB(nc)
    x3 = nc.dram_tensor("x3fm", [EMB, 1024], BF16, kind="ExternalInput").ap()
    wqkv = nc.dram_tensor("wqkv2_t", [EMB, 6144], F32, kind="ExternalInput").ap()
    wgate = nc.dram_tensor("wgate2_t", [EMB, 16], F32, kind="ExternalInput").ap()
    out = nc.dram_tensor("outrows", [1024, EMB], F32, kind="ExternalOutput").ap()
    dbg = None
    if debug:
        dbg = {}
        for nm_, shp, dt_ in (("d_glog", [128, 128], F32), ("d_ksh", [128, 1024], BF16),
                              ("d_vsh", [128, 1024], BF16), ("d_kshT", [128, 1024], BF16),
                              ("d_qT", [128, 2048], BF16), ("d_knm", [1, 18], F32),
                              ("d_yin", [8 * 1024, 256], BF16),
                              ("d_vfull", [128, 2048], BF16), ("d_kfull", [128, 2048], BF16),
                              ("d_gt1", [128, 16], F32), ("d_ind", [128, 16], F32),
                              ("d_top1", [128, 4], F32)):
            dbg[nm_] = nc.dram_tensor(nm_, shp, dt_, kind="ExternalOutput").ap()
    with pb.es:
        pb.consts()
        _l1ffn_body(pb, x3, wqkv, wgate, out, dbg=dbg)
    return nc


def build_fused(dbg_mid=False, nphases=4):
    nc = bass.Bass("TRN2", target_bir_lowering=False)
    pb = PB(nc)
    ap = lambda *a, **k: nc.dram_tensor(*a, kind="ExternalInput", **k).ap()
    x_lo = ap("x_lo", [515, EMB], F32)
    x_hi = ap("x_hi", [515, EMB], F32)
    wproj = ap("w_inproj_t", [EMB, PROJ], F32)
    cw = ap("conv_w", [128, 192], F32)
    cb = ap("conv_b", [128, 48], F32)
    dtb = ap("dt_bias", [64, 1], F32)
    alog = ap("a_log", [64, 1], F32)
    dpair = ap("d_pair", [128, 32], F32)
    nw = ap("norm_w", [128, 32], F32)
    wout = ap("w_outproj_t", [INTER, EMB], F32)
    lut = ap("lut", [1, 16], F32)
    gidx = ap("gidx_sl", [FFN_I, 256], I32)
    uidx = ap("uidx_sl", [FFN_I, 256], I32)
    didx = ap("didx_sl", [EMB, 688], I32)
    qkv = ap("qkv1", [EMB, 6144], F32)
    oproj = ap("oproj", [EMB, EMB], F32)
    gam = ap("gamma", [128, 16], F32)
    bet = ap("beta", [128, 16], F32)
    allow = ap("allow", [96, 128, 512], BF16)
    wqkv = ap("wqkv2_t", [EMB, 6144], F32)
    wgate = ap("wgate2_t", [EMB, 16], F32)
    out = nc.dram_tensor("outrows", [1024, EMB], F32, kind="ExternalOutput").ap()
    kind = "ExternalOutput" if dbg_mid else "Internal"
    x1 = nc.dram_tensor("x1fm", [EMB, 1024], BF16, kind=kind).ap()
    if dbg_mid:
        nc._dbg_wgt = True
    x2 = nc.dram_tensor("x2fm", [EMB, 1024], BF16, kind=kind).ap()
    x3 = nc.dram_tensor("x3fm", [EMB, 1024], BF16, kind=kind).ap()
    with pb.es:
        pb.consts()
        _l0attn_body(pb, (x_lo, x_hi), wproj, cw, cb, dtb, alog, dpair, nw, wout, x1)
        if nphases >= 2:
            _l0ffn_body(pb, x1, lut, gidx, uidx, didx, x2)
        if nphases >= 3:
            _l1attn_body(pb, None, x2, qkv, oproj, gam, bet, allow, x3)
        if nphases >= 4:
            _l1ffn_body(pb, x3, wqkv, wgate, out)
    return nc


def _prep_host(ins):
    x0 = np.ascontiguousarray(ins["L0_attn_input_states"][0])
    A = {
        "w_inproj_t": np.ascontiguousarray(ins["L0_attn_in_proj_w"].T),
        "conv_w": np.ascontiguousarray(
            ins["L0_attn_conv_w"][:, 0, :].reshape(48, 128, 4).transpose(1, 0, 2)
            .reshape(128, 192)),
        "conv_b": np.ascontiguousarray(ins["L0_attn_conv_b"].reshape(48, 128).T),
        "dt_bias": ins["L0_attn_dt_bias"].reshape(64, 1).copy(),
        "a_log": ins["L0_attn_A_log"].reshape(64, 1).copy(),
        "d_pair": np.ascontiguousarray(
            ins["L0_attn_D"].reshape(32, 2).repeat(64, axis=1).reshape(32, 128).T),
        "norm_w": np.ascontiguousarray(ins["L0_attn_norm_w"].reshape(32, 128).T),
        "w_outproj_t": np.ascontiguousarray(ins["L0_attn_out_proj_w"].T),
    }
    B = {"lut": np.ascontiguousarray(ins["L0_ffn_lut"].reshape(1, 16))}
    C = {
        "qkv1": np.ascontiguousarray(ins["L1_attn_qkv"]),
        "oproj": np.ascontiguousarray(ins["L1_attn_o_proj"]),
        "gamma": np.ascontiguousarray(ins["L1_attn_gamma"].reshape(16, 128).T),
        "beta": np.ascontiguousarray(ins["L1_attn_beta"].reshape(16, 128).T),
    }
    D = {
        "wqkv2_t": np.ascontiguousarray(ins["L1_ffn_W_qkv"].T),
        "wgate2_t": np.ascontiguousarray(ins["L1_ffn_W_gate"].T),
    }
    mapsA, mapsB, mapsC, mapsD = [], [], [], []
    for c in range(NCORES):
        mA = dict(A)
        for key, b in zip(("x_lo", "x_hi"), blocks_of(c)):
            xe = np.zeros((515, EMB), np.float32)
            xe[3:] = x0[512 * b:512 * (b + 1)]
            if b > 0:
                xe[:3] = x0[512 * b - 3:512 * b]
            mA[key] = xe
        mapsA.append(mA)
        mB = dict(B)
        mB["gidx_sl"] = np.ascontiguousarray(ins["L0_ffn_gate_idx"][:, 256 * c:256 * (c + 1)])
        mB["uidx_sl"] = np.ascontiguousarray(ins["L0_ffn_up_idx"][:, 256 * c:256 * (c + 1)])
        mB["didx_sl"] = np.ascontiguousarray(ins["L0_ffn_down_idx"][:, 688 * c:688 * (c + 1)])
        mapsB.append(mB)
        mC = dict(C)
        allow = np.zeros((96, 128, 512), np.float32)
        i = 0
        for half, b in enumerate(blocks_of(c)):
            nslots = 8 if half == 0 else 16
            for qt in range(4):
                qglob = 512 * b + 128 * qt + np.arange(128)
                for j in range(nslots):
                    if j < b:
                        allow[i] = 1.0
                    elif j == b:
                        kglob = 512 * j + np.arange(512)
                        allow[i] = (kglob[None, :] <= qglob[:, None])
                    i += 1
        mC["allow"] = allow.astype(BF)
        mapsC.append(mC)
        mapsD.append(dict(D))
    return mapsA, mapsB, mapsC, mapsD


def kernel(**inputs):
    mapsA, mapsB, mapsC, mapsD = _prep_host(inputs)
    cores = list(range(NCORES))
    maps = [{**mapsA[c], **mapsB[c], **mapsC[c], **mapsD[c]} for c in range(NCORES)]
    res = run_bass_kernel_spmd(build_fused(), maps, core_ids=cores)
    out = np.zeros((1, SEQ, EMB), np.float32)
    for c in range(NCORES):
        rows = res.results[c]["outrows"]
        b0, b1 = blocks_of(c)
        out[0, 512 * b0:512 * (b0 + 1)] = rows[0:512]
        out[0, 512 * b1:512 * (b1 + 1)] = rows[512:1024]
    return out
